# version 10
# baseline (speedup 1.0000x reference)
import numpy as np
import ml_dtypes
import concourse.bass as bass
import concourse.mybir as mybir
from concourse.bass_isa import ReduceOp
from concourse.bass_utils import run_bass_kernel_spmd

F32 = mybir.dt.float32
BF16 = mybir.dt.bfloat16
I32 = mybir.dt.int32
AF = mybir.ActivationFunctionType
ALU = mybir.AluOpType
AX = mybir.AxisListType
BF = ml_dtypes.bfloat16
BF16_ = mybir.dt.bfloat16

B, T, C = 2, 4096, 2048
H, D = 16, 128
L = 128
NTOK = 1024
NCH = 8
INNER = 8192
EPS = 1e-5
NCORES = 8
KC = C // 128
IT = INNER // 128

GAMMA = (1.0 - 2.0 ** (-5.0 - np.arange(H, dtype=np.float64)))
GL = [float(np.float32(g ** L)) for g in GAMMA]

_CACHE = {}


class Rec:
    ENGS = ("sync", "gpsimd", "vector", "scalar", "tensor")
    ESEM = {"sync": "c_sy", "gpsimd": "c_gp", "vector": "c_ve",
            "scalar": "c_ac", "tensor": "c_te"}

    def __init__(self, nc):
        self.nc = nc
        self.ops = {e: [] for e in self.ENGS}
        self.sems = {}
        self.emitted_inc = {}
        for s in self.ESEM.values():
            self.sem(s)

    def sem(self, name):
        if name not in self.sems:
            h = self.nc.semaphore(name).__enter__()
            self.sems[name] = [h, 0]
        return name

    def inc(self, name, by=1):
        ent = self.sems[name]
        ent[1] += by
        h = ent[0]
        return lambda inst: inst.then_inc(h, by)

    def cur_tok(self, eng):
        s = self.ESEM[eng]
        return (s, self.sems[s][1])

    def step(self, eng, fn):
        sname = self.ESEM[eng]
        ap = self.inc(sname)
        cnt = self.sems[sname][1]
        h = self.sems[sname][0]

        def wrapped(e, fn=fn, ap=ap, h=h, cnt=cnt):
            inst = fn(e)
            ap(inst)
            self.emitted_inc[id(inst)] = (h, cnt)
        self.ops[eng].append(wrapped)
        return (sname, cnt)

    def wait(self, eng, tok):
        if tok is None:
            return
        name, v = tok
        if v <= 0:
            return
        ent = self.sems[name]
        self.ops[eng].append(lambda e, h=ent[0], v=v: e.wait_ge(h, v))

    def wait_all(self, eng, prod_eng):
        self.wait(eng, self.cur_tok(prod_eng))

    def dma(self, eng, dst, src, sem):
        self.sem(sem)
        ap = self.inc(sem, 16)
        tok = (sem, self.sems[sem][1])
        self.ops[eng].append(lambda e, d=dst, s=src, ap=ap: ap(e.dma_start(out=d, in_=s)))
        return tok

    def emit_all(self, block):
        for eng in self.ENGS:
            def body(e, eng=eng):
                h = HazardGuard(e, self, eng) if eng in ("vector", "scalar") else e
                for fn in self.ops[eng]:
                    fn(h)
            getattr(block, eng)(body)


def _ap_range(a):
    t = getattr(a, "tensor", None)
    if t is None:
        return None, None
    rng = getattr(t, "manual_sbuf_range", None)
    name = getattr(t, "name", None)
    if rng is None:
        return name, None
    esz = mybir.dt.size(t.dtype)
    try:
        aps = list(a.ap)
        ext = 1
        for st, ct in aps[1:]:
            ext += abs(int(st)) * (int(ct) - 1)
        off = int(a.offset)
        row = int(aps[0][0]) if aps else ext
        if row > 0:
            off = off % row
        lo = rng[0] + off * esz
        hi = lo + ext * esz
    except Exception:
        lo, hi = rng[0], rng[1]
    return name, (lo, hi)


_AFC = mybir.ActivationFunctionType.Copy

_GUARDED_OPS = {
    "tensor_copy", "tensor_tensor", "tensor_add", "tensor_sub", "tensor_mul",
    "tensor_max", "tensor_scalar", "tensor_single_scalar", "tensor_scalar_add",
    "tensor_scalar_sub", "tensor_scalar_mul", "tensor_scalar_max",
    "tensor_scalar_min", "tensor_reduce", "reduce_max", "reduce_sum",
    "scalar_tensor_tensor", "reciprocal", "memset", "memzero", "activation",
    "copy", "mul", "add", "square", "sqrt", "sign",
}


class HazardGuard:

    def __init__(self, e, rec, eng):
        self._e = e
        self._rec = rec
        self._eng = eng
        self._zsem = rec.sem("z_" + eng)
        self._last = None

    def wait_ge(self, *a, **k):
        return self._e.wait_ge(*a, **k)

    def nop(self, *a, **k):
        return self._e.nop(*a, **k)

    def dma_start(self, *a, **k):
        return self._e.dma_start(*a, **k)

    def __getattr__(self, name):
        fn = getattr(self._e, name)
        if name not in _GUARDED_OPS or not callable(fn):
            return fn

        def call(*args, **kw):
            outs, ins = [], []
            aps = []
            for i, a in enumerate(args):
                if hasattr(a, "tensor"):
                    (outs if i == 0 else ins).append(_ap_range(a))
            for kname, a in kw.items():
                if hasattr(a, "tensor"):
                    (outs if kname == "accum_out" else ins).append(_ap_range(a))
            if self._last is not None:
                li, lr, lw, ltok = self._last
                if self._conf(lw, ins) or self._conf(lw, outs) or self._conf(lr, outs):
                    prev = self._rec.emitted_inc.get(id(li))
                    if prev is not None:
                        self._e.wait_ge(prev[0], prev[1])
                    else:
                        ent = self._rec.sems[self._zsem]
                        if ltok is None:
                            ent[1] += 1
                            li.then_inc(ent[0], 1)
                            ltok = ent[1]
                            self._rec.emitted_inc[id(li)] = (ent[0], ent[1])
                            self._last = (li, lr, lw, ltok)
                        self._e.wait_ge(ent[0], ltok)
                    scr = getattr(self._rec, "guard_scratch", {}).get(self._eng)
                    if scr is not None:
                        sa, sb = scr
                        if self._eng == "vector":
                            self._e.tensor_copy(sa[:, :], sb[:, :])
                            self._e.tensor_copy(sa[:, :], sb[:, :])
                        else:
                            self._e.activation(sa[:, :], sb[:, :], _AFC)
                            self._e.activation(sa[:, :], sb[:, :], _AFC)
            inst = fn(*args, **kw)
            self._last = (inst, ins, outs, None)
            return inst
        return call

    @staticmethod
    def _conf(aset, bset):
        for an, ar in aset:
            for bn, br in bset:
                if ar is None or br is None:
                    if an is not None and an == bn:
                        return True
                elif ar[0] < br[1] and br[0] < ar[1]:
                    return True
        return False


def build_program(taps=()):
    nc = bass.Bass("TRN2", target_bir_lowering=False, debug=False,
                   detect_race_conditions=False)
    taps = set(taps)

    def din(name, shape, dt=F32):
        return nc.dram_tensor(name, shape, dt, kind="ExternalInput").ap()

    x_in = din("x_loc", [NTOK, C])
    tab_in = {("q", 0): din("cosq", [C, NTOK], BF16),
              ("q", 1): din("sinq", [C, NTOK], BF16),
              ("k", 0): din("cosk", [C, NTOK], BF16),
              ("k", 1): din("sink", [C, NTOK], BF16)}
    causal_in = din("causal", [L, L], BF16)
    identf_in = din("ident_f", [128, 128], F32)
    identb_in = din("ident_b", [128, 128], BF16)
    gnw_in = din("gnw_row", [1, C], BF16)
    gnb_in = din("gnb_row", [1, C], BF16)
    b1fm_in = din("b1_fm", [128, IT])
    b2_in = din("b2_row", [1, C], BF16)
    lng_in = din("lng_fm", [128, IT])
    lnb_in = din("lnb_fm", [128, IT])
    cvec_in = din("cvec", [128, NCORES * H])
    wqkv_in = din("wqkv_s", [C // 8, 3 * C])
    wg_in = din("wg_s", [C // 8, C])
    wproj_in = din("wproj_s", [C // 8, C])
    w1t_in = din("w1t_s", [C // 8, INNER])
    w2t_in = din("w2t_s", [INNER // 8, C])

    out_ap = nc.dram_tensor("out", [NTOK, C], F32, kind="ExternalOutput").ap()
    tap_aps = {}

    def mktap(tname, shape, dt):
        if tname in taps:
            tap_aps[tname] = nc.dram_tensor("tap_" + tname, shape, dt,
                                            kind="ExternalOutput").ap()
    mktap("xn_tm", [128, 8 * C], F32)
    mktap("xraw", [128, 8 * C], F32)
    mktap("sstat1", [128, 64], F32)
    mktap("statsA", [128, 64], F32)
    mktap("q2", [128, H * NTOK], BF16)
    mktap("k2", [128, H * NTOK], BF16)
    mktap("v", [128, 8 * C], BF16)
    mktap("y", [128, H * NTOK], BF16)
    mktap("S", [128, H * D], F32)
    mktap("h", [128, 2 * IT * 512], BF16)
    mktap("n2", [128, 2 * IT * 512], BF16)
    mktap("g", [128, 8 * C], BF16)
    mktap("scl4", [128, 4], F32)
    mktap("ybuf0", [128, NTOK], BF16)
    mktap("gn0", [128, 16], F32)
    mktap("sinc2", [128, H * D], BF16)
    mktap("sproj", [128, H * NTOK], BF16)
    mktap("stats", [128, 64], F32)
    mktap("statB", [128, 64], F32)
    mktap("sinc", [128, H * D], BF16)

    def dint(name, shape, dt):
        return nc.dram_tensor(name, shape, dt).ap()

    g_wqkv_i = dint("g_wqkv_i", [C // 8, 3 * C], BF16)
    g_wqkv = dint("g_wqkv", [C, 3 * C], BF16)
    g_wg_i = dint("g_wg_i", [C // 8, C], BF16)
    g_wg = dint("g_wg", [C, C], BF16)
    g_wp_i = dint("g_wp_i", [C // 8, C], BF16)
    g_wp = dint("g_wp", [C, C], BF16)
    g_q1_i = dint("g_q1_i", [C // 8, INNER], BF16)
    g_q1 = dint("g_q1", [C, INNER], BF16)
    g_q2_i = dint("g_q2_i", [INNER // 8, C], BF16)
    g_q2 = dint("g_q2", [INNER, C], BF16)
    g_ws_i = dint("g_ws_i", [1, 16], F32)
    g_ws = dint("g_ws", [1, 16], F32)
    g_sl_i = dint("g_sl_i", [128, H * D], F32)
    g_sl = dint("g_sl", [NCORES * 128, H * D], F32)
    g_ysp = dint("g_ysp", [128, H * NTOK], BF16)
    g_qsp = dint("g_qsp", [128, H * NTOK], BF16)
    g_vsp = dint("g_vsp", [128, 8 * C], BF16)
    g_xnsp = dint("g_xnsp", [128, KC * NTOK], BF16)
    g_n1sp = dint("g_n1sp", [128, KC * NTOK], BF16)

    BASE = nc.sbuf_base + 4096
    TOP = nc.SBUF_PARTITION_SIZE_BYTES
    _nt = [0]

    def at(off_kb, shape, dt, name="m"):
        _nt[0] += 1
        esz = mybir.dt.size(dt)
        off = BASE + int(round(off_kb * 1024))
        off = (off + 31) // 32 * 32
        free = int(np.prod(shape[1:])) * esz
        assert off + free <= TOP, f"{name}: {off}+{free} > {TOP}"
        return nc.alloc_sbuf_tensor_at(f"{name}{_nt[0]}", shape, dt, offset=off)

    stats = at(0, [128, 64], F32, "stats")
    sstat = at(0.25, [128, 64], F32, "sstat")
    stat2 = at(0.5, [128, 64], F32, "stat2")
    statB = at(0.75, [128, 64], F32, "statB")
    identf = at(1.0, [128, 128], F32, "idf")
    identb = at(1.55, [128, 128], BF16, "idb")
    causal = at(1.85, [L, L], BF16, "caus")
    cvec = at(2.15, [128, NCORES * H], F32, "cvec")
    lng_fm = at(2.7, [128, IT], F32, "lng")
    lnb_fm = at(3.0, [128, IT], F32, "lnb")
    b1_fm = at(3.3, [128, IT], F32, "b1f")
    onesb_c = at(3.6, [128, 1], BF16, "o1")
    onesf_c = at(3.65, [128, 1], F32, "o1f")
    onesb_r = at(3.7, [1, 128], BF16, "o2")
    onesf_r = at(4.0, [1, 128], F32, "o3")
    sw_b = at(4.6, [128, 2], F32, "swb")
    wsum = at(4.7, [1, 16], F32, "wsum")
    rowf = at(4.8, [1, 128], F32, "rowf")
    rowb = at(5.4, [1, 128], BF16, "rowb")
    gnw_r = at(5.7, [1, C], BF16, "gnwr")
    gnb_r = at(9.8, [1, C], BF16, "gnbr")
    b2_r = at(13.9, [1, C], BF16, "b2r")
    S_t = at(18.0, [128, H * D], F32, "S")
    Sbf = at(26.1, [128, H * D], BF16, "Sbf")
    c1rows = at(30.2, [1, 8 * 128], F32, "c1r")
    svrows = at(34.3, [1, 8 * 128], BF16, "svr")
    sv2rows = at(36.4, [1, 8 * 128], BF16, "sv2")
    rmax = at(38.5, [128, 512], F32, "rmax")
    rmin = at(40.5, [128, 512], F32, "rmin")
    ybuf = at(42.5, [128, NTOK], BF16, "ybuf")
    scl4 = at(47.0, [128, 4], F32, "scl4")
    pre4 = at(47.1, [128, 128], F32, "pre4")
    scrv_a = at(47.6, [128, 16], F32, "scrva")
    scrv_b = at(47.7, [128, 16], F32, "scrvb")
    scrs_a = at(47.8, [128, 16], F32, "scrsa")
    scrs_b = at(47.9, [128, 16], F32, "scrsb")
    gnmu = at(44.5, [128, 16], F32, "gnmu")
    gnrs = at(44.6, [128, 16], F32, "gnrs")
    gnmp = at(44.7, [128, 16], F32, "gnmp")
    t4buf = at(44.8, [128, 512], F32, "t4buf")

    R0 = 48.0

    wbuf = [at(R0 + 32 * i, [128, INNER], F32, f"wb{i}") for i in range(2)]
    qbuf = at(R0 + 64, [128, INNER], BF16, "qb")
    qscr = at(R0 + 80, [128, C], BF16, "qs")

    x_tm = at(R0, [128, 8 * C], F32, "x")
    xn_tm = at(R0 + 64, [128, 8 * C], F32, "xn")
    sqd = at(R0 + 128.5, [128, C], BF16, "sqd")
    n1_tm = at(R0, [128, 8 * C], BF16, "n1tm")
    n1i_f = at(R0 + 32, [128, C], F32, "n1ir")
    xn_fm = at(R0 + 32, [128, KC * NTOK], BF16, "xnfm")
    n1_fm = at(R0 + 64, [128, KC * NTOK], BF16, "n1fm")

    q2_a = at(R0, [128, H * NTOK], BF16, "q2")
    k2_a = at(R0 + 64, [128, H * NTOK], BF16, "k2")
    wsl = [at(R0 + 96 + 16 * i, [128, KC * 512], BF16, f"wsl{i}") for i in range(2)]
    tabr = at(R0 + 128, [128, 2 * 1024], BF16, "tab")
    v_r = [at(R0 + 132 + 4 * i, [128, C], BF16, f"vr{i}") for i in range(2)]
    y_r = [at(R0 + 140 + 4 * i, [128, H * L], BF16, f"yr{i}") for i in range(2)]
    smr = [at(R0 + 148 + 0.25 * i, [128, L], BF16, f"smr{i}") for i in range(2)]
    kdr = [at(R0 + 148.5 + 0.25 * i, [128, L], BF16, f"kdr{i}") for i in range(2)]
    fr1 = at(R0 + 152, [128, 512], F32, "fr1")
    fr1b = at(R0 + 132, [128, 512], F32, "fr1b")
    fr2b = at(R0 + 134, [128, 512], F32, "fr2b")
    fr2 = at(45.0, [128, 512], F32, "fr2")

    harena = at(R0, [128, IT * 512], BF16, "h")
    n1r = at(R0 + 64, [128, KC * NTOK], BF16, "n1r")
    q1b = [at(R0 + 96, [128, KC * 128], BF16, "q1b")]
    w2b = [at(R0 + 100 + 4 * i, [128, 4 * 512], BF16, f"w2b{i}") for i in range(2)]
    fri = at(R0 + 108, [128, 512], I32, "fri")
    gbr = [at(R0 + 110 + i, [128, 512], BF16, f"gbr{i}") for i in range(2)]
    sqrr = [at(R0 + 144 + i, [128, 512], BF16, f"sqr{i}") for i in range(2)]
    C1R = at(R0 + 146, [128, 512], F32, "C1R")
    R1 = at(R0 + 148, [128, 512], F32, "R1")
    M1R = at(R0 + 150, [128, 512], F32, "M1R")
    C2R = at(43.0, [128, 512], F32, "C2R")
    g_tm = at(R0 + 112, [128, 8 * C], BF16, "g")

    sproj = at(R0, [128, H * NTOK], BF16, "sp")
    xn_fm4 = at(R0 + 32, [128, KC * NTOK], BF16, "xnfm4")
    wsl4 = at(R0 + 64, [128, KC * 512], BF16, "wsl4")
    sincA = at(R0 + 80, [128, H * D], F32, "siA")
    tfin = at(R0 + 80, [128, 1024], F32, "tfin")
    sincB = at(R0 + 88, [128, H * D], F32, "siB")
    sring = [at(R0 + 96 + 4 * i, [128, H * D], BF16, f"sg{i}") for i in range(2)]
    sincbf = at(R0 + 104, [128, H * D], BF16, "sibf")
    y4r = [at(R0 + 80 + 2 * i, [128, NTOK], BF16, f"y4{i}") for i in range(2)]
    q4r = [at(R0 + 84 + 2 * i, [128, NTOK], BF16, f"q4{i}") for i in range(2)]
    gsil = [at(R0 + 88 + 2 * i, [128, NTOK], BF16, f"gs{i}") for i in range(2)]
    qdr4 = [at(R0 + 92 + 0.25 * i, [128, L], BF16, f"qd4{i}") for i in range(2)]
    ytot = at(R0 + 92.5, [128, L], BF16, "yt")
    ysq = at(R0 + 92.75, [128, L], BF16, "ysq")
    t4a = at(R0 + 93, [128, L], F32, "t4a")
    t4b = at(R0 + 93.5, [128, L], F32, "t4b")
    xrg = [at(R0 + 94 + 2 * i, [128, 512], F32, f"xr{i}") for i in range(2)]
    org = [at(R0 + 98 + 2 * i, [128, 512], F32, f"og{i}") for i in range(2)]

    pb = [nc.alloc_psum_tensor(f"pb{i}", [128, 512], F32) for i in range(8)]
    bkg = {i: None for i in range(8)}

    rec = Rec(nc)
    rec.guard_scratch = {"vector": (scrv_a, scrv_b), "scalar": (scrs_a, scrs_b)}
    RG = [list(range(NCORES))]

    def acq(i):
        rec.wait("tensor", bkg[i])
        return pb[i]

    def tap_dump(name, src_ap):
        if name in tap_aps:
            rec.wait_all("sync", "vector")
            rec.wait_all("sync", "scalar")
            rec.wait_all("sync", "tensor")
            rec.wait_all("sync", "gpsimd")
            t = rec.dma("sync", tap_aps[name][:, :], src_ap, "d_tap")
            for e in ("vector", "scalar", "tensor", "gpsimd"):
                rec.wait(e, t)

    def col_to_row(colap, rowslice, bf16=False):
        bank = acq(7)
        tok = rec.step("tensor", lambda t, colap=colap, bank=bank:
                       t.transpose(bank[0:1, 0:128], colap, identf[:, :]))
        rec.wait("vector", tok)
        ctok = rec.step("vector", lambda v, r=rowslice, bank=bank:
                        v.tensor_copy(r, bank[0:1, 0:128]))
        bkg[7] = ctok
        return ctok

    def rank1_to(dst_sb, row_ap, jj, tokdep=None):
        bank = acq(6)
        if tokdep is not None:
            rec.wait("tensor", tokdep)
        tok = rec.step("tensor", lambda t, bank=bank, row_ap=row_ap, jj=jj:
                       t.matmul(bank[:, jj * 128:(jj + 1) * 128], onesf_r[:, :],
                                row_ap, start=True, stop=True))
        rec.wait("vector", tok)
        ctok = rec.step("vector", lambda v, bank=bank, jj=jj, dst_sb=dst_sb:
                        v.tensor_copy(dst_sb[:, jj * 128:(jj + 1) * 128],
                                      bank[:, jj * 128:(jj + 1) * 128]))
        bkg[6] = ctok
        return ctok

    t_conv = [rec.dma("gpsimd", d[:, :], s[:, :], "d_conv")
              for d, s in [(g_wqkv_i, wqkv_in), (g_wg_i, wg_in), (g_wp_i, wproj_in)]]

    t_pers = [rec.dma("sync", d, s, "d_pers") for d, s in [
        (identf[:, :], identf_in[:, :]), (identb[:, :], identb_in[:, :]),
        (causal[:, :], causal_in[:, :]), (cvec[:, :], cvec_in[:, :]),
        (lng_fm[:, :], lng_in[:, :]), (lnb_fm[:, :], lnb_in[:, :]),
        (b1_fm[:, :], b1fm_in[:, :]), (gnw_r[:, :], gnw_in[:, :]),
        (gnb_r[:, :], gnb_in[:, :]), (b2_r[:, :], b2_in[:, :])]]
    TPERS = t_pers[-1]

    def v_init(v):
        v.memset(stats[:, :], 0.0)
        v.memset(sstat[:, :], 0.0)
        v.memset(stat2[:, :], 0.0)
        v.memset(statB[:, :], 0.0)
        v.memset(wsum[:, :], 0.0)
        v.memset(onesb_c[:, :], 1.0)
        v.memset(onesf_c[:, :], 1.0)
        v.memset(onesb_r[:, :], 1.0)
        v.memset(onesf_r[:, :], 1.0)
        v.memset(S_t[:, :], 0.0)
        return v.memset(Sbf[:, :], 0.0)
    T_VINIT = rec.step("vector", v_init)

    W_TILES = [(w1t_in, 2, INNER), (w2t_in, 8, C)]
    absq = 0
    abs_toks = []
    for src, nt, width in W_TILES:
        for i in range(nt):
            buf = wbuf[absq % 2]
            if absq >= 2:
                rec.wait("sync", abs_toks[absq - 2])
            tok = rec.dma("sync", buf[:, 0:width],
                          src[:, :].rearrange("(c p) n -> p c n", p=128)[:, i, :],
                          f"d_w{absq % 2}")
            rec.wait("vector", tok)
            col = 16 + absq
            abs_toks.append(rec.step(
                "vector", lambda v, buf=buf, width=width, col=col:
                v.tensor_reduce(sstat[:, col:col + 1], buf[:, 0:width],
                                AX.X, ALU.add, apply_absolute_value=True)))
            absq += 1

    def v_comb(v):
        v.tensor_add(sstat[:, 0:1], sstat[:, 16:17], sstat[:, 17:18])
        return v.tensor_reduce(sstat[:, 1:2], sstat[:, 18:26], AX.X, ALU.add)
    tok = rec.step("vector", v_comb)
    rec.wait("tensor", tok)
    rec.wait("tensor", TPERS)
    bank0 = pb[0]
    ttok = rec.step("tensor", lambda t: t.matmul(
        bank0[0:1, 0:2], onesf_c[:, :], sstat[:, 0:2], start=True, stop=True))
    rec.wait("vector", ttok)
    tok = rec.step("vector", lambda v: v.tensor_copy(wsum[0:1, 0:2], bank0[0:1, 0:2]))
    bkg[0] = tok
    rec.wait("scalar", tok)
    tok = rec.dma("scalar", g_ws_i[:, :], wsum[0:1, 0:16], "d_ws")
    rec.wait("gpsimd", tok)
    for t in t_conv:
        rec.wait("gpsimd", t)
    T_AR = rec.step("gpsimd", lambda g: g.collective_compute(
        "AllReduce", ALU.add, replica_groups=RG,
        ins=[g_ws_i.tensor.ap().opt()], outs=[g_ws.tensor.ap().opt()]))
    cc = {}
    for nm, gi, go in [("wqkv", g_wqkv_i, g_wqkv), ("wg", g_wg_i, g_wg),
                       ("wp", g_wp_i, g_wp)]:
        cc[nm] = rec.step("gpsimd", lambda g, gi=gi, go=go: g.collective_compute(
            "AllGather", ALU.bypass, replica_groups=RG,
            ins=[gi.tensor.ap().opt()], outs=[go.tensor.ap().opt()]))

    rec.wait("scalar", T_AR)
    tok = rec.dma("scalar", wsum[0:1, 0:16], g_ws[:, :], "d_ws2")
    rec.wait("vector", tok)

    def v_sw(v):
        v.tensor_scalar(wsum[0:1, 4:6], wsum[0:1, 0:2], 1.0 / (INNER * C), None, ALU.mult)
        v.tensor_scalar(wsum[0:1, 6:8], wsum[0:1, 4:6], 1e-5, None, ALU.max)
        return v.reciprocal(wsum[0:1, 8:10], wsum[0:1, 6:8])
    tok = rec.step("vector", v_sw)
    rec.wait("tensor", tok)
    bank1 = pb[1]
    ttok = rec.step("tensor", lambda t: t.matmul(
        bank1[:, 0:2], onesf_r[:, :], wsum[0:1, 8:10], start=True, stop=True))
    rec.wait("vector", ttok)
    T_SWB = rec.step("vector", lambda v: v.tensor_copy(sw_b[:, 0:2], bank1[:, 0:2]))
    bkg[1] = T_SWB

    stq = 0
    q_store = []
    vq_toks = []
    for wi, (src, nt, width) in enumerate(W_TILES):
        dsti = (g_q1_i, g_q2_i)[wi]
        for i in range(nt):
            buf = wbuf[stq % 2]
            if stq == 0:
                rec.wait("sync", abs_toks[-1])
            if stq >= 2:
                rec.wait("sync", vq_toks[stq - 2])
            tok = rec.dma("sync", buf[:, 0:width],
                          src[:, :].rearrange("(c p) n -> p c n", p=128)[:, i, :],
                          f"d_p2w{stq % 2}")
            rec.wait("vector", tok)
            if q_store:
                rec.wait("vector", q_store[-1])

            def v_q(v, buf=buf, width=width, wi=wi):
                last = None
                for sbi in range(width // C):
                    sl = slice(sbi * C, (sbi + 1) * C)
                    v.tensor_scalar(qbuf[:, sl], buf[:, sl], sw_b[:, wi:wi + 1],
                                    0.5, ALU.mult, ALU.is_gt)
                    v.tensor_scalar(qscr[:, 0:C], buf[:, sl], sw_b[:, wi:wi + 1],
                                    -0.5, ALU.mult, ALU.is_lt)
                    last = v.tensor_sub(qbuf[:, sl], qbuf[:, sl], qscr[:, 0:C])
                return last
            tok = rec.step("vector", v_q)
            vq_toks.append(tok)
            rec.wait("scalar", tok)
            q_store.append(rec.dma(
                "scalar", dsti[:, :].rearrange("(c p) n -> p c n", p=128)[:, i, :],
                qbuf[:, 0:width], "d_qst"))
            stq += 1
    for t in q_store:
        rec.wait("gpsimd", t)
    for nm, gi, go in [("q1", g_q1_i, g_q1), ("q2", g_q2_i, g_q2)]:
        cc[nm] = rec.step("gpsimd", lambda g, gi=gi, go=go: g.collective_compute(
            "AllGather", ALU.bypass, replica_groups=RG,
            ins=[gi.tensor.ap().opt()], outs=[go.tensor.ap().opt()]))

    rec.wait("sync", q_store[-1])
    rec.wait("sync", q_store[-2])
    tok_x = rec.dma("sync", x_tm[:, :].rearrange("p (c n) -> p c n", c=8),
                    x_in[:, :].rearrange("(c p) n -> p c n", p=128), "d_x")
    rec.wait("scalar", tok_x)
    rec.wait("scalar", TPERS)
    rec.wait("scalar", T_VINIT)
    rec.wait("vector", tok_x)

    def sc_ss(s):
        last = None
        for j in range(8):
            last = s.activation(sqd[:, :], x_tm[:, j * C:(j + 1) * C], AF.Square,
                                accum_out=sstat[:, 8 + j:9 + j])
        return last
    T_SS = rec.step("scalar", sc_ss)
    if "xraw" in tap_aps:
        rec.wait("sync", tok_x)
        t = rec.dma("sync", tap_aps["xraw"][:, :], x_tm[:, :], "d_tap")
        for e in ("vector", "scalar", "tensor"):
            rec.wait(e, t)
    rec.wait("vector", T_SS)

    def v_ln(v):
        for j in range(8):
            v.tensor_reduce(sstat[:, j:j + 1], x_tm[:, j * C:(j + 1) * C],
                            AX.X, ALU.add)
        v.tensor_scalar(stats[:, 0:8], sstat[:, 0:8], 1.0 / C, None, ALU.mult)
        v.tensor_scalar(sstat[:, 16:24], sstat[:, 8:16], 1.0 / C, None, ALU.mult)
        v.tensor_tensor(sstat[:, 24:32], stats[:, 0:8], stats[:, 0:8], ALU.mult)
        v.tensor_sub(sstat[:, 32:40], sstat[:, 16:24], sstat[:, 24:32])
        v.tensor_scalar(sstat[:, 40:48], sstat[:, 32:40], EPS, None, ALU.add)
        return v.reciprocal(stats[:, 16:24], sstat[:, 40:48])
    tok = rec.step("vector", v_ln)
    if "sstat1" in tap_aps:
        rec.wait("sync", tok)
        rec.wait("sync", T_SS)
        t = rec.dma("sync", tap_aps["sstat1"][:, :], sstat[:, :], "d_tap")
        for e in ("vector", "scalar", "tensor"):
            rec.wait(e, t)
    rec.wait("scalar", tok)

    def sc_rstd(s):
        s.activation(sstat[:, 48:56], stats[:, 16:24], AF.Ln)
        return s.activation(stats[:, 8:16], sstat[:, 48:56], AF.Exp, scale=0.5)
    tok = rec.step("scalar", sc_rstd)
    rec.wait("vector", tok)

    def v_xn(v):
        for j in range(8):
            v.tensor_tensor(sstat[:, 48 + j:49 + j], stats[:, j:j + 1],
                            stats[:, 8 + j:9 + j], ALU.mult)
            v.tensor_scalar(sstat[:, 48 + j:49 + j], sstat[:, 48 + j:49 + j],
                            -1.0, None, ALU.mult)
            v.tensor_scalar(xn_tm[:, j * C:(j + 1) * C], x_tm[:, j * C:(j + 1) * C],
                            stats[:, 8 + j:9 + j], sstat[:, 48 + j:49 + j],
                            ALU.mult, ALU.add)
        for j in range(8):
            v.tensor_reduce(stats[:, 24 + j:25 + j], xn_tm[:, j * C:(j + 1) * C],
                            AX.X, ALU.max, apply_absolute_value=True)
        v.tensor_tensor(sstat[:, 0:8], sstat[:, 32:40], stats[:, 16:24], ALU.mult)
        v.tensor_scalar(sstat[:, 0:8], sstat[:, 0:8], EPS, None, ALU.add)
        return v.reciprocal(sstat[:, 8:16], sstat[:, 0:8])
    tok = rec.step("vector", v_xn)
    tap_dump("statsA", stats[:, :])
    tap_dump("xn_tm", xn_tm[:, :])
    rec.wait("scalar", tok)

    def sc_rs1(s):
        s.activation(sstat[:, 16:24], sstat[:, 8:16], AF.Ln)
        return s.activation(stats[:, 32:40], sstat[:, 16:24], AF.Exp, scale=0.5)
    tok = rec.step("scalar", sc_rs1)
    rec.wait("vector", tok)

    def v_n1(v):
        v.tensor_tensor(sstat[:, 0:8], stats[:, 32:40], stats[:, 24:32], ALU.mult)
        v.tensor_scalar(sstat[:, 0:8], sstat[:, 0:8], 1e-5, None, ALU.max)
        v.reciprocal(sstat[:, 8:16], sstat[:, 0:8])
        v.tensor_scalar(stats[:, 40:48], sstat[:, 8:16], 127.0, None, ALU.mult)
        v.tensor_tensor(stats[:, 48:56], stats[:, 32:40], stats[:, 40:48], ALU.mult)
        last = None
        MAGIC = 12582912.0
        for j in range(8):
            v.tensor_scalar(xn_tm[:, j * C:(j + 1) * C] if False else n1i_f[:, :],
                            xn_tm[:, j * C:(j + 1) * C],
                            stats[:, 48 + j:49 + j], MAGIC, ALU.mult, ALU.add)
            last = v.tensor_scalar(n1_tm[:, j * C:(j + 1) * C], n1i_f[:, :],
                                   -MAGIC, None, ALU.add)
        return last
    T_N1 = rec.step("vector", v_n1)

    rec.wait("tensor", TPERS)
    rec.wait("tensor", T_N1)
    for src_tm, dst_fm, ident, isbf in [(xn_tm, xn_fm, identf, False),
                                        (n1_tm, n1_fm, identb, True)]:
        for j in range(8):
            for kc in range(KC):
                bkid = 4 + (j * KC + kc) % 4
                bank = acq(bkid)
                bout = bank.ap().bitcast(BF16_) if isbf else bank

                def t_tr(t, s=src_tm, j=j, kc=kc, bout=bout, ident=ident):
                    return t.transpose(
                        bout[:, 0:128],
                        s[:, j * C + kc * 128:j * C + kc * 128 + 128],
                        ident[:, :])
                tok = rec.step("tensor", t_tr)
                rec.wait("vector", tok)
                ctok = rec.step("vector", lambda v, d=dst_fm, j=j, kc=kc, bout=bout:
                                v.tensor_copy(
                                    d[:, kc * NTOK + j * 128:kc * NTOK + j * 128 + 128],
                                    bout[:, 0:128]))
                bkg[bkid] = ctok

    rec.wait_all("sync", "vector")
    T_N1SP = rec.dma("sync", g_n1sp[:, :], n1_fm[:, :], "d_sp")

    def v_sv(v):
        v.tensor_scalar(statB[:, 0:8], stats[:, 40:48], sw_b[:, 0:1], None, ALU.mult)
        return v.reciprocal(statB[:, 8:16], statB[:, 0:8])
    T_SV = rec.step("vector", v_sv)
    rec.wait("tensor", T_SV)
    for j in range(8):
        col_to_row(statB[:, 8 + j:9 + j], c1rows[0:1, j * 128:(j + 1) * 128])
        col_to_row(statB[:, j:j + 1], svrows[0:1, j * 128:(j + 1) * 128])

    rec.wait("sync", cc["wqkv"])
    rec.wait_all("tensor", "vector")
    wsl_tok = [None, None]
    wsl_free = [None, None]

    def load_wslice(sl_i, which):
        s = sl_i % 2
        rec.wait("sync", wsl_free[s])
        src = {"qkv": g_wqkv}[which]
        wsl_tok[s] = rec.dma("sync", wsl[s][:, :].rearrange("p (c n) -> p c n", c=KC),
                             src[:, sl_i * 512:(sl_i + 1) * 512]
                             .rearrange("(c p) n -> p c n", p=128), f"d_wsl{s}")

    qk_tiles = [("q", hh, t2) for hh in range(H) for t2 in range(2)] + \
               [("k", hh, t2) for hh in range(H) for t2 in range(2)]
    tabslot_free = [None] * 2
    load_wslice(0, "qkv")
    cur_slice = [0]
    for pidx in range(0, len(qk_tiles), 2):
        pair_ctx = []
        for idx in (pidx, pidx + 1):
            qk, hh, t2 = qk_tiles[idx]
            sl_i = hh // 4 + (0 if qk == "q" else 4)
            if sl_i > cur_slice[0]:
                wsl_free[sl_i % 2] = rec.cur_tok("tensor")
                load_wslice(sl_i, "qkv")
                cur_slice[0] = sl_i
            slot = idx % 2
            rec.wait("sync", tabslot_free[slot])
            rec.sem("d_tab")
            rec.inc("d_tab", 32)
            ttok = ("d_tab", rec.sems["d_tab"][1])
            h_tab = rec.sems["d_tab"][0]

            def sy_tab(sy, slot=slot, qk=qk, hh=hh, t2=t2, h_tab=h_tab):
                sy.dma_start(out=tabr[:, slot * 1024:slot * 1024 + 512],
                             in_=tab_in[(qk, 0)][hh * 128:(hh + 1) * 128,
                                                 t2 * 512:(t2 + 1) * 512]).then_inc(h_tab, 16)
                sy.dma_start(out=tabr[:, slot * 1024 + 512:slot * 1024 + 1024],
                             in_=tab_in[(qk, 1)][hh * 128:(hh + 1) * 128,
                                                 t2 * 512:(t2 + 1) * 512]).then_inc(h_tab, 16)
            rec.ops["sync"].append(sy_tab)

            bkid = idx % 2
            bank = acq(bkid)
            rec.wait("tensor", wsl_tok[sl_i % 2])

            def t_mm(t, buf=wsl[sl_i % 2], co=hh % 4, t2=t2, bank=bank):
                i = None
                for kc in range(KC):
                    i = t.matmul(bank[:, :],
                                 buf[:, kc * 512 + co * 128:kc * 512 + co * 128 + 128],
                                 xn_fm[:, kc * NTOK + t2 * 512:kc * NTOK + (t2 + 1) * 512],
                                 start=(kc == 0), stop=(kc == KC - 1))
                return i
            mtok = rec.step("tensor", t_mm)
            dst = q2_a if qk == "q" else k2_a
            pair_ctx.append((bank, dst, hh, t2, slot, mtok, ttok))

        for (_, _, _, _, _, mtok, ttok) in pair_ctx:
            rec.wait("vector", mtok)
            rec.wait("vector", ttok)

        def v_xpair(v, ctxs=tuple(pair_ctx)):
            frs = ((fr1, fr2), (fr1b, fr2b))
            oaps = []
            for (bank, dst, hh, t2, slot, _, _), (f1, f2) in zip(ctxs, frs):
                ct = tabr[:, slot * 1024:slot * 1024 + 512]
                st = tabr[:, slot * 1024 + 512:slot * 1024 + 1024]
                o = dst[:, hh * NTOK + t2 * 512:hh * NTOK + (t2 + 1) * 512]
                v.tensor_tensor(f2[:, :], bank[:, :], ct[:, :], ALU.mult)
                v.tensor_tensor(f1[0:64, :], bank[64:128, :], st[0:64, :], ALU.mult)
                v.tensor_tensor(f1[64:128, :], bank[0:64, :], st[64:128, :], ALU.mult)
                oaps.append((o, f1, f2))
            last = None
            for o, f1, f2 in oaps:
                v.tensor_sub(o[0:64, :], f2[0:64, :], f1[0:64, :])
                last = v.tensor_add(o[64:128, :], f2[64:128, :], f1[64:128, :])
            return last
        ctok = rec.step("vector", v_xpair)
        bkg[0] = ctok
        bkg[1] = ctok
        tabslot_free[0] = ctok
        tabslot_free[1] = ctok
    wsl_free[0] = rec.cur_tok("tensor")
    wsl_free[1] = rec.cur_tok("tensor")

    vsp = []
    for vs in range(4):
        sl_i = 8 + vs
        load_wslice(sl_i, "qkv")
        for tt in range(8):
            bkid = (vs * 8 + tt) % 2
            bank = acq(bkid)
            rec.wait("tensor", wsl_tok[sl_i % 2])

            def t_vmm(t, buf=wsl[sl_i % 2], tt=tt, bank=bank):
                i = None
                for kc in range(KC):
                    i = t.matmul(bank[:, :],
                                 xn_fm[:, kc * NTOK + tt * 128:kc * NTOK + tt * 128 + 128],
                                 buf[:, kc * 512:(kc + 1) * 512],
                                 start=(kc == 0), stop=(kc == KC - 1))
                return i
            mtok = rec.step("tensor", t_vmm)
            rec.wait("vector", mtok)
            if len(vsp) >= 2:
                rec.wait("vector", vsp[-2])
            vr = v_r[(vs * 8 + tt) % 2]
            ctok = rec.step("vector", lambda v, bank=bank, vr=vr:
                            v.tensor_copy(vr[:, 0:512], bank[:, :]))
            bkg[bkid] = ctok
            rec.wait("sync", ctok)
            vsp.append(rec.dma(
                "sync", g_vsp[:, tt * C + vs * 512:tt * C + (vs + 1) * 512],
                vr[:, 0:512], "d_vsp"))
        wsl_free[sl_i % 2] = rec.cur_tok("tensor")

    tap_dump("q2", q2_a[:, :])
    tap_dump("k2", k2_a[:, :])
    if "v" in tap_aps:
        for t in vsp:
            rec.wait("sync", t)
        rec.dma("sync", tap_aps["v"][:, :], g_vsp[:, :], "d_tap")

    ysp = {}
    vld = [None, None]
    te_chunk = {}
    ve_chunk = {}
    T_SBF = None
    for n in range(NCH):
        vr = v_r[n % 2]
        if n < 2:
            for t in vsp:
                rec.wait("sync", t)
        else:
            rec.wait("sync", ("c_te", te_chunk[n - 2]))
        vld[n % 2] = rec.dma("sync", vr[:, :], g_vsp[:, n * C:(n + 1) * C], "d_vld")
        if n > 0:
            rec.wait("vector", ("c_ve", ve_chunk[n - 1]))
            T_SBF = rec.step("vector", lambda v: v.tensor_copy(Sbf[:, :], S_t[:, :]))
        yslot = y_r[n % 2]
        if n >= 2:
            rec.wait("scalar", ysp[n - 2])
        for h in range(H):
            i = n * H + h
            q2c = q2_a[:, h * NTOK + n * 128:h * NTOK + n * 128 + 128]
            k2c = k2_a[:, h * NTOK + n * 128:h * NTOK + n * 128 + 128]
            sm = smr[i % 2]
            kd = kdr[i % 2]
            bank_s = acq(i % 2)
            stok = rec.step("tensor", lambda t, bank=bank_s, k2c=k2c, q2c=q2c:
                            t.matmul(bank[:, 0:128], k2c, q2c, start=True, stop=True))
            bank_k = acq(4 + i % 2)
            bkb = bank_k.ap().bitcast(BF16_)
            ktok = rec.step("tensor", lambda t, bkb=bkb, k2c=k2c:
                            t.transpose(bkb[:, 0:128], k2c, identb[:, :]))
            rec.wait("vector", stok)
            smtok = rec.step("vector", lambda v, sm=sm, bank=bank_s:
                             v.tensor_tensor(sm[:, :], bank[:, 0:128],
                                             causal[:, :], ALU.mult))
            bkg[i % 2] = smtok
            rec.wait("scalar", ktok)
            kdtok = rec.step("scalar", lambda s, kd=kd, bkb=bkb, h=h:
                             s.activation(kd[:, :], bkb[:, 0:128], AF.Copy,
                                          scale=GL[h]))
            bkg[4 + i % 2] = kdtok
            bank_y = acq(2 + i % 2)
            rec.wait("tensor", smtok)
            rec.wait("tensor", vld[n % 2])
            if n > 0:
                rec.wait("tensor", T_SBF)

            def t_y(t, bank=bank_y, vr=vr, h=h, sm=sm, q2c=q2c, n=n):
                i1 = t.matmul(bank[:, 0:128], vr[:, h * 128:(h + 1) * 128],
                              sm[:, :], start=True, stop=(n == 0))
                if n > 0:
                    i1 = t.matmul(bank[:, 0:128], Sbf[:, h * 128:(h + 1) * 128],
                                  q2c, start=False, stop=True)
                return i1
            ytok = rec.step("tensor", t_y)
            bank_u = acq(6 + i % 2)
            rec.wait("tensor", kdtok)
            utok = rec.step("tensor", lambda t, bank=bank_u, kd=kd, vr=vr, h=h:
                            t.matmul(bank[:, 0:128], kd[:, :],
                                     vr[:, h * 128:(h + 1) * 128],
                                     start=True, stop=True))
            rec.wait("scalar", ytok)
            yetok = rec.step("scalar", lambda s, yslot=yslot, h=h, bank=bank_y:
                             s.activation(yslot[:, h * 128:(h + 1) * 128],
                                          bank[:, 0:128], AF.Copy))
            bkg[2 + i % 2] = yetok
            rec.wait("vector", utok)
            sutok = rec.step("vector", lambda v, h=h, bank=bank_u:
                             v.scalar_tensor_tensor(
                                 S_t[:, h * 128:(h + 1) * 128],
                                 S_t[:, h * 128:(h + 1) * 128], GL[h],
                                 bank[:, 0:128], ALU.mult, ALU.add))
            bkg[6 + i % 2] = sutok
        te_chunk[n] = rec.sems["c_te"][1]
        ve_chunk[n] = rec.sems["c_ve"][1]
        rec.wait_all("sync", "scalar")
        ysp[n] = rec.dma(
            "sync",
            g_ysp[:, :].rearrange("p (hh tt) -> p hh tt", hh=H)[:, :, n * 128:(n + 1) * 128],
            yslot[:, :].rearrange("p (hh ll) -> p hh ll", hh=H), "d_ysp")

    tap_dump("S", S_t[:, :])
    tap_dump("stats", stats[:, :])
    if "y" in tap_aps:
        for n in range(NCH):
            rec.wait("sync", ysp[n])
        rec.dma("sync", tap_aps["y"][:, :], g_ysp[:, :], "d_tap")
    rec.wait_all("sync", "vector")
    T_QSP = rec.dma("sync", g_qsp[:, :], q2_a[:, :], "d_sp")
    T_XNSP = rec.dma("sync", g_xnsp[:, :], xn_fm[:, :], "d_sp")
    T_SLSP = rec.dma("sync", g_sl_i[:, :], S_t[:, :], "d_sp")
    rec.wait("gpsimd", T_SLSP)
    cc["sl"] = rec.step("gpsimd", lambda g: g.collective_compute(
        "AllGather", ALU.bypass, replica_groups=RG,
        ins=[g_sl_i.tensor.ap().opt()], outs=[g_sl.tensor.ap().opt()]))

    rec.wait("sync", T_N1SP)
    rec.wait("sync", T_QSP)
    rec.wait("sync", T_XNSP)
    for n in range(NCH):
        rec.wait("sync", ysp[n])
    rec.wait_all("sync", "tensor")
    rec.wait_all("sync", "vector")
    rec.wait_all("sync", "scalar")
    T_N1R = rec.dma("sync", n1r[:, :], g_n1sp[:, :], "d_n1r")
    rec.wait("sync", cc["q1"])
    rec.wait("scalar", T_QSP)
    rec.wait("scalar", T_XNSP)
    rec.wait("vector", T_QSP)
    rec.wait("vector", T_XNSP)

    for hf in range(2):
        ctk = None
        for jj in range(4):
            j = hf * 4 + jj
            ctk = rank1_to(C1R, c1rows[0:1, j * 128:(j + 1) * 128], jj)

        q1free = [None]
        ep_prev = []
        for it in range(IT):
            rec.wait("sync", q1free[0])
            q1tok = rec.dma("sync", q1b[0][:, :].rearrange("p (c n) -> p c n", c=KC),
                            g_q1[:, it * 128:(it + 1) * 128]
                            .rearrange("(c p) n -> p c n", p=128), "d_q1")
            bkid = it % 2
            bank = acq(bkid)
            rec.wait("tensor", q1tok)
            if it == 0:
                rec.wait("tensor", T_N1R)
                rec.wait("tensor", ctk)

            def t_mm1(t, qb=q1b[0], hf=hf, bank=bank):
                i = None
                for kc in range(KC):
                    i = t.matmul(bank[:, :], qb[:, kc * 128:(kc + 1) * 128],
                                 n1r[:, kc * NTOK + hf * 512:kc * NTOK + (hf + 1) * 512],
                                 start=(kc == 0), stop=(kc == KC - 1))
                return i
            mtok = rec.step("tensor", t_mm1)
            q1free[0] = rec.cur_tok("tensor")
            rec.wait("vector", mtok)
            if ep_prev:
                rec.wait("vector", ep_prev[-1])
            vtok = rec.step("vector", lambda v, bank=bank:
                            v.tensor_tensor(fr1[:, :], bank[:, :], C1R[:, :], ALU.mult))
            bkg[bkid] = vtok
            rec.wait("scalar", vtok)
            atok = rec.step("scalar", lambda a, it=it:
                            a.activation(harena[:, it * 512:(it + 1) * 512],
                                         fr1[:, :], AF.Silu, bias=b1_fm[:, it:it + 1]))
            ep_prev.append(atok)

        gel = []
        for it in range(IT):
            r = gbr[it % 2]
            if it < 2:
                rec.wait("scalar", ep_prev[-1])
            else:
                rec.wait("scalar", gel[it - 2][1])
            gt = rec.step("scalar", lambda a, it=it, r=r:
                          a.activation(r[:, :], harena[:, it * 512:(it + 1) * 512],
                                       AF.Gelu))
            rec.wait("vector", gt)
            ct = rec.step("vector", lambda v, it=it, r=r:
                          v.tensor_copy(harena[:, it * 512:(it + 1) * 512], r[:, :]))
            gel.append((gt, ct))

        if "h" in tap_aps:
            rec.wait_all("sync", "vector")
            rec.wait_all("sync", "scalar")
            rec.wait_all("sync", "tensor")
            t = rec.dma("sync", tap_aps["h"][:, hf * IT * 512:(hf + 1) * IT * 512],
                        harena[:, :], "d_tap")
            for e in ("vector", "scalar", "tensor"):
                rec.wait(e, t)
        bank7 = acq(7)
        sq = []
        for it in range(IT):
            r = sqrr[it % 2]
            rec.wait("scalar", gel[it][1])
            if it >= 2:
                rec.wait("scalar", sq[it - 2][1])
            qt = rec.step("scalar", lambda a, it=it, r=r:
                          a.activation(r[:, :], harena[:, it * 512:(it + 1) * 512],
                                       AF.Square))
            rec.wait("tensor", qt)
            if it == 0:
                rec.wait("tensor", gel[-1][1])

            def t_st(t, it=it, r=r, bank7=bank7):
                i = None
                for sub in range(4):
                    t.matmul(bank7[:, sub:sub + 1],
                             harena[:, it * 512 + sub * 128:it * 512 + (sub + 1) * 128],
                             onesb_c[:, :], start=(it == 0 and sub == 0),
                             stop=False)
                    i = t.matmul(bank7[:, 8 + sub:9 + sub],
                                 r[:, sub * 128:(sub + 1) * 128], onesb_c[:, :],
                                 start=False,
                                 stop=(it == IT - 1 and sub == 3))
                return i
            sq.append((qt, rec.step("tensor", t_st)))
        rec.wait("vector", sq[-1][1])

        def v_hst(v, bank7=bank7):
            v.tensor_scalar(stat2[:, 0:4], bank7[:, 0:4], 1.0 / INNER, None, ALU.mult)
            v.tensor_scalar(stat2[:, 4:8], bank7[:, 8:12], 1.0 / INNER, None, ALU.mult)
            v.tensor_tensor(stat2[:, 8:12], stat2[:, 0:4], stat2[:, 0:4], ALU.mult)
            v.tensor_sub(stat2[:, 12:16], stat2[:, 4:8], stat2[:, 8:12])
            v.tensor_scalar(stat2[:, 16:20], stat2[:, 12:16], EPS, None, ALU.add)
            return v.reciprocal(stat2[:, 20:24], stat2[:, 16:20])
        vt = rec.step("vector", v_hst)
        bkg[7] = vt
        rec.wait("scalar", vt)
        at2 = rec.step("scalar", lambda a: (
            a.activation(stat2[:, 24:28], stat2[:, 20:24], AF.Ln),
            a.activation(stat2[:, 28:32], stat2[:, 24:28], AF.Exp, scale=0.5))[1])
        rec.wait("vector", at2)
        vt = rec.step("vector", lambda v: v.tensor_tensor(
            stat2[:, 32:36], stat2[:, 0:4], stat2[:, 28:32], ALU.mult))
        rec.wait("tensor", vt)
        for jj in range(4):
            col_to_row(stat2[:, 28 + jj:29 + jj], rowf[0:1, :])
            rank1_to(R1, rowf[0:1, :], jj, tokdep=rec.cur_tok("vector"))
            col_to_row(stat2[:, 32 + jj:33 + jj], rowf[0:1, :])
            rank1_to(M1R, rowf[0:1, :], jj, tokdep=rec.cur_tok("vector"))

        T_RM = rec.step("vector", lambda v: (v.memset(rmax[:, :], -1e30),
                                             v.memset(rmin[:, :], 1e30))[1])
        hln = []
        for it in range(IT):
            def v_hln(v, it=it):
                v.tensor_tensor(fr1[:, :], harena[:, it * 512:(it + 1) * 512],
                                R1[:, :], ALU.mult)
                v.tensor_sub(fr2[:, :], fr1[:, :], M1R[:, :])
                v.tensor_scalar(fr1[:, :], fr2[:, :], lng_fm[:, it:it + 1],
                                lnb_fm[:, it:it + 1], ALU.mult, ALU.add)
                v.tensor_copy(harena[:, it * 512:(it + 1) * 512], fr1[:, :])
                v.tensor_max(rmax[:, :], rmax[:, :], fr1[:, :])
                return v.tensor_tensor(rmin[:, :], rmin[:, :], fr1[:, :], ALU.min)
            hln.append(rec.step("vector", v_hln))
        vt0 = rec.step("vector", lambda v: (
            v.tensor_scalar(fr1[:, :], rmin[:, :], -1.0, None, ALU.mult),
            v.tensor_max(fr2[:, :], rmax[:, :], fr1[:, :]))[1])
        rec.wait("tensor", vt0)
        for jj in range(4):
            bank = acq(5)
            tok = rec.step("tensor", lambda t, jj=jj, bank=bank:
                           t.transpose(bank[:, 0:128],
                                       fr2[:, jj * 128:(jj + 1) * 128], identf[:, :]))
            rec.wait("vector", tok)
            ct = rec.step("vector", lambda v, jj=jj, bank=bank:
                          v.tensor_reduce(stat2[:, 36 + jj:37 + jj], bank[:, 0:128],
                                          AX.X, ALU.max))
            bkg[5] = ct

        bank7 = acq(7)
        sq2 = []
        for it in range(IT):
            r = sqrr[it % 2]
            if it >= 2:
                rec.wait("scalar", sq2[it - 2][1])
            rec.wait("scalar", hln[it] if it != IT - 1 else hln[-1])
            qt = rec.step("scalar", lambda a, it=it, r=r:
                          a.activation(r[:, :], harena[:, it * 512:(it + 1) * 512],
                                       AF.Square))
            rec.wait("tensor", qt)

            def t_st2(t, it=it, r=r, bank7=bank7):
                i = None
                for sub in range(4):
                    i = t.matmul(bank7[:, 16 + sub:17 + sub],
                                 r[:, sub * 128:(sub + 1) * 128], onesb_c[:, :],
                                 start=(it == 0 and sub == 0),
                                 stop=(it == IT - 1 and sub == 3))
                return i
            sq2.append((qt, rec.step("tensor", t_st2)))
        rec.wait("vector", sq2[-1][1])

        def v_rms2(v, bank7=bank7):
            v.tensor_scalar(stat2[:, 40:44], bank7[:, 16:20], 1.0 / INNER, None, ALU.mult)
            v.tensor_scalar(stat2[:, 44:48], stat2[:, 40:44], EPS, None, ALU.add)
            return v.reciprocal(stat2[:, 48:52], stat2[:, 44:48])
        vt = rec.step("vector", v_rms2)
        bkg[7] = vt
        rec.wait("scalar", vt)
        at3 = rec.step("scalar", lambda a: (
            a.activation(stat2[:, 52:56], stat2[:, 48:52], AF.Ln),
            a.activation(stat2[:, 56:60], stat2[:, 52:56], AF.Exp, scale=0.5))[1])
        rec.wait("vector", at3)

        def v_c2(v, hf=hf):
            v.tensor_tensor(statB[:, 16:20], stat2[:, 56:60], stat2[:, 36:40], ALU.mult)
            v.tensor_scalar(statB[:, 16:20], statB[:, 16:20], 1e-5, None, ALU.max)
            v.reciprocal(statB[:, 20:24], statB[:, 16:20])
            v.tensor_scalar(statB[:, 24:28], statB[:, 20:24], 127.0, None, ALU.mult)
            v.tensor_tensor(statB[:, 28:32], stat2[:, 56:60], statB[:, 24:28], ALU.mult)
            return v.tensor_scalar(statB[:, 32:36], statB[:, 24:28], sw_b[:, 1:2], None, ALU.mult)
        vtA = rec.step("vector", v_c2)
        rec.wait("scalar", vtA)
        stA = rec.step("scalar", lambda a: a.nop())
        rec.wait("vector", stA)
        vt = rec.step("vector", lambda v: v.reciprocal(scl4[:, 0:4], statB[:, 32:36]))
        rec.wait("scalar", vt)
        stB = rec.step("scalar", lambda a: a.nop())
        rec.wait("vector", stB)
        vt2 = vt
        if hf == 0:
            tap_dump("statB", statB[:, :])
            tap_dump("scl4", scl4[:, :])
        rec.wait("tensor", vt)
        for jj in range(4):
            j = hf * 4 + jj
            col_to_row(statB[:, 28 + jj:29 + jj], rowf[0:1, :])
            rank1_to(C2R, rowf[0:1, :], jj, tokdep=rec.cur_tok("vector"))
            col_to_row(statB[:, 32 + jj:33 + jj], sv2rows[0:1, j * 128:(j + 1) * 128])

        n2t = []
        for it in range(IT):
            def v_n2(v, it=it):
                MAGIC = 12582912.0
                v.tensor_tensor(fr1[:, :], harena[:, it * 512:(it + 1) * 512],
                                C2R[:, :], ALU.mult)
                v.tensor_scalar(fr2[:, :], fr1[:, :], MAGIC, None, ALU.add)
                return v.tensor_scalar(harena[:, it * 512:(it + 1) * 512],
                                       fr2[:, :], -MAGIC, None, ALU.add)
            n2t.append(rec.step("vector", v_n2))
        if "n2" in tap_aps:
            rec.wait_all("sync", "vector")
            rec.wait_all("sync", "tensor")
            t = rec.dma("sync", tap_aps["n2"][:, hf * IT * 512:(hf + 1) * IT * 512],
                        harena[:, :], "d_tap")
            for e in ("vector", "scalar", "tensor"):
                rec.wait(e, t)

        rec.wait("sync", cc["q2"])
        w2tok = [None, None]
        w2free = [None, None]
        for cs in range(4):
            banks4 = [acq(tcj) for tcj in range(4)]
            rec.wait("tensor", n2t[-1])
            if cs > 0:
                w2free[0] = rec.cur_tok("tensor")
                w2free[1] = rec.cur_tok("tensor")
            for ikb in range(16):
                s = ikb % 2
                rec.wait("sync", w2free[s])
                w2tok[s] = rec.dma(
                    "sync", w2b[s][:, :].rearrange("p (c n) -> p c n", c=4),
                    g_q2[ikb * 512:(ikb + 1) * 512, cs * 512:(cs + 1) * 512]
                    .rearrange("(c p) n -> p c n", p=128), f"d_w2{s}")
                rec.wait("tensor", w2tok[s])

                def t_mm2(t, s=s, ikb=ikb, banks4=banks4):
                    i = None
                    for ikc in range(4):
                        it = ikb * 4 + ikc
                        for tcj in range(4):
                            i = t.matmul(
                                banks4[tcj][:, :],
                                harena[:, it * 512 + tcj * 128:it * 512 + (tcj + 1) * 128],
                                w2b[s][:, ikc * 512:(ikc + 1) * 512],
                                start=(it == 0), stop=False)
                    return i
                rec.step("tensor", t_mm2)
                w2free[s] = rec.cur_tok("tensor")
            for tcj in range(4):
                j = hf * 4 + tcj
                bank = banks4[tcj]
                if cs == 0 and tcj == 0:
                    rec.wait("tensor", rec.cur_tok("vector"))
                btok = rec.step("tensor", lambda t, bank=bank, j=j, cs=cs:
                                t.matmul(bank[:, :],
                                         sv2rows[0:1, j * 128:(j + 1) * 128],
                                         b2_r[0:1, cs * 512:(cs + 1) * 512],
                                         start=False, stop=True))
                rec.wait("vector", btok)
                rec.wait("vector", vt2)
                if cs > 0 or tcj > 0:
                    rec.wait("vector", rec.cur_tok("scalar"))
                vtok2 = rec.step("vector", lambda v, bank=bank, tcj=tcj:
                                 v.tensor_scalar(fr1[:, :], bank[:, :],
                                                 scl4[:, tcj:tcj + 1], None, ALU.mult))
                bkg[tcj] = vtok2
                rec.wait("scalar", vtok2)
                etok = rec.step("scalar", lambda a, j=j, cs=cs:
                                a.activation(
                                    g_tm[:, j * C + cs * 512:j * C + (cs + 1) * 512],
                                    fr1[:, :], AF.Silu))
    tap_dump("g", g_tm[:, :])

    TE_S3 = rec.cur_tok("tensor")
    AC_S3 = rec.cur_tok("scalar")
    VE_S3 = rec.cur_tok("vector")
    rec.wait("sync", T_XNSP)
    rec.wait("sync", TE_S3)
    rec.wait("sync", AC_S3)
    rec.wait("sync", VE_S3)
    T_XNR = rec.dma("sync", xn_fm4[:, :], g_xnsp[:, :], "d_xnr")
    rec.wait("gpsimd", cc["sl"])
    rec.wait("gpsimd", TE_S3)
    rec.wait("gpsimd", AC_S3)
    rec.wait("vector", TE_S3)
    rec.wait("vector", AC_S3)
    rec.wait("scalar", TE_S3)
    rec.wait("scalar", VE_S3)
    T_SI0 = rec.step("vector", lambda v: v.memset(sincA[:, :], 0.0))
    sl_toks = []
    prev = T_SI0
    srcbuf, dstbuf = sincA, sincB
    for pp in range(NCORES):
        sslot = sring[pp % 2]
        if pp >= 2:
            rec.wait("gpsimd", sl_toks[pp - 2][1])
        ltok = rec.dma("gpsimd", sslot[:, :], g_sl[pp * 128:(pp + 1) * 128, :],
                       "d_sl")
        rec.wait("vector", ltok)
        rec.wait("vector", prev)

        def v_si(v, pp=pp, sslot=sslot, src=srcbuf, dst=dstbuf):
            last = None
            for h in range(H):
                last = v.scalar_tensor_tensor(
                    dst[:, h * 128:(h + 1) * 128],
                    sslot[:, h * 128:(h + 1) * 128],
                    cvec[:, pp * H + h:pp * H + h + 1],
                    src[:, h * 128:(h + 1) * 128], ALU.mult, ALU.add)
            return last
        prev = rec.step("vector", v_si)
        sl_toks.append((ltok, prev))
        srcbuf, dstbuf = dstbuf, srcbuf
    T_SINC = rec.step("vector", lambda v, src=srcbuf: v.tensor_copy(
        sincbf[:, :], src[:, :]))
    if "sinc" in tap_aps:
        rec.wait_all("sync", "vector")
        rec.dma("sync", tap_aps["sinc"][:, :], sincbf[:, :], "d_tap")

    rec.wait("sync", cc["wg"])
    wg_tok = [None]
    wg_free = [None]
    y4tok = [None, None]
    q4tok = [None, None]
    gs_done = [None, None]
    sp_done = []
    ve_h = {}
    for h in range(H):
        if h % 4 == 0:
            hs = h // 4
            rec.wait("sync", wg_free[0])
            if h == 0:
                rec.wait_all("sync", "tensor")
            wg_tok[0] = rec.dma("sync", wsl4[:, :].rearrange("p (c n) -> p c n", c=KC),
                                g_wg[:, hs * 512:(hs + 1) * 512]
                                .rearrange("(c p) n -> p c n", p=128), "d_wg")
        if h >= 2:
            rec.wait("sync", ("c_ve", ve_h[h - 2]))
        else:
            rec.wait("sync", T_SINC)
        y4tok[h % 2] = rec.dma("sync", y4r[h % 2][:, :],
                               g_ysp[:, h * NTOK:(h + 1) * NTOK], "d_y4")
        q4tok[h % 2] = rec.dma("sync", q4r[h % 2][:, :],
                               g_qsp[:, h * NTOK:(h + 1) * NTOK], "d_q4")
        gslot = gsil[h % 2]
        for t2 in range(2):
            bank = acq(4 + t2)
            rec.wait("tensor", wg_tok[0])
            if h == 0 and t2 == 0:
                rec.wait("tensor", T_XNR)

            def t_wg(t, h=h, t2=t2, bank=bank):
                i = None
                for kc in range(KC):
                    i = t.matmul(bank[:, :],
                                 wsl4[:, kc * 512 + (h % 4) * 128:kc * 512 + (h % 4) * 128 + 128],
                                 xn_fm4[:, kc * NTOK + t2 * 512:kc * NTOK + (t2 + 1) * 512],
                                 start=(kc == 0), stop=(kc == KC - 1))
                return i
            wtok = rec.step("tensor", t_wg)
            rec.wait("scalar", wtok)
            if h < 2:
                rec.wait("scalar", T_SINC)
            if gs_done[h % 2] is not None:
                rec.wait("scalar", gs_done[h % 2])
            gtok = rec.step("scalar", lambda a, bank=bank, gslot=gslot, t2=t2:
                            a.activation(gslot[:, t2 * 512:(t2 + 1) * 512],
                                         bank[:, :], AF.Silu))
            bkg[4 + t2] = gtok
        if h % 4 == 3:
            wg_free[0] = rec.cur_tok("tensor")
        bank7 = acq(7)
        yt_toks = {}
        for n in range(NCH):
            rec.wait("vector", q4tok[h % 2])
            qd = qdr4[n % 2]
            gnl = float(np.float32(GAMMA[h] ** (n * L)))
            if n >= 2:
                rec.wait("vector", yt_toks[n - 2][1])
            qtok = rec.step("vector", lambda v, qd=qd, h=h, n=n, gnl=gnl:
                            v.tensor_scalar(qd[:, :],
                                            q4r[h % 2][:, n * 128:(n + 1) * 128],
                                            gnl, None, ALU.mult))
            bank6 = acq(5)
            rec.wait("tensor", qtok)
            rec.wait("tensor", T_SINC)
            itok = rec.step("tensor", lambda t, bank=bank6, h=h, qd=qd:
                            t.matmul(bank[:, 0:128],
                                     sincbf[:, h * 128:(h + 1) * 128], qd[:, :],
                                     start=True, stop=True))
            rec.wait("vector", itok)
            rec.wait("vector", y4tok[h % 2])
            yttok = rec.step("vector", lambda v, bank=bank6, h=h, n=n:
                             v.tensor_tensor(ybuf[:, n * 128:(n + 1) * 128],
                                             y4r[h % 2][:, n * 128:(n + 1) * 128],
                                             bank[:, 0:128], ALU.add))
            bkg[5] = yttok
            rec.wait("scalar", yttok)
            if n >= 1:
                rec.wait("scalar", yt_toks[n - 1][2])
            sqtok = rec.step("scalar", lambda a, n=n:
                             a.activation(ysq[:, :], ybuf[:, n * 128:(n + 1) * 128],
                                          AF.Square))
            rec.wait("tensor", sqtok)

            def t_gst(t, n=n, bank7=bank7):
                t.matmul(bank7[:, n:n + 1], ybuf[:, n * 128:(n + 1) * 128],
                         onesb_c[:, :], start=(n == 0), stop=False)
                return t.matmul(bank7[:, 8 + n:9 + n], ysq[:, :], onesb_c[:, :],
                                start=False, stop=(n == NCH - 1))
            sttok = rec.step("tensor", t_gst)
            yt_toks[n] = (qtok, yttok, sttok)
        rec.wait("vector", yt_toks[NCH - 1][2])

        def v_gn(v, bank7=bank7):
            last = None
            for g4 in range(4):
                sl2 = slice(2 * g4, 2 * g4 + 2)
                v.tensor_scalar(gnmu[:, sl2], bank7[:, 2 * g4:2 * g4 + 2],
                                1.0 / D, None, ALU.mult)
            for g4 in range(4):
                sl2 = slice(2 * g4, 2 * g4 + 2)
                v.tensor_scalar(gnrs[:, sl2], bank7[:, 8 + 2 * g4:10 + 2 * g4],
                                1.0 / D, None, ALU.mult)
            for g4 in range(4):
                sl2 = slice(2 * g4, 2 * g4 + 2)
                v.tensor_tensor(gnmp[:, sl2], gnmu[:, sl2], gnmu[:, sl2], ALU.mult)
            for g4 in range(4):
                sl2 = slice(2 * g4, 2 * g4 + 2)
                v.tensor_sub(gnrs[:, sl2], gnrs[:, sl2], gnmp[:, sl2])
            for g4 in range(4):
                sl2 = slice(2 * g4, 2 * g4 + 2)
                last = v.tensor_scalar(gnrs[:, sl2], gnrs[:, sl2], EPS, None, ALU.add)
            return last
        vt = rec.step("vector", v_gn)
        bkg[7] = vt
        if h == 1:
            tap_dump("ybuf0", ybuf[:, :])
            tap_dump("sinc2", sincbf[:, :])
        if h == 1:
            tap_dump("gn0", gnrs[:, 0:16])
        rec.wait("scalar", vt)
        at4 = rec.step("scalar", lambda a: (
            a.activation(gnmp[:, 0:8], gnrs[:, 0:8], AF.Ln),
            a.activation(gnrs[:, 0:8], gnmp[:, 0:8], AF.Exp, scale=-0.5))[1])
        rec.wait("vector", at4)
        vt = rec.step("vector", lambda v: (
            v.tensor_tensor(gnmp[:, 0:8], gnmu[:, 0:8], gnrs[:, 0:8], ALU.mult),
            v.tensor_scalar(gnmp[:, 0:8], gnmp[:, 0:8], -1.0, None, ALU.mult))[1])
        rec.wait("tensor", vt)
        for n in range(NCH):
            rtok = col_to_row(gnrs[:, n:n + 1], rowb[0:1, :], bf16=True)
            bankA = acq(2)
            rec.wait("tensor", rtok)
            atok = rec.step("tensor", lambda t, bank=bankA, h=h:
                            t.matmul(bank[:, 0:128],
                                     gnw_r[0:1, h * 128:(h + 1) * 128],
                                     rowb[0:1, :], start=True, stop=True))
            rtok = col_to_row(gnmp[:, n:n + 1], rowb[0:1, :], bf16=True)
            bankB = acq(3)
            rec.wait("tensor", rtok)

            def t_B(t, bank=bankB, h=h):
                t.matmul(bank[:, 0:128], gnw_r[0:1, h * 128:(h + 1) * 128],
                         rowb[0:1, :], start=True, stop=False)
                return t.matmul(bank[:, 0:128], gnb_r[0:1, h * 128:(h + 1) * 128],
                                onesb_r[:, :], start=False, stop=True)
            btok = rec.step("tensor", t_B)
            rec.wait("vector", atok)
            rec.wait("vector", btok)
            rec.wait("vector", ("c_ac", rec.sems["c_ac"][1]))

            def v_sp(v, bankA=bankA, bankB=bankB, h=h, n=n, gslot=gslot):
                for qq in range(4):
                    s32 = slice(qq * 32, qq * 32 + 32)
                    v.tensor_tensor(t4a[:, s32], ybuf[:, n * 128 + qq * 32:
                                                      n * 128 + qq * 32 + 32],
                                    bankA[:, s32], ALU.mult)
                for qq in range(4):
                    s32 = slice(qq * 32, qq * 32 + 32)
                    v.tensor_add(t4b[:, s32], t4a[:, s32], bankB[:, s32])
                last = None
                for qq in range(4):
                    s32 = slice(qq * 32, qq * 32 + 32)
                    last = v.tensor_tensor(
                        sproj[:, h * NTOK + n * 128 + qq * 32:
                              h * NTOK + n * 128 + qq * 32 + 32],
                        t4b[:, s32], gslot[:, n * 128 + qq * 32:
                                           n * 128 + qq * 32 + 32], ALU.mult)
                return last
            sptok = rec.step("vector", v_sp)
            bkg[2] = sptok
            bkg[3] = sptok
            sp_done.append(sptok)
        gs_done[h % 2] = sp_done[-1]
        ve_h[h] = rec.sems["c_ve"][1]
    tap_dump("sproj", sproj[:, :])

    rec.wait("sync", cc["wp"])
    nfin = 0
    fin_toks = {}
    for co in range(4):
        rec.wait("sync", rec.cur_tok("tensor"))
        wp_tok = rec.dma("sync", wsl4[:, :].rearrange("p (c n) -> p c n", c=KC),
                         g_wp[:, co * 512:(co + 1) * 512]
                         .rearrange("(c p) n -> p c n", p=128), "d_wp")
        for tt in range(8):
            i4 = co * 8 + tt
            xs = xrg[i4 % 2]
            osl = org[i4 % 2]
            if i4 >= 2:
                rec.wait("sync", fin_toks[i4 - 2])
            xtok = rec.dma("sync", xs[:, :],
                           x_in[:, :].rearrange("(c p) n -> p c n", p=128)
                           [:, tt, co * 512:(co + 1) * 512], "d_xr")
            bank = acq(i4 % 4)
            rec.wait("tensor", wp_tok)
            if i4 == 0:
                rec.wait("tensor", sp_done[-1])

            def t_pr(t, tt=tt, bank=bank):
                i = None
                for kc in range(KC):
                    i = t.matmul(bank[:, :],
                                 sproj[:, kc * NTOK + tt * 128:kc * NTOK + tt * 128 + 128],
                                 wsl4[:, kc * 512:(kc + 1) * 512],
                                 start=(kc == 0), stop=(kc == KC - 1))
                return i
            mtok = rec.step("tensor", t_pr)
            rec.wait("vector", mtok)
            rec.wait("vector", xtok)

            def v_fin(v, bank=bank, tt=tt, co=co, xs=xs, osl=osl, i4=i4):
                base = (i4 % 2) * 512
                for qq in range(4):
                    s = slice(qq * 128, qq * 128 + 128)
                    v.tensor_tensor(tfin[:, base + qq * 128:base + qq * 128 + 128],
                                    bank[:, s],
                                    g_tm[:, tt * C + co * 512 + qq * 128:
                                         tt * C + co * 512 + qq * 128 + 128],
                                    ALU.mult)
                last = None
                for qq in range(4):
                    s = slice(qq * 128, qq * 128 + 128)
                    last = v.tensor_add(osl[:, s],
                                        tfin[:, base + qq * 128:base + qq * 128 + 128],
                                        xs[:, s])
                return last
            vtok = rec.step("vector", v_fin)
            bkg[i4 % 4] = vtok
            rec.wait("sync", vtok)
            fin_toks[i4] = rec.dma(
                "sync", out_ap[:, :].rearrange("(c p) n -> p c n", p=128)
                [:, tt, co * 512:(co + 1) * 512], osl[:, :], "d_out")
            nfin += 1
    rec.wait("sync", ("d_out", 16 * nfin))
    if tap_aps:
        rec.wait("sync", ("d_tap", rec.sems["d_tap"][1]))

    with nc.Block() as block:
        rec.emit_all(block)
    assert nc.sbuf_base <= BASE, (
        f"const pool overran guard: sbuf_base={nc.sbuf_base} > BASE={BASE}")
    return nc


def _perm_idx():
    return np.concatenate([np.arange(0, 128, 2), np.arange(1, 128, 2)])


def _build_tables():
    d2 = np.arange(0, C, 2, dtype=np.float64)
    dim_scale = (d2 + 0.4 * C) / (1.4 * C)
    pos = np.arange(T, dtype=np.float64)
    scale = dim_scale[None, :] ** (pos[:, None] / 512.0)
    inv_freq = 1.0 / 10000.0 ** (d2 / C)
    sinusoid = pos[:, None] * inv_freq[None, :]
    sin_full = np.repeat(np.sin(sinusoid) * scale, 2, axis=-1)
    cos_full = np.repeat(np.cos(sinusoid) * scale, 2, axis=-1)
    pidx = _perm_idx()
    gperm = np.concatenate([hh * 128 + pidx for hh in range(H)])
    cosp = cos_full[:, gperm]
    sinp = sin_full[:, gperm]
    i_loc = (pos % L)
    qfac = GAMMA[None, :] ** (i_loc[:, None] + 1.0) * D ** -0.5
    kfac = GAMMA[None, :] ** (-(i_loc[:, None] + 1.0))
    qf = np.repeat(qfac, 128, axis=1)
    kf = np.repeat(kfac, 128, axis=1)
    return (np.ascontiguousarray((cosp * qf).T).astype(BF),
            np.ascontiguousarray((sinp * qf).T).astype(BF),
            np.ascontiguousarray((cosp * kf).T).astype(BF),
            np.ascontiguousarray((sinp * kf).T).astype(BF))


def _host_prep(inputs):
    x = np.asarray(inputs["L0_attn_x"], np.float32)
    wqkv = np.asarray(inputs["L0_attn_w_qkv"], np.float32)
    wg = np.asarray(inputs["L0_attn_w_g"], np.float32)
    wproj = np.asarray(inputs["L0_attn_w_proj"], np.float32)
    gnw = np.asarray(inputs["L0_attn_gn_weight"], np.float32)
    gnb = np.asarray(inputs["L0_attn_gn_bias"], np.float32)
    w1 = np.asarray(inputs["L0_ffn_w1"], np.float32)
    b1 = np.asarray(inputs["L0_ffn_b1"], np.float32)
    lng = np.asarray(inputs["L0_ffn_ln_g"], np.float32)
    lnb = np.asarray(inputs["L0_ffn_ln_b"], np.float32)
    w2 = np.asarray(inputs["L0_ffn_w2"], np.float32)
    b2 = np.asarray(inputs["L0_ffn_b2"], np.float32)

    pidx = _perm_idx()
    gperm = np.concatenate([hh * 128 + pidx for hh in range(H)])
    wqkv_p = wqkv.copy()
    wqkv_p[:, :C] = wqkv[:, :C][:, gperm]
    wqkv_p[:, C:2 * C] = wqkv[:, C:2 * C][:, gperm]

    cosq, sinq, cosk, sink = _build_tables()
    causal = np.triu(np.ones((L, L), np.float32)).astype(BF)
    identf = np.eye(128, dtype=np.float32)
    identb = np.eye(128, dtype=np.float32).astype(BF)
    b1_fm = np.ascontiguousarray(b1.reshape(IT, 128).T)
    lng_fm = np.ascontiguousarray(lng.reshape(IT, 128).T)
    lnb_fm = np.ascontiguousarray(lnb.reshape(IT, 128).T)

    in_maps = []
    for core in range(NCORES):
        b_, p_ = core // 4, core % 4
        rows = slice(p_ * NTOK, (p_ + 1) * NTOK)
        trows = slice(b_ * 0 + p_ * NTOK, p_ * NTOK + NTOK)
        cvec = np.zeros((NCORES, H), np.float64)
        for pp in range(p_):
            cvec[b_ * 4 + pp] = GAMMA ** (1024.0 * (p_ - pp - 1))
        cvec_b = np.broadcast_to(cvec.reshape(1, NCORES * H).astype(np.float32),
                                 (128, NCORES * H)).copy()
        sl8 = core
        m = {
            "x_loc": np.ascontiguousarray(x[b_, rows]),
            "cosq": np.ascontiguousarray(cosq[:, trows]),
            "sinq": np.ascontiguousarray(sinq[:, trows]),
            "cosk": np.ascontiguousarray(cosk[:, trows]),
            "sink": np.ascontiguousarray(sink[:, trows]),
            "causal": causal,
            "ident_f": identf,
            "ident_b": identb,
            "gnw_row": gnw.reshape(1, C).astype(BF),
            "gnb_row": gnb.reshape(1, C).astype(BF),
            "b1_fm": b1_fm,
            "b2_row": b2.reshape(1, C).astype(BF),
            "lng_fm": lng_fm,
            "lnb_fm": lnb_fm,
            "cvec": cvec_b,
            "wqkv_s": np.ascontiguousarray(wqkv_p[sl8 * 256:(sl8 + 1) * 256]),
            "wg_s": np.ascontiguousarray(wg[sl8 * 256:(sl8 + 1) * 256]),
            "wproj_s": np.ascontiguousarray(wproj[sl8 * 256:(sl8 + 1) * 256]),
            "w1t_s": np.ascontiguousarray(w1[:, sl8 * 256:(sl8 + 1) * 256].T),
            "w2t_s": np.ascontiguousarray(w2[:, sl8 * 1024:(sl8 + 1) * 1024].T),
        }
        in_maps.append(m)
    return in_maps


def kernel(**inputs):
    if "nc" not in _CACHE:
        _CACHE["nc"] = build_program()
    nc = _CACHE["nc"]
    in_maps = _host_prep(inputs)
    res = run_bass_kernel_spmd(nc, in_maps, core_ids=list(range(NCORES)))
    out = np.zeros((B, T, C), np.float32)
    for core in range(NCORES):
        b_, p_ = core // 4, core % 4
        out[b_, p_ * NTOK:(p_ + 1) * NTOK] = res.results[core]["out"]
    return out


# version 11
# speedup vs baseline: 1.0136x; 1.0136x over previous
import numpy as np
import ml_dtypes
import concourse.bass as bass
import concourse.mybir as mybir
from concourse.bass_isa import ReduceOp
from concourse.bass_utils import run_bass_kernel_spmd

F32 = mybir.dt.float32
BF16 = mybir.dt.bfloat16
I32 = mybir.dt.int32
AF = mybir.ActivationFunctionType
ALU = mybir.AluOpType
AX = mybir.AxisListType
BF = ml_dtypes.bfloat16
BF16_ = mybir.dt.bfloat16

B, T, C = 2, 4096, 2048
H, D = 16, 128
L = 128
NTOK = 1024
NCH = 8
INNER = 8192
EPS = 1e-5
NCORES = 8
KC = C // 128
IT = INNER // 128

GAMMA = (1.0 - 2.0 ** (-5.0 - np.arange(H, dtype=np.float64)))
GL = [float(np.float32(g ** L)) for g in GAMMA]

_CACHE = {}


class Rec:
    ENGS = ("sync", "gpsimd", "vector", "scalar", "tensor")
    ESEM = {"sync": "c_sy", "gpsimd": "c_gp", "vector": "c_ve",
            "scalar": "c_ac", "tensor": "c_te"}

    def __init__(self, nc):
        self.nc = nc
        self.ops = {e: [] for e in self.ENGS}
        self.sems = {}
        self.emitted_inc = {}
        for s in self.ESEM.values():
            self.sem(s)

    def sem(self, name):
        if name not in self.sems:
            h = self.nc.semaphore(name).__enter__()
            self.sems[name] = [h, 0]
        return name

    def inc(self, name, by=1):
        ent = self.sems[name]
        ent[1] += by
        h = ent[0]
        return lambda inst: inst.then_inc(h, by)

    def cur_tok(self, eng):
        s = self.ESEM[eng]
        return (s, self.sems[s][1])

    def step(self, eng, fn):
        sname = self.ESEM[eng]
        ap = self.inc(sname)
        cnt = self.sems[sname][1]
        h = self.sems[sname][0]

        def wrapped(e, fn=fn, ap=ap, h=h, cnt=cnt):
            inst = fn(e)
            ap(inst)
            self.emitted_inc[id(inst)] = (h, cnt)
        self.ops[eng].append(wrapped)
        return (sname, cnt)

    def wait(self, eng, tok):
        if tok is None:
            return
        name, v = tok
        if v <= 0:
            return
        ent = self.sems[name]
        self.ops[eng].append(lambda e, h=ent[0], v=v: e.wait_ge(h, v))

    def wait_all(self, eng, prod_eng):
        self.wait(eng, self.cur_tok(prod_eng))

    def dma(self, eng, dst, src, sem):
        self.sem(sem)
        ap = self.inc(sem, 16)
        tok = (sem, self.sems[sem][1])
        self.ops[eng].append(lambda e, d=dst, s=src, ap=ap: ap(e.dma_start(out=d, in_=s)))
        return tok

    def emit_all(self, block):
        for eng in self.ENGS:
            def body(e, eng=eng):
                h = HazardGuard(e, self, eng) if eng in ("vector", "scalar") else e
                for fn in self.ops[eng]:
                    fn(h)
            getattr(block, eng)(body)


def _ap_range(a):
    t = getattr(a, "tensor", None)
    if t is None:
        return None, None
    rng = getattr(t, "manual_sbuf_range", None)
    name = getattr(t, "name", None)
    if rng is None:
        return name, None
    esz = mybir.dt.size(t.dtype)
    try:
        aps = list(a.ap)
        ext = 1
        for st, ct in aps[1:]:
            ext += abs(int(st)) * (int(ct) - 1)
        off = int(a.offset)
        row = int(aps[0][0]) if aps else ext
        if row > 0:
            off = off % row
        lo = rng[0] + off * esz
        hi = lo + ext * esz
    except Exception:
        lo, hi = rng[0], rng[1]
    return name, (lo, hi)


_AFC = mybir.ActivationFunctionType.Copy

_GUARDED_OPS = {
    "tensor_copy", "tensor_tensor", "tensor_add", "tensor_sub", "tensor_mul",
    "tensor_max", "tensor_scalar", "tensor_single_scalar", "tensor_scalar_add",
    "tensor_scalar_sub", "tensor_scalar_mul", "tensor_scalar_max",
    "tensor_scalar_min", "tensor_reduce", "reduce_max", "reduce_sum",
    "scalar_tensor_tensor", "reciprocal", "memset", "memzero", "activation",
    "copy", "mul", "add", "square", "sqrt", "sign",
}


class HazardGuard:

    def __init__(self, e, rec, eng):
        self._e = e
        self._rec = rec
        self._eng = eng
        self._zsem = rec.sem("z_" + eng)
        self._last = None

    def wait_ge(self, *a, **k):
        return self._e.wait_ge(*a, **k)

    def nop(self, *a, **k):
        return self._e.nop(*a, **k)

    def dma_start(self, *a, **k):
        return self._e.dma_start(*a, **k)

    def __getattr__(self, name):
        fn = getattr(self._e, name)
        if name not in _GUARDED_OPS or not callable(fn):
            return fn

        def call(*args, **kw):
            outs, ins = [], []
            aps = []
            for i, a in enumerate(args):
                if hasattr(a, "tensor"):
                    (outs if i == 0 else ins).append(_ap_range(a))
            for kname, a in kw.items():
                if hasattr(a, "tensor"):
                    (outs if kname == "accum_out" else ins).append(_ap_range(a))
            if self._last is not None:
                li, lr, lw, ltok = self._last
                if self._conf(lw, ins) or self._conf(lw, outs) or self._conf(lr, outs):
                    scr = getattr(self._rec, "guard_scratch", {}).get(self._eng)
                    if scr is not None:
                        sa, sb = scr
                        if self._eng == "vector":
                            self._e.tensor_copy(sa[:, :], sb[:, :])
                            self._e.tensor_copy(sa[:, :], sb[:, :])
                            self._e.tensor_copy(sa[:, :], sb[:, :])
                        else:
                            self._e.activation(sa[:, :], sb[:, :], _AFC)
                            self._e.activation(sa[:, :], sb[:, :], _AFC)
                            self._e.activation(sa[:, :], sb[:, :], _AFC)
                    else:
                        prev = self._rec.emitted_inc.get(id(li))
                        if prev is not None:
                            self._e.wait_ge(prev[0], prev[1])
                        else:
                            ent = self._rec.sems[self._zsem]
                            if ltok is None:
                                ent[1] += 1
                                li.then_inc(ent[0], 1)
                                ltok = ent[1]
                                self._rec.emitted_inc[id(li)] = (ent[0], ent[1])
                                self._last = (li, lr, lw, ltok)
                            self._e.wait_ge(ent[0], ltok)
            inst = fn(*args, **kw)
            self._last = (inst, ins, outs, None)
            return inst
        return call

    @staticmethod
    def _conf(aset, bset):
        for an, ar in aset:
            for bn, br in bset:
                if ar is None or br is None:
                    if an is not None and an == bn:
                        return True
                elif ar[0] < br[1] and br[0] < ar[1]:
                    return True
        return False


def build_program(taps=()):
    nc = bass.Bass("TRN2", target_bir_lowering=False, debug=False,
                   detect_race_conditions=False)
    taps = set(taps)

    def din(name, shape, dt=F32):
        return nc.dram_tensor(name, shape, dt, kind="ExternalInput").ap()

    x_in = din("x_loc", [NTOK, C])
    tab_in = {("q", 0): din("cosq", [C, NTOK], BF16),
              ("q", 1): din("sinq", [C, NTOK], BF16),
              ("k", 0): din("cosk", [C, NTOK], BF16),
              ("k", 1): din("sink", [C, NTOK], BF16)}
    causal_in = din("causal", [L, L], BF16)
    identf_in = din("ident_f", [128, 128], F32)
    identb_in = din("ident_b", [128, 128], BF16)
    gnw_in = din("gnw_row", [1, C], BF16)
    gnb_in = din("gnb_row", [1, C], BF16)
    b1fm_in = din("b1_fm", [128, IT])
    b2_in = din("b2_row", [1, C], BF16)
    lng_in = din("lng_fm", [128, IT])
    lnb_in = din("lnb_fm", [128, IT])
    cvec_in = din("cvec", [128, NCORES * H])
    wqkv_in = din("wqkv_s", [C // 8, 3 * C])
    wg_in = din("wg_s", [C // 8, C])
    wproj_in = din("wproj_s", [C // 8, C])
    w1t_in = din("w1t_s", [C // 8, INNER])
    w2t_in = din("w2t_s", [INNER // 8, C])

    out_ap = nc.dram_tensor("out", [NTOK, C], F32, kind="ExternalOutput").ap()
    tap_aps = {}

    def mktap(tname, shape, dt):
        if tname in taps:
            tap_aps[tname] = nc.dram_tensor("tap_" + tname, shape, dt,
                                            kind="ExternalOutput").ap()
    mktap("xn_tm", [128, 8 * C], F32)
    mktap("xraw", [128, 8 * C], F32)
    mktap("sstat1", [128, 64], F32)
    mktap("statsA", [128, 64], F32)
    mktap("q2", [128, H * NTOK], BF16)
    mktap("k2", [128, H * NTOK], BF16)
    mktap("v", [128, 8 * C], BF16)
    mktap("y", [128, H * NTOK], BF16)
    mktap("S", [128, H * D], F32)
    mktap("h", [128, 2 * IT * 512], BF16)
    mktap("n2", [128, 2 * IT * 512], BF16)
    mktap("g", [128, 8 * C], BF16)
    mktap("scl4", [128, 4], F32)
    mktap("ybuf0", [128, NTOK], BF16)
    mktap("gn0", [128, 16], F32)
    mktap("sinc2", [128, H * D], BF16)
    mktap("sproj", [128, H * NTOK], BF16)
    mktap("stats", [128, 64], F32)
    mktap("statB", [128, 64], F32)
    mktap("sinc", [128, H * D], BF16)

    def dint(name, shape, dt):
        return nc.dram_tensor(name, shape, dt).ap()

    g_wqkv_i = dint("g_wqkv_i", [C // 8, 3 * C], BF16)
    g_wqkv = dint("g_wqkv", [C, 3 * C], BF16)
    g_wg_i = dint("g_wg_i", [C // 8, C], BF16)
    g_wg = dint("g_wg", [C, C], BF16)
    g_wp_i = dint("g_wp_i", [C // 8, C], BF16)
    g_wp = dint("g_wp", [C, C], BF16)
    g_q1_i = dint("g_q1_i", [C // 8, INNER], BF16)
    g_q1 = dint("g_q1", [C, INNER], BF16)
    g_q2_i = dint("g_q2_i", [INNER // 8, C], BF16)
    g_q2 = dint("g_q2", [INNER, C], BF16)
    g_ws_i = dint("g_ws_i", [1, 16], F32)
    g_ws = dint("g_ws", [1, 16], F32)
    g_sl_i = dint("g_sl_i", [128, H * D], F32)
    g_sl = dint("g_sl", [NCORES * 128, H * D], F32)
    g_ysp = dint("g_ysp", [128, H * NTOK], BF16)
    g_qsp = dint("g_qsp", [128, H * NTOK], BF16)
    g_vsp = dint("g_vsp", [128, 8 * C], BF16)
    g_xnsp = dint("g_xnsp", [128, KC * NTOK], BF16)
    g_n1sp = dint("g_n1sp", [128, KC * NTOK], BF16)

    BASE = nc.sbuf_base + 4096
    TOP = nc.SBUF_PARTITION_SIZE_BYTES
    _nt = [0]

    def at(off_kb, shape, dt, name="m"):
        _nt[0] += 1
        esz = mybir.dt.size(dt)
        off = BASE + int(round(off_kb * 1024))
        off = (off + 31) // 32 * 32
        free = int(np.prod(shape[1:])) * esz
        assert off + free <= TOP, f"{name}: {off}+{free} > {TOP}"
        return nc.alloc_sbuf_tensor_at(f"{name}{_nt[0]}", shape, dt, offset=off)

    stats = at(0, [128, 64], F32, "stats")
    sstat = at(0.25, [128, 64], F32, "sstat")
    stat2 = at(0.5, [128, 64], F32, "stat2")
    statB = at(0.75, [128, 64], F32, "statB")
    identf = at(1.0, [128, 128], F32, "idf")
    identb = at(1.55, [128, 128], BF16, "idb")
    causal = at(1.85, [L, L], BF16, "caus")
    cvec = at(2.15, [128, NCORES * H], F32, "cvec")
    lng_fm = at(2.7, [128, IT], F32, "lng")
    lnb_fm = at(3.0, [128, IT], F32, "lnb")
    b1_fm = at(3.3, [128, IT], F32, "b1f")
    onesb_c = at(3.6, [128, 1], BF16, "o1")
    onesf_c = at(3.65, [128, 1], F32, "o1f")
    onesb_r = at(3.7, [1, 128], BF16, "o2")
    onesf_r = at(4.0, [1, 128], F32, "o3")
    sw_b = at(4.6, [128, 2], F32, "swb")
    wsum = at(4.7, [1, 16], F32, "wsum")
    rowf = at(4.8, [1, 128], F32, "rowf")
    rowb = at(5.4, [1, 128], BF16, "rowb")
    gnw_r = at(5.7, [1, C], BF16, "gnwr")
    gnb_r = at(9.8, [1, C], BF16, "gnbr")
    b2_r = at(13.9, [1, C], BF16, "b2r")
    S_t = at(18.0, [128, H * D], F32, "S")
    Sbf = at(26.1, [128, H * D], BF16, "Sbf")
    c1rows = at(30.2, [1, 8 * 128], F32, "c1r")
    svrows = at(34.3, [1, 8 * 128], BF16, "svr")
    sv2rows = at(36.4, [1, 8 * 128], BF16, "sv2")
    rmax = at(38.5, [128, 512], F32, "rmax")
    rmin = at(40.5, [128, 512], F32, "rmin")
    ybuf = at(42.5, [128, NTOK], BF16, "ybuf")
    scl4 = at(47.0, [128, 4], F32, "scl4")
    pre4 = at(47.1, [128, 128], F32, "pre4")
    scrv_a = at(47.6, [128, 32], F32, "scrva")
    scrv_b = at(47.73, [128, 32], F32, "scrvb")
    scrs_a = at(46.8, [128, 16], F32, "scrsa")
    scrs_b = at(46.9, [128, 16], F32, "scrsb")
    gnmu = at(44.5, [128, 16], F32, "gnmu")
    gnrs = at(44.6, [128, 16], F32, "gnrs")
    gnmp = at(44.7, [128, 16], F32, "gnmp")
    t4buf = at(44.8, [128, 512], F32, "t4buf")

    R0 = 48.0

    wbuf = [at(R0 + 32 * i, [128, INNER], F32, f"wb{i}") for i in range(2)]
    qbuf = at(R0 + 64, [128, INNER], BF16, "qb")
    qscr = at(R0 + 80, [128, C], BF16, "qs")

    x_tm = at(R0, [128, 8 * C], F32, "x")
    xn_tm = at(R0 + 64, [128, 8 * C], F32, "xn")
    sqd = at(R0 + 128.5, [128, C], BF16, "sqd")
    n1_tm = at(R0, [128, 8 * C], BF16, "n1tm")
    n1i_f = at(R0 + 32, [128, C], F32, "n1ir")
    xn_fm = at(R0 + 32, [128, KC * NTOK], BF16, "xnfm")
    n1_fm = at(R0 + 64, [128, KC * NTOK], BF16, "n1fm")

    q2_a = at(R0, [128, H * NTOK], BF16, "q2")
    k2_a = at(R0 + 64, [128, H * NTOK], BF16, "k2")
    wsl = [at(R0 + 96 + 16 * i, [128, KC * 512], BF16, f"wsl{i}") for i in range(2)]
    tabr = at(R0 + 128, [128, 2 * 1024], BF16, "tab")
    v_r = [at(R0 + 132 + 4 * i, [128, C], BF16, f"vr{i}") for i in range(2)]
    y_r = [at(R0 + 140 + 4 * i, [128, H * L], BF16, f"yr{i}") for i in range(2)]
    smr = [at(R0 + 148 + 0.25 * i, [128, L], BF16, f"smr{i}") for i in range(2)]
    kdr = [at(R0 + 148.5 + 0.25 * i, [128, L], BF16, f"kdr{i}") for i in range(2)]
    fr1 = at(R0 + 152, [128, 512], F32, "fr1")
    fr1b = at(R0 + 132, [128, 512], F32, "fr1b")
    fr2b = at(R0 + 134, [128, 512], F32, "fr2b")
    fr2 = at(45.0, [128, 512], F32, "fr2")

    harena = at(R0, [128, IT * 512], BF16, "h")
    n1r = at(R0 + 64, [128, KC * NTOK], BF16, "n1r")
    q1b = [at(R0 + 96, [128, KC * 128], BF16, "q1b")]
    w2b = [at(R0 + 100 + 4 * i, [128, 4 * 512], BF16, f"w2b{i}") for i in range(2)]
    fri = at(R0 + 108, [128, 512], I32, "fri")
    gbr = [at(R0 + 110 + i, [128, 512], BF16, f"gbr{i}") for i in range(2)]
    sqrr = [at(R0 + 144 + i, [128, 512], BF16, f"sqr{i}") for i in range(2)]
    C1R = at(R0 + 146, [128, 512], F32, "C1R")
    R1 = at(R0 + 148, [128, 512], F32, "R1")
    M1R = at(R0 + 150, [128, 512], F32, "M1R")
    C2R = at(43.0, [128, 512], F32, "C2R")
    g_tm = at(R0 + 112, [128, 8 * C], BF16, "g")

    sproj = at(R0, [128, H * NTOK], BF16, "sp")
    xn_fm4 = at(R0 + 32, [128, KC * NTOK], BF16, "xnfm4")
    wsl4 = at(R0 + 64, [128, KC * 512], BF16, "wsl4")
    sincA = at(R0 + 80, [128, H * D], F32, "siA")
    tfin = at(R0 + 80, [128, 1024], F32, "tfin")
    sincB = at(R0 + 88, [128, H * D], F32, "siB")
    sring = [at(R0 + 96 + 4 * i, [128, H * D], BF16, f"sg{i}") for i in range(2)]
    sincbf = at(R0 + 104, [128, H * D], BF16, "sibf")
    y4r = [at(R0 + 80 + 2 * i, [128, NTOK], BF16, f"y4{i}") for i in range(2)]
    q4r = [at(R0 + 84 + 2 * i, [128, NTOK], BF16, f"q4{i}") for i in range(2)]
    gsil = [at(R0 + 88 + 2 * i, [128, NTOK], BF16, f"gs{i}") for i in range(2)]
    qdr4 = [at(R0 + 92 + 0.25 * i, [128, L], BF16, f"qd4{i}") for i in range(2)]
    ytot = at(R0 + 92.5, [128, L], BF16, "yt")
    ysq = at(R0 + 92.75, [128, L], BF16, "ysq")
    t4a = at(R0 + 93, [128, L], F32, "t4a")
    t4b = at(R0 + 93.5, [128, L], F32, "t4b")
    xrg = [at(R0 + 94 + 2 * i, [128, 512], F32, f"xr{i}") for i in range(2)]
    org = [at(R0 + 98 + 2 * i, [128, 512], F32, f"og{i}") for i in range(2)]

    pb = [nc.alloc_psum_tensor(f"pb{i}", [128, 512], F32) for i in range(8)]
    bkg = {i: None for i in range(8)}

    rec = Rec(nc)
    rec.guard_scratch = {"vector": (scrv_a, scrv_b), "scalar": (scrs_a, scrs_b)}
    RG = [list(range(NCORES))]

    def acq(i):
        rec.wait("tensor", bkg[i])
        return pb[i]

    def tap_dump(name, src_ap):
        if name in tap_aps:
            rec.wait_all("sync", "vector")
            rec.wait_all("sync", "scalar")
            rec.wait_all("sync", "tensor")
            rec.wait_all("sync", "gpsimd")
            t = rec.dma("sync", tap_aps[name][:, :], src_ap, "d_tap")
            for e in ("vector", "scalar", "tensor", "gpsimd"):
                rec.wait(e, t)

    def col_to_row(colap, rowslice, bf16=False):
        bank = acq(7)
        tok = rec.step("tensor", lambda t, colap=colap, bank=bank:
                       t.transpose(bank[0:1, 0:128], colap, identf[:, :]))
        rec.wait("vector", tok)
        ctok = rec.step("vector", lambda v, r=rowslice, bank=bank:
                        v.tensor_copy(r, bank[0:1, 0:128]))
        bkg[7] = ctok
        return ctok

    def rank1_to(dst_sb, row_ap, jj, tokdep=None):
        bank = acq(6)
        if tokdep is not None:
            rec.wait("tensor", tokdep)
        tok = rec.step("tensor", lambda t, bank=bank, row_ap=row_ap, jj=jj:
                       t.matmul(bank[:, jj * 128:(jj + 1) * 128], onesf_r[:, :],
                                row_ap, start=True, stop=True))
        rec.wait("vector", tok)
        ctok = rec.step("vector", lambda v, bank=bank, jj=jj, dst_sb=dst_sb:
                        v.tensor_copy(dst_sb[:, jj * 128:(jj + 1) * 128],
                                      bank[:, jj * 128:(jj + 1) * 128]))
        bkg[6] = ctok
        return ctok

    t_conv = [rec.dma("gpsimd", d[:, :], s[:, :], "d_conv")
              for d, s in [(g_wqkv_i, wqkv_in), (g_wg_i, wg_in), (g_wp_i, wproj_in)]]

    t_pers = [rec.dma("sync", d, s, "d_pers") for d, s in [
        (identf[:, :], identf_in[:, :]), (identb[:, :], identb_in[:, :]),
        (causal[:, :], causal_in[:, :]), (cvec[:, :], cvec_in[:, :]),
        (lng_fm[:, :], lng_in[:, :]), (lnb_fm[:, :], lnb_in[:, :]),
        (b1_fm[:, :], b1fm_in[:, :]), (gnw_r[:, :], gnw_in[:, :]),
        (gnb_r[:, :], gnb_in[:, :]), (b2_r[:, :], b2_in[:, :])]]
    TPERS = t_pers[-1]

    def v_init(v):
        v.memset(stats[:, :], 0.0)
        v.memset(sstat[:, :], 0.0)
        v.memset(stat2[:, :], 0.0)
        v.memset(statB[:, :], 0.0)
        v.memset(wsum[:, :], 0.0)
        v.memset(onesb_c[:, :], 1.0)
        v.memset(onesf_c[:, :], 1.0)
        v.memset(onesb_r[:, :], 1.0)
        v.memset(onesf_r[:, :], 1.0)
        v.memset(S_t[:, :], 0.0)
        return v.memset(Sbf[:, :], 0.0)
    T_VINIT = rec.step("vector", v_init)

    W_TILES = [(w1t_in, 2, INNER), (w2t_in, 8, C)]
    absq = 0
    abs_toks = []
    for src, nt, width in W_TILES:
        for i in range(nt):
            buf = wbuf[absq % 2]
            if absq >= 2:
                rec.wait("sync", abs_toks[absq - 2])
            tok = rec.dma("sync", buf[:, 0:width],
                          src[:, :].rearrange("(c p) n -> p c n", p=128)[:, i, :],
                          f"d_w{absq % 2}")
            rec.wait("vector", tok)
            col = 16 + absq
            abs_toks.append(rec.step(
                "vector", lambda v, buf=buf, width=width, col=col:
                v.tensor_reduce(sstat[:, col:col + 1], buf[:, 0:width],
                                AX.X, ALU.add, apply_absolute_value=True)))
            absq += 1

    def v_comb(v):
        v.tensor_add(sstat[:, 0:1], sstat[:, 16:17], sstat[:, 17:18])
        return v.tensor_reduce(sstat[:, 1:2], sstat[:, 18:26], AX.X, ALU.add)
    tok = rec.step("vector", v_comb)
    rec.wait("tensor", tok)
    rec.wait("tensor", TPERS)
    bank0 = pb[0]
    ttok = rec.step("tensor", lambda t: t.matmul(
        bank0[0:1, 0:2], onesf_c[:, :], sstat[:, 0:2], start=True, stop=True))
    rec.wait("vector", ttok)
    tok = rec.step("vector", lambda v: v.tensor_copy(wsum[0:1, 0:2], bank0[0:1, 0:2]))
    bkg[0] = tok
    rec.wait("scalar", tok)
    tok = rec.dma("scalar", g_ws_i[:, :], wsum[0:1, 0:16], "d_ws")
    rec.wait("gpsimd", tok)
    for t in t_conv:
        rec.wait("gpsimd", t)
    T_AR = rec.step("gpsimd", lambda g: g.collective_compute(
        "AllReduce", ALU.add, replica_groups=RG,
        ins=[g_ws_i.tensor.ap().opt()], outs=[g_ws.tensor.ap().opt()]))
    cc = {}
    for nm, gi, go in [("wqkv", g_wqkv_i, g_wqkv), ("wg", g_wg_i, g_wg),
                       ("wp", g_wp_i, g_wp)]:
        cc[nm] = rec.step("gpsimd", lambda g, gi=gi, go=go: g.collective_compute(
            "AllGather", ALU.bypass, replica_groups=RG,
            ins=[gi.tensor.ap().opt()], outs=[go.tensor.ap().opt()]))

    rec.wait("scalar", T_AR)
    tok = rec.dma("scalar", wsum[0:1, 0:16], g_ws[:, :], "d_ws2")
    rec.wait("vector", tok)

    def v_sw(v):
        v.tensor_scalar(wsum[0:1, 4:6], wsum[0:1, 0:2], 1.0 / (INNER * C), None, ALU.mult)
        v.tensor_scalar(wsum[0:1, 6:8], wsum[0:1, 4:6], 1e-5, None, ALU.max)
        return v.reciprocal(wsum[0:1, 8:10], wsum[0:1, 6:8])
    tok = rec.step("vector", v_sw)
    rec.wait("tensor", tok)
    bank1 = pb[1]
    ttok = rec.step("tensor", lambda t: t.matmul(
        bank1[:, 0:2], onesf_r[:, :], wsum[0:1, 8:10], start=True, stop=True))
    rec.wait("vector", ttok)
    T_SWB = rec.step("vector", lambda v: v.tensor_copy(sw_b[:, 0:2], bank1[:, 0:2]))
    bkg[1] = T_SWB

    stq = 0
    q_store = []
    vq_toks = []
    for wi, (src, nt, width) in enumerate(W_TILES):
        dsti = (g_q1_i, g_q2_i)[wi]
        for i in range(nt):
            buf = wbuf[stq % 2]
            if stq == 0:
                rec.wait("sync", abs_toks[-1])
            if stq >= 2:
                rec.wait("sync", vq_toks[stq - 2])
            tok = rec.dma("sync", buf[:, 0:width],
                          src[:, :].rearrange("(c p) n -> p c n", p=128)[:, i, :],
                          f"d_p2w{stq % 2}")
            rec.wait("vector", tok)
            if q_store:
                rec.wait("vector", q_store[-1])

            def v_q(v, buf=buf, width=width, wi=wi):
                last = None
                for sbi in range(width // C):
                    sl = slice(sbi * C, (sbi + 1) * C)
                    v.tensor_scalar(qbuf[:, sl], buf[:, sl], sw_b[:, wi:wi + 1],
                                    0.5, ALU.mult, ALU.is_gt)
                    v.tensor_scalar(qscr[:, 0:C], buf[:, sl], sw_b[:, wi:wi + 1],
                                    -0.5, ALU.mult, ALU.is_lt)
                    last = v.tensor_sub(qbuf[:, sl], qbuf[:, sl], qscr[:, 0:C])
                return last
            tok = rec.step("vector", v_q)
            vq_toks.append(tok)
            rec.wait("scalar", tok)
            q_store.append(rec.dma(
                "scalar", dsti[:, :].rearrange("(c p) n -> p c n", p=128)[:, i, :],
                qbuf[:, 0:width], "d_qst"))
            stq += 1
    for t in q_store:
        rec.wait("gpsimd", t)
    for nm, gi, go in [("q1", g_q1_i, g_q1), ("q2", g_q2_i, g_q2)]:
        cc[nm] = rec.step("gpsimd", lambda g, gi=gi, go=go: g.collective_compute(
            "AllGather", ALU.bypass, replica_groups=RG,
            ins=[gi.tensor.ap().opt()], outs=[go.tensor.ap().opt()]))

    rec.wait("sync", q_store[-1])
    rec.wait("sync", q_store[-2])
    tok_x = rec.dma("sync", x_tm[:, :].rearrange("p (c n) -> p c n", c=8),
                    x_in[:, :].rearrange("(c p) n -> p c n", p=128), "d_x")
    rec.wait("scalar", tok_x)
    rec.wait("scalar", TPERS)
    rec.wait("scalar", T_VINIT)
    rec.wait("vector", tok_x)

    def sc_ss(s):
        last = None
        for j in range(8):
            last = s.activation(sqd[:, :], x_tm[:, j * C:(j + 1) * C], AF.Square,
                                accum_out=sstat[:, 8 + j:9 + j])
        return last
    T_SS = rec.step("scalar", sc_ss)
    if "xraw" in tap_aps:
        rec.wait("sync", tok_x)
        t = rec.dma("sync", tap_aps["xraw"][:, :], x_tm[:, :], "d_tap")
        for e in ("vector", "scalar", "tensor"):
            rec.wait(e, t)
    rec.wait("vector", T_SS)

    def v_ln(v):
        for j in range(8):
            v.tensor_reduce(sstat[:, j:j + 1], x_tm[:, j * C:(j + 1) * C],
                            AX.X, ALU.add)
        v.tensor_scalar(stats[:, 0:8], sstat[:, 0:8], 1.0 / C, None, ALU.mult)
        v.tensor_scalar(sstat[:, 16:24], sstat[:, 8:16], 1.0 / C, None, ALU.mult)
        v.tensor_tensor(sstat[:, 24:32], stats[:, 0:8], stats[:, 0:8], ALU.mult)
        v.tensor_sub(sstat[:, 32:40], sstat[:, 16:24], sstat[:, 24:32])
        v.tensor_scalar(sstat[:, 40:48], sstat[:, 32:40], EPS, None, ALU.add)
        return v.reciprocal(stats[:, 16:24], sstat[:, 40:48])
    tok = rec.step("vector", v_ln)
    if "sstat1" in tap_aps:
        rec.wait("sync", tok)
        rec.wait("sync", T_SS)
        t = rec.dma("sync", tap_aps["sstat1"][:, :], sstat[:, :], "d_tap")
        for e in ("vector", "scalar", "tensor"):
            rec.wait(e, t)
    rec.wait("scalar", tok)

    def sc_rstd(s):
        s.activation(sstat[:, 48:56], stats[:, 16:24], AF.Ln)
        return s.activation(stats[:, 8:16], sstat[:, 48:56], AF.Exp, scale=0.5)
    tok = rec.step("scalar", sc_rstd)
    rec.wait("vector", tok)

    def v_xn(v):
        for j in range(8):
            v.tensor_tensor(sstat[:, 48 + j:49 + j], stats[:, j:j + 1],
                            stats[:, 8 + j:9 + j], ALU.mult)
            v.tensor_scalar(sstat[:, 48 + j:49 + j], sstat[:, 48 + j:49 + j],
                            -1.0, None, ALU.mult)
            v.tensor_scalar(xn_tm[:, j * C:(j + 1) * C], x_tm[:, j * C:(j + 1) * C],
                            stats[:, 8 + j:9 + j], sstat[:, 48 + j:49 + j],
                            ALU.mult, ALU.add)
        for j in range(8):
            v.tensor_reduce(stats[:, 24 + j:25 + j], xn_tm[:, j * C:(j + 1) * C],
                            AX.X, ALU.max, apply_absolute_value=True)
        v.tensor_tensor(sstat[:, 0:8], sstat[:, 32:40], stats[:, 16:24], ALU.mult)
        v.tensor_scalar(sstat[:, 0:8], sstat[:, 0:8], EPS, None, ALU.add)
        return v.reciprocal(sstat[:, 8:16], sstat[:, 0:8])
    tok = rec.step("vector", v_xn)
    tap_dump("statsA", stats[:, :])
    tap_dump("xn_tm", xn_tm[:, :])
    rec.wait("scalar", tok)

    def sc_rs1(s):
        s.activation(sstat[:, 16:24], sstat[:, 8:16], AF.Ln)
        return s.activation(stats[:, 32:40], sstat[:, 16:24], AF.Exp, scale=0.5)
    tok = rec.step("scalar", sc_rs1)
    rec.wait("vector", tok)

    def v_n1(v):
        v.tensor_tensor(sstat[:, 0:8], stats[:, 32:40], stats[:, 24:32], ALU.mult)
        v.tensor_scalar(sstat[:, 0:8], sstat[:, 0:8], 1e-5, None, ALU.max)
        v.reciprocal(sstat[:, 8:16], sstat[:, 0:8])
        v.tensor_scalar(stats[:, 40:48], sstat[:, 8:16], 127.0, None, ALU.mult)
        v.tensor_tensor(stats[:, 48:56], stats[:, 32:40], stats[:, 40:48], ALU.mult)
        last = None
        MAGIC = 12582912.0
        for j in range(8):
            v.tensor_scalar(xn_tm[:, j * C:(j + 1) * C] if False else n1i_f[:, :],
                            xn_tm[:, j * C:(j + 1) * C],
                            stats[:, 48 + j:49 + j], MAGIC, ALU.mult, ALU.add)
            last = v.tensor_scalar(n1_tm[:, j * C:(j + 1) * C], n1i_f[:, :],
                                   -MAGIC, None, ALU.add)
        return last
    T_N1 = rec.step("vector", v_n1)

    rec.wait("tensor", TPERS)
    rec.wait("tensor", T_N1)
    for src_tm, dst_fm, ident, isbf in [(xn_tm, xn_fm, identf, False),
                                        (n1_tm, n1_fm, identb, True)]:
        for j in range(8):
            for kc in range(KC):
                bkid = 4 + (j * KC + kc) % 4
                bank = acq(bkid)
                bout = bank.ap().bitcast(BF16_) if isbf else bank

                def t_tr(t, s=src_tm, j=j, kc=kc, bout=bout, ident=ident):
                    return t.transpose(
                        bout[:, 0:128],
                        s[:, j * C + kc * 128:j * C + kc * 128 + 128],
                        ident[:, :])
                tok = rec.step("tensor", t_tr)
                rec.wait("vector", tok)
                ctok = rec.step("vector", lambda v, d=dst_fm, j=j, kc=kc, bout=bout:
                                v.tensor_copy(
                                    d[:, kc * NTOK + j * 128:kc * NTOK + j * 128 + 128],
                                    bout[:, 0:128]))
                bkg[bkid] = ctok

    rec.wait_all("sync", "vector")
    T_N1SP = rec.dma("sync", g_n1sp[:, :], n1_fm[:, :], "d_sp")

    def v_sv(v):
        v.tensor_scalar(statB[:, 0:8], stats[:, 40:48], sw_b[:, 0:1], None, ALU.mult)
        return v.reciprocal(statB[:, 8:16], statB[:, 0:8])
    T_SV = rec.step("vector", v_sv)
    rec.wait("tensor", T_SV)
    for j in range(8):
        col_to_row(statB[:, 8 + j:9 + j], c1rows[0:1, j * 128:(j + 1) * 128])
        col_to_row(statB[:, j:j + 1], svrows[0:1, j * 128:(j + 1) * 128])

    rec.wait("sync", cc["wqkv"])
    rec.wait_all("tensor", "vector")
    wsl_tok = [None, None]
    wsl_free = [None, None]

    def load_wslice(sl_i, which):
        s = sl_i % 2
        rec.wait("sync", wsl_free[s])
        src = {"qkv": g_wqkv}[which]
        wsl_tok[s] = rec.dma("sync", wsl[s][:, :].rearrange("p (c n) -> p c n", c=KC),
                             src[:, sl_i * 512:(sl_i + 1) * 512]
                             .rearrange("(c p) n -> p c n", p=128), f"d_wsl{s}")

    qk_tiles = [("q", hh, t2) for hh in range(H) for t2 in range(2)] + \
               [("k", hh, t2) for hh in range(H) for t2 in range(2)]
    tabslot_free = [None] * 2
    load_wslice(0, "qkv")
    cur_slice = [0]
    for pidx in range(0, len(qk_tiles), 2):
        pair_ctx = []
        for idx in (pidx, pidx + 1):
            qk, hh, t2 = qk_tiles[idx]
            sl_i = hh // 4 + (0 if qk == "q" else 4)
            if sl_i > cur_slice[0]:
                wsl_free[sl_i % 2] = rec.cur_tok("tensor")
                load_wslice(sl_i, "qkv")
                cur_slice[0] = sl_i
            slot = idx % 2
            rec.wait("sync", tabslot_free[slot])
            rec.sem("d_tab")
            rec.inc("d_tab", 32)
            ttok = ("d_tab", rec.sems["d_tab"][1])
            h_tab = rec.sems["d_tab"][0]

            def sy_tab(sy, slot=slot, qk=qk, hh=hh, t2=t2, h_tab=h_tab):
                sy.dma_start(out=tabr[:, slot * 1024:slot * 1024 + 512],
                             in_=tab_in[(qk, 0)][hh * 128:(hh + 1) * 128,
                                                 t2 * 512:(t2 + 1) * 512]).then_inc(h_tab, 16)
                sy.dma_start(out=tabr[:, slot * 1024 + 512:slot * 1024 + 1024],
                             in_=tab_in[(qk, 1)][hh * 128:(hh + 1) * 128,
                                                 t2 * 512:(t2 + 1) * 512]).then_inc(h_tab, 16)
            rec.ops["sync"].append(sy_tab)

            bkid = idx % 2
            bank = acq(bkid)
            rec.wait("tensor", wsl_tok[sl_i % 2])

            def t_mm(t, buf=wsl[sl_i % 2], co=hh % 4, t2=t2, bank=bank):
                i = None
                for kc in range(KC):
                    i = t.matmul(bank[:, :],
                                 buf[:, kc * 512 + co * 128:kc * 512 + co * 128 + 128],
                                 xn_fm[:, kc * NTOK + t2 * 512:kc * NTOK + (t2 + 1) * 512],
                                 start=(kc == 0), stop=(kc == KC - 1))
                return i
            mtok = rec.step("tensor", t_mm)
            dst = q2_a if qk == "q" else k2_a
            pair_ctx.append((bank, dst, hh, t2, slot, mtok, ttok))

        for (_, _, _, _, _, mtok, ttok) in pair_ctx:
            rec.wait("vector", mtok)
            rec.wait("vector", ttok)

        def v_xpair(v, ctxs=tuple(pair_ctx)):
            frs = ((fr1, fr2), (fr1b, fr2b))
            oaps = []
            for (bank, dst, hh, t2, slot, _, _), (f1, f2) in zip(ctxs, frs):
                ct = tabr[:, slot * 1024:slot * 1024 + 512]
                st = tabr[:, slot * 1024 + 512:slot * 1024 + 1024]
                o = dst[:, hh * NTOK + t2 * 512:hh * NTOK + (t2 + 1) * 512]
                v.tensor_tensor(f2[:, :], bank[:, :], ct[:, :], ALU.mult)
                v.tensor_tensor(f1[0:64, :], bank[64:128, :], st[0:64, :], ALU.mult)
                v.tensor_tensor(f1[64:128, :], bank[0:64, :], st[64:128, :], ALU.mult)
                oaps.append((o, f1, f2))
            last = None
            for o, f1, f2 in oaps:
                v.tensor_sub(o[0:64, :], f2[0:64, :], f1[0:64, :])
                last = v.tensor_add(o[64:128, :], f2[64:128, :], f1[64:128, :])
            return last
        ctok = rec.step("vector", v_xpair)
        bkg[0] = ctok
        bkg[1] = ctok
        tabslot_free[0] = ctok
        tabslot_free[1] = ctok
    wsl_free[0] = rec.cur_tok("tensor")
    wsl_free[1] = rec.cur_tok("tensor")

    vsp = []
    for vs in range(4):
        sl_i = 8 + vs
        load_wslice(sl_i, "qkv")
        for tt in range(8):
            bkid = (vs * 8 + tt) % 2
            bank = acq(bkid)
            rec.wait("tensor", wsl_tok[sl_i % 2])

            def t_vmm(t, buf=wsl[sl_i % 2], tt=tt, bank=bank):
                i = None
                for kc in range(KC):
                    i = t.matmul(bank[:, :],
                                 xn_fm[:, kc * NTOK + tt * 128:kc * NTOK + tt * 128 + 128],
                                 buf[:, kc * 512:(kc + 1) * 512],
                                 start=(kc == 0), stop=(kc == KC - 1))
                return i
            mtok = rec.step("tensor", t_vmm)
            rec.wait("vector", mtok)
            if len(vsp) >= 2:
                rec.wait("vector", vsp[-2])
            vr = v_r[(vs * 8 + tt) % 2]
            ctok = rec.step("vector", lambda v, bank=bank, vr=vr:
                            v.tensor_copy(vr[:, 0:512], bank[:, :]))
            bkg[bkid] = ctok
            rec.wait("sync", ctok)
            vsp.append(rec.dma(
                "sync", g_vsp[:, tt * C + vs * 512:tt * C + (vs + 1) * 512],
                vr[:, 0:512], "d_vsp"))
        wsl_free[sl_i % 2] = rec.cur_tok("tensor")

    tap_dump("q2", q2_a[:, :])
    tap_dump("k2", k2_a[:, :])
    if "v" in tap_aps:
        for t in vsp:
            rec.wait("sync", t)
        rec.dma("sync", tap_aps["v"][:, :], g_vsp[:, :], "d_tap")

    ysp = {}
    vld = [None, None]
    te_chunk = {}
    ve_chunk = {}
    T_SBF = None
    for n in range(NCH):
        vr = v_r[n % 2]
        if n < 2:
            for t in vsp:
                rec.wait("sync", t)
        else:
            rec.wait("sync", ("c_te", te_chunk[n - 2]))
        vld[n % 2] = rec.dma("sync", vr[:, :], g_vsp[:, n * C:(n + 1) * C], "d_vld")
        if n > 0:
            rec.wait("vector", ("c_ve", ve_chunk[n - 1]))
            T_SBF = rec.step("vector", lambda v: v.tensor_copy(Sbf[:, :], S_t[:, :]))
        yslot = y_r[n % 2]
        if n >= 2:
            rec.wait("scalar", ysp[n - 2])
        for h in range(H):
            i = n * H + h
            q2c = q2_a[:, h * NTOK + n * 128:h * NTOK + n * 128 + 128]
            k2c = k2_a[:, h * NTOK + n * 128:h * NTOK + n * 128 + 128]
            sm = smr[i % 2]
            kd = kdr[i % 2]
            bank_s = acq(i % 2)
            stok = rec.step("tensor", lambda t, bank=bank_s, k2c=k2c, q2c=q2c:
                            t.matmul(bank[:, 0:128], k2c, q2c, start=True, stop=True))
            bank_k = acq(4 + i % 2)
            bkb = bank_k.ap().bitcast(BF16_)
            ktok = rec.step("tensor", lambda t, bkb=bkb, k2c=k2c:
                            t.transpose(bkb[:, 0:128], k2c, identb[:, :]))
            rec.wait("vector", stok)
            smtok = rec.step("vector", lambda v, sm=sm, bank=bank_s:
                             v.tensor_tensor(sm[:, :], bank[:, 0:128],
                                             causal[:, :], ALU.mult))
            bkg[i % 2] = smtok
            rec.wait("scalar", ktok)
            kdtok = rec.step("scalar", lambda s, kd=kd, bkb=bkb, h=h:
                             s.activation(kd[:, :], bkb[:, 0:128], AF.Copy,
                                          scale=GL[h]))
            bkg[4 + i % 2] = kdtok
            bank_y = acq(2 + i % 2)
            rec.wait("tensor", smtok)
            rec.wait("tensor", vld[n % 2])
            if n > 0:
                rec.wait("tensor", T_SBF)

            def t_y(t, bank=bank_y, vr=vr, h=h, sm=sm, q2c=q2c, n=n):
                i1 = t.matmul(bank[:, 0:128], vr[:, h * 128:(h + 1) * 128],
                              sm[:, :], start=True, stop=(n == 0))
                if n > 0:
                    i1 = t.matmul(bank[:, 0:128], Sbf[:, h * 128:(h + 1) * 128],
                                  q2c, start=False, stop=True)
                return i1
            ytok = rec.step("tensor", t_y)
            bank_u = acq(6 + i % 2)
            rec.wait("tensor", kdtok)
            utok = rec.step("tensor", lambda t, bank=bank_u, kd=kd, vr=vr, h=h:
                            t.matmul(bank[:, 0:128], kd[:, :],
                                     vr[:, h * 128:(h + 1) * 128],
                                     start=True, stop=True))
            rec.wait("scalar", ytok)
            yetok = rec.step("scalar", lambda s, yslot=yslot, h=h, bank=bank_y:
                             s.activation(yslot[:, h * 128:(h + 1) * 128],
                                          bank[:, 0:128], AF.Copy))
            bkg[2 + i % 2] = yetok
            rec.wait("vector", utok)
            sutok = rec.step("vector", lambda v, h=h, bank=bank_u:
                             v.scalar_tensor_tensor(
                                 S_t[:, h * 128:(h + 1) * 128],
                                 S_t[:, h * 128:(h + 1) * 128], GL[h],
                                 bank[:, 0:128], ALU.mult, ALU.add))
            bkg[6 + i % 2] = sutok
        te_chunk[n] = rec.sems["c_te"][1]
        ve_chunk[n] = rec.sems["c_ve"][1]
        rec.wait_all("sync", "scalar")
        ysp[n] = rec.dma(
            "sync",
            g_ysp[:, :].rearrange("p (hh tt) -> p hh tt", hh=H)[:, :, n * 128:(n + 1) * 128],
            yslot[:, :].rearrange("p (hh ll) -> p hh ll", hh=H), "d_ysp")

    tap_dump("S", S_t[:, :])
    tap_dump("stats", stats[:, :])
    if "y" in tap_aps:
        for n in range(NCH):
            rec.wait("sync", ysp[n])
        rec.dma("sync", tap_aps["y"][:, :], g_ysp[:, :], "d_tap")
    rec.wait_all("sync", "vector")
    T_QSP = rec.dma("sync", g_qsp[:, :], q2_a[:, :], "d_sp")
    T_XNSP = rec.dma("sync", g_xnsp[:, :], xn_fm[:, :], "d_sp")
    T_SLSP = rec.dma("sync", g_sl_i[:, :], S_t[:, :], "d_sp")
    rec.wait("gpsimd", T_SLSP)
    cc["sl"] = rec.step("gpsimd", lambda g: g.collective_compute(
        "AllGather", ALU.bypass, replica_groups=RG,
        ins=[g_sl_i.tensor.ap().opt()], outs=[g_sl.tensor.ap().opt()]))

    rec.wait("sync", T_N1SP)
    rec.wait("sync", T_QSP)
    rec.wait("sync", T_XNSP)
    for n in range(NCH):
        rec.wait("sync", ysp[n])
    rec.wait_all("sync", "tensor")
    rec.wait_all("sync", "vector")
    rec.wait_all("sync", "scalar")
    T_N1R = rec.dma("sync", n1r[:, :], g_n1sp[:, :], "d_n1r")
    rec.wait("sync", cc["q1"])
    rec.wait("scalar", T_QSP)
    rec.wait("scalar", T_XNSP)
    rec.wait("vector", T_QSP)
    rec.wait("vector", T_XNSP)

    for hf in range(2):
        ctk = None
        for jj in range(4):
            j = hf * 4 + jj
            ctk = rank1_to(C1R, c1rows[0:1, j * 128:(j + 1) * 128], jj)

        q1free = [None]
        ep_prev = []
        for it in range(IT):
            rec.wait("sync", q1free[0])
            q1tok = rec.dma("sync", q1b[0][:, :].rearrange("p (c n) -> p c n", c=KC),
                            g_q1[:, it * 128:(it + 1) * 128]
                            .rearrange("(c p) n -> p c n", p=128), "d_q1")
            bkid = it % 2
            bank = acq(bkid)
            rec.wait("tensor", q1tok)
            if it == 0:
                rec.wait("tensor", T_N1R)
                rec.wait("tensor", ctk)

            def t_mm1(t, qb=q1b[0], hf=hf, bank=bank):
                i = None
                for kc in range(KC):
                    i = t.matmul(bank[:, :], qb[:, kc * 128:(kc + 1) * 128],
                                 n1r[:, kc * NTOK + hf * 512:kc * NTOK + (hf + 1) * 512],
                                 start=(kc == 0), stop=(kc == KC - 1))
                return i
            mtok = rec.step("tensor", t_mm1)
            q1free[0] = rec.cur_tok("tensor")
            rec.wait("vector", mtok)
            if ep_prev:
                rec.wait("vector", ep_prev[-1])
            vtok = rec.step("vector", lambda v, bank=bank:
                            v.tensor_tensor(fr1[:, :], bank[:, :], C1R[:, :], ALU.mult))
            bkg[bkid] = vtok
            rec.wait("scalar", vtok)
            atok = rec.step("scalar", lambda a, it=it:
                            a.activation(harena[:, it * 512:(it + 1) * 512],
                                         fr1[:, :], AF.Silu, bias=b1_fm[:, it:it + 1]))
            ep_prev.append(atok)

        gel = []
        for it in range(IT):
            r = gbr[it % 2]
            if it < 2:
                rec.wait("scalar", ep_prev[-1])
            else:
                rec.wait("scalar", gel[it - 2][1])
            gt = rec.step("scalar", lambda a, it=it, r=r:
                          a.activation(r[:, :], harena[:, it * 512:(it + 1) * 512],
                                       AF.Gelu))
            rec.wait("vector", gt)
            ct = rec.step("vector", lambda v, it=it, r=r:
                          v.tensor_copy(harena[:, it * 512:(it + 1) * 512], r[:, :]))
            gel.append((gt, ct))

        if "h" in tap_aps:
            rec.wait_all("sync", "vector")
            rec.wait_all("sync", "scalar")
            rec.wait_all("sync", "tensor")
            t = rec.dma("sync", tap_aps["h"][:, hf * IT * 512:(hf + 1) * IT * 512],
                        harena[:, :], "d_tap")
            for e in ("vector", "scalar", "tensor"):
                rec.wait(e, t)
        bank7 = acq(7)
        sq = []
        for it in range(IT):
            r = sqrr[it % 2]
            rec.wait("scalar", gel[it][1])
            if it >= 2:
                rec.wait("scalar", sq[it - 2][1])
            qt = rec.step("scalar", lambda a, it=it, r=r:
                          a.activation(r[:, :], harena[:, it * 512:(it + 1) * 512],
                                       AF.Square))
            rec.wait("tensor", qt)
            if it == 0:
                rec.wait("tensor", gel[-1][1])

            def t_st(t, it=it, r=r, bank7=bank7):
                i = None
                for sub in range(4):
                    t.matmul(bank7[:, sub:sub + 1],
                             harena[:, it * 512 + sub * 128:it * 512 + (sub + 1) * 128],
                             onesb_c[:, :], start=(it == 0 and sub == 0),
                             stop=False)
                    i = t.matmul(bank7[:, 8 + sub:9 + sub],
                                 r[:, sub * 128:(sub + 1) * 128], onesb_c[:, :],
                                 start=False,
                                 stop=(it == IT - 1 and sub == 3))
                return i
            sq.append((qt, rec.step("tensor", t_st)))
        rec.wait("vector", sq[-1][1])

        def v_hst(v, bank7=bank7):
            v.tensor_scalar(stat2[:, 0:4], bank7[:, 0:4], 1.0 / INNER, None, ALU.mult)
            v.tensor_scalar(stat2[:, 4:8], bank7[:, 8:12], 1.0 / INNER, None, ALU.mult)
            v.tensor_tensor(stat2[:, 8:12], stat2[:, 0:4], stat2[:, 0:4], ALU.mult)
            v.tensor_sub(stat2[:, 12:16], stat2[:, 4:8], stat2[:, 8:12])
            v.tensor_scalar(stat2[:, 16:20], stat2[:, 12:16], EPS, None, ALU.add)
            return v.reciprocal(stat2[:, 20:24], stat2[:, 16:20])
        vt = rec.step("vector", v_hst)
        bkg[7] = vt
        rec.wait("scalar", vt)
        at2 = rec.step("scalar", lambda a: (
            a.activation(stat2[:, 24:28], stat2[:, 20:24], AF.Ln),
            a.activation(stat2[:, 28:32], stat2[:, 24:28], AF.Exp, scale=0.5))[1])
        rec.wait("vector", at2)
        vt = rec.step("vector", lambda v: v.tensor_tensor(
            stat2[:, 32:36], stat2[:, 0:4], stat2[:, 28:32], ALU.mult))
        rec.wait("tensor", vt)
        for jj in range(4):
            col_to_row(stat2[:, 28 + jj:29 + jj], rowf[0:1, :])
            rank1_to(R1, rowf[0:1, :], jj, tokdep=rec.cur_tok("vector"))
            col_to_row(stat2[:, 32 + jj:33 + jj], rowf[0:1, :])
            rank1_to(M1R, rowf[0:1, :], jj, tokdep=rec.cur_tok("vector"))

        T_RM = rec.step("vector", lambda v: (v.memset(rmax[:, :], -1e30),
                                             v.memset(rmin[:, :], 1e30))[1])
        hln = []
        for it in range(IT):
            def v_hln(v, it=it):
                v.tensor_tensor(fr1[:, :], harena[:, it * 512:(it + 1) * 512],
                                R1[:, :], ALU.mult)
                v.tensor_sub(fr2[:, :], fr1[:, :], M1R[:, :])
                v.tensor_scalar(fr1[:, :], fr2[:, :], lng_fm[:, it:it + 1],
                                lnb_fm[:, it:it + 1], ALU.mult, ALU.add)
                v.tensor_copy(harena[:, it * 512:(it + 1) * 512], fr1[:, :])
                v.tensor_max(rmax[:, :], rmax[:, :], fr1[:, :])
                return v.tensor_tensor(rmin[:, :], rmin[:, :], fr1[:, :], ALU.min)
            hln.append(rec.step("vector", v_hln))
        vt0 = rec.step("vector", lambda v: (
            v.tensor_scalar(fr1[:, :], rmin[:, :], -1.0, None, ALU.mult),
            v.tensor_max(fr2[:, :], rmax[:, :], fr1[:, :]))[1])
        rec.wait("tensor", vt0)
        for jj in range(4):
            bank = acq(5)
            tok = rec.step("tensor", lambda t, jj=jj, bank=bank:
                           t.transpose(bank[:, 0:128],
                                       fr2[:, jj * 128:(jj + 1) * 128], identf[:, :]))
            rec.wait("vector", tok)
            ct = rec.step("vector", lambda v, jj=jj, bank=bank:
                          v.tensor_reduce(stat2[:, 36 + jj:37 + jj], bank[:, 0:128],
                                          AX.X, ALU.max))
            bkg[5] = ct

        bank7 = acq(7)
        sq2 = []
        for it in range(IT):
            r = sqrr[it % 2]
            if it >= 2:
                rec.wait("scalar", sq2[it - 2][1])
            rec.wait("scalar", hln[it] if it != IT - 1 else hln[-1])
            qt = rec.step("scalar", lambda a, it=it, r=r:
                          a.activation(r[:, :], harena[:, it * 512:(it + 1) * 512],
                                       AF.Square))
            rec.wait("tensor", qt)

            def t_st2(t, it=it, r=r, bank7=bank7):
                i = None
                for sub in range(4):
                    i = t.matmul(bank7[:, 16 + sub:17 + sub],
                                 r[:, sub * 128:(sub + 1) * 128], onesb_c[:, :],
                                 start=(it == 0 and sub == 0),
                                 stop=(it == IT - 1 and sub == 3))
                return i
            sq2.append((qt, rec.step("tensor", t_st2)))
        rec.wait("vector", sq2[-1][1])

        def v_rms2(v, bank7=bank7):
            v.tensor_scalar(stat2[:, 40:44], bank7[:, 16:20], 1.0 / INNER, None, ALU.mult)
            v.tensor_scalar(stat2[:, 44:48], stat2[:, 40:44], EPS, None, ALU.add)
            return v.reciprocal(stat2[:, 48:52], stat2[:, 44:48])
        vt = rec.step("vector", v_rms2)
        bkg[7] = vt
        rec.wait("scalar", vt)
        at3 = rec.step("scalar", lambda a: (
            a.activation(stat2[:, 52:56], stat2[:, 48:52], AF.Ln),
            a.activation(stat2[:, 56:60], stat2[:, 52:56], AF.Exp, scale=0.5))[1])
        rec.wait("vector", at3)

        def v_c2(v, hf=hf):
            v.tensor_tensor(statB[:, 16:20], stat2[:, 56:60], stat2[:, 36:40], ALU.mult)
            v.tensor_scalar(statB[:, 16:20], statB[:, 16:20], 1e-5, None, ALU.max)
            v.reciprocal(statB[:, 20:24], statB[:, 16:20])
            v.tensor_scalar(statB[:, 24:28], statB[:, 20:24], 127.0, None, ALU.mult)
            v.tensor_tensor(statB[:, 28:32], stat2[:, 56:60], statB[:, 24:28], ALU.mult)
            return v.tensor_scalar(statB[:, 32:36], statB[:, 24:28], sw_b[:, 1:2], None, ALU.mult)
        vtA = rec.step("vector", v_c2)
        rec.wait("scalar", vtA)
        stA = rec.step("scalar", lambda a: a.nop())
        rec.wait("vector", stA)
        vt = rec.step("vector", lambda v: v.reciprocal(scl4[:, 0:4], statB[:, 32:36]))
        rec.wait("scalar", vt)
        stB = rec.step("scalar", lambda a: a.nop())
        rec.wait("vector", stB)
        vt2 = vt
        if hf == 0:
            tap_dump("statB", statB[:, :])
            tap_dump("scl4", scl4[:, :])
        rec.wait("tensor", vt)
        for jj in range(4):
            j = hf * 4 + jj
            col_to_row(statB[:, 28 + jj:29 + jj], rowf[0:1, :])
            rank1_to(C2R, rowf[0:1, :], jj, tokdep=rec.cur_tok("vector"))
            col_to_row(statB[:, 32 + jj:33 + jj], sv2rows[0:1, j * 128:(j + 1) * 128])

        n2t = []
        for it in range(IT):
            def v_n2(v, it=it):
                MAGIC = 12582912.0
                v.tensor_tensor(fr1[:, :], harena[:, it * 512:(it + 1) * 512],
                                C2R[:, :], ALU.mult)
                v.tensor_scalar(fr2[:, :], fr1[:, :], MAGIC, None, ALU.add)
                return v.tensor_scalar(harena[:, it * 512:(it + 1) * 512],
                                       fr2[:, :], -MAGIC, None, ALU.add)
            n2t.append(rec.step("vector", v_n2))
        if "n2" in tap_aps:
            rec.wait_all("sync", "vector")
            rec.wait_all("sync", "tensor")
            t = rec.dma("sync", tap_aps["n2"][:, hf * IT * 512:(hf + 1) * IT * 512],
                        harena[:, :], "d_tap")
            for e in ("vector", "scalar", "tensor"):
                rec.wait(e, t)

        rec.wait("sync", cc["q2"])
        w2tok = [None, None]
        w2free = [None, None]
        for cs in range(4):
            banks4 = [acq(tcj) for tcj in range(4)]
            rec.wait("tensor", n2t[-1])
            if cs > 0:
                w2free[0] = rec.cur_tok("tensor")
                w2free[1] = rec.cur_tok("tensor")
            for ikb in range(16):
                s = ikb % 2
                rec.wait("sync", w2free[s])
                w2tok[s] = rec.dma(
                    "sync", w2b[s][:, :].rearrange("p (c n) -> p c n", c=4),
                    g_q2[ikb * 512:(ikb + 1) * 512, cs * 512:(cs + 1) * 512]
                    .rearrange("(c p) n -> p c n", p=128), f"d_w2{s}")
                rec.wait("tensor", w2tok[s])

                def t_mm2(t, s=s, ikb=ikb, banks4=banks4):
                    i = None
                    for ikc in range(4):
                        it = ikb * 4 + ikc
                        for tcj in range(4):
                            i = t.matmul(
                                banks4[tcj][:, :],
                                harena[:, it * 512 + tcj * 128:it * 512 + (tcj + 1) * 128],
                                w2b[s][:, ikc * 512:(ikc + 1) * 512],
                                start=(it == 0), stop=False)
                    return i
                rec.step("tensor", t_mm2)
                w2free[s] = rec.cur_tok("tensor")
            for tcj in range(4):
                j = hf * 4 + tcj
                bank = banks4[tcj]
                if cs == 0 and tcj == 0:
                    rec.wait("tensor", rec.cur_tok("vector"))
                btok = rec.step("tensor", lambda t, bank=bank, j=j, cs=cs:
                                t.matmul(bank[:, :],
                                         sv2rows[0:1, j * 128:(j + 1) * 128],
                                         b2_r[0:1, cs * 512:(cs + 1) * 512],
                                         start=False, stop=True))
                rec.wait("vector", btok)
                rec.wait("vector", vt2)
                if cs > 0 or tcj > 0:
                    rec.wait("vector", rec.cur_tok("scalar"))
                vtok2 = rec.step("vector", lambda v, bank=bank, tcj=tcj:
                                 v.tensor_scalar(fr1[:, :], bank[:, :],
                                                 scl4[:, tcj:tcj + 1], None, ALU.mult))
                bkg[tcj] = vtok2
                rec.wait("scalar", vtok2)
                etok = rec.step("scalar", lambda a, j=j, cs=cs:
                                a.activation(
                                    g_tm[:, j * C + cs * 512:j * C + (cs + 1) * 512],
                                    fr1[:, :], AF.Silu))
    tap_dump("g", g_tm[:, :])

    TE_S3 = rec.cur_tok("tensor")
    AC_S3 = rec.cur_tok("scalar")
    VE_S3 = rec.cur_tok("vector")
    rec.wait("sync", T_XNSP)
    rec.wait("sync", TE_S3)
    rec.wait("sync", AC_S3)
    rec.wait("sync", VE_S3)
    T_XNR = rec.dma("sync", xn_fm4[:, :], g_xnsp[:, :], "d_xnr")
    rec.wait("gpsimd", cc["sl"])
    rec.wait("gpsimd", TE_S3)
    rec.wait("gpsimd", AC_S3)
    rec.wait("vector", TE_S3)
    rec.wait("vector", AC_S3)
    rec.wait("scalar", TE_S3)
    rec.wait("scalar", VE_S3)
    T_SI0 = rec.step("vector", lambda v: v.memset(sincA[:, :], 0.0))
    sl_toks = []
    prev = T_SI0
    srcbuf, dstbuf = sincA, sincB
    for pp in range(NCORES):
        sslot = sring[pp % 2]
        if pp >= 2:
            rec.wait("gpsimd", sl_toks[pp - 2][1])
        ltok = rec.dma("gpsimd", sslot[:, :], g_sl[pp * 128:(pp + 1) * 128, :],
                       "d_sl")
        rec.wait("vector", ltok)
        rec.wait("vector", prev)

        def v_si(v, pp=pp, sslot=sslot, src=srcbuf, dst=dstbuf):
            last = None
            for h in range(H):
                last = v.scalar_tensor_tensor(
                    dst[:, h * 128:(h + 1) * 128],
                    sslot[:, h * 128:(h + 1) * 128],
                    cvec[:, pp * H + h:pp * H + h + 1],
                    src[:, h * 128:(h + 1) * 128], ALU.mult, ALU.add)
            return last
        prev = rec.step("vector", v_si)
        sl_toks.append((ltok, prev))
        srcbuf, dstbuf = dstbuf, srcbuf
    T_SINC = rec.step("vector", lambda v, src=srcbuf: v.tensor_copy(
        sincbf[:, :], src[:, :]))
    if "sinc" in tap_aps:
        rec.wait_all("sync", "vector")
        rec.dma("sync", tap_aps["sinc"][:, :], sincbf[:, :], "d_tap")

    rec.wait("sync", cc["wg"])
    wg_tok = [None]
    wg_free = [None]
    y4tok = [None, None]
    q4tok = [None, None]
    gs_done = [None, None]
    sp_done = []
    ve_h = {}
    for h in range(H):
        if h % 4 == 0:
            hs = h // 4
            rec.wait("sync", wg_free[0])
            if h == 0:
                rec.wait_all("sync", "tensor")
            wg_tok[0] = rec.dma("sync", wsl4[:, :].rearrange("p (c n) -> p c n", c=KC),
                                g_wg[:, hs * 512:(hs + 1) * 512]
                                .rearrange("(c p) n -> p c n", p=128), "d_wg")
        if h >= 2:
            rec.wait("sync", ("c_ve", ve_h[h - 2]))
        else:
            rec.wait("sync", T_SINC)
        y4tok[h % 2] = rec.dma("sync", y4r[h % 2][:, :],
                               g_ysp[:, h * NTOK:(h + 1) * NTOK], "d_y4")
        q4tok[h % 2] = rec.dma("sync", q4r[h % 2][:, :],
                               g_qsp[:, h * NTOK:(h + 1) * NTOK], "d_q4")
        gslot = gsil[h % 2]
        for t2 in range(2):
            bank = acq(4 + t2)
            rec.wait("tensor", wg_tok[0])
            if h == 0 and t2 == 0:
                rec.wait("tensor", T_XNR)

            def t_wg(t, h=h, t2=t2, bank=bank):
                i = None
                for kc in range(KC):
                    i = t.matmul(bank[:, :],
                                 wsl4[:, kc * 512 + (h % 4) * 128:kc * 512 + (h % 4) * 128 + 128],
                                 xn_fm4[:, kc * NTOK + t2 * 512:kc * NTOK + (t2 + 1) * 512],
                                 start=(kc == 0), stop=(kc == KC - 1))
                return i
            wtok = rec.step("tensor", t_wg)
            rec.wait("scalar", wtok)
            if h < 2:
                rec.wait("scalar", T_SINC)
            if gs_done[h % 2] is not None:
                rec.wait("scalar", gs_done[h % 2])
            gtok = rec.step("scalar", lambda a, bank=bank, gslot=gslot, t2=t2:
                            a.activation(gslot[:, t2 * 512:(t2 + 1) * 512],
                                         bank[:, :], AF.Silu))
            bkg[4 + t2] = gtok
        if h % 4 == 3:
            wg_free[0] = rec.cur_tok("tensor")
        bank7 = acq(7)
        yt_toks = {}
        for n in range(NCH):
            rec.wait("vector", q4tok[h % 2])
            qd = qdr4[n % 2]
            gnl = float(np.float32(GAMMA[h] ** (n * L)))
            if n >= 2:
                rec.wait("vector", yt_toks[n - 2][1])
            qtok = rec.step("vector", lambda v, qd=qd, h=h, n=n, gnl=gnl:
                            v.tensor_scalar(qd[:, :],
                                            q4r[h % 2][:, n * 128:(n + 1) * 128],
                                            gnl, None, ALU.mult))
            bank6 = acq(5)
            rec.wait("tensor", qtok)
            rec.wait("tensor", T_SINC)
            itok = rec.step("tensor", lambda t, bank=bank6, h=h, qd=qd:
                            t.matmul(bank[:, 0:128],
                                     sincbf[:, h * 128:(h + 1) * 128], qd[:, :],
                                     start=True, stop=True))
            rec.wait("vector", itok)
            rec.wait("vector", y4tok[h % 2])
            yttok = rec.step("vector", lambda v, bank=bank6, h=h, n=n:
                             v.tensor_tensor(ybuf[:, n * 128:(n + 1) * 128],
                                             y4r[h % 2][:, n * 128:(n + 1) * 128],
                                             bank[:, 0:128], ALU.add))
            bkg[5] = yttok
            rec.wait("scalar", yttok)
            if n >= 1:
                rec.wait("scalar", yt_toks[n - 1][2])
            sqtok = rec.step("scalar", lambda a, n=n:
                             a.activation(ysq[:, :], ybuf[:, n * 128:(n + 1) * 128],
                                          AF.Square))
            rec.wait("tensor", sqtok)

            def t_gst(t, n=n, bank7=bank7):
                t.matmul(bank7[:, n:n + 1], ybuf[:, n * 128:(n + 1) * 128],
                         onesb_c[:, :], start=(n == 0), stop=False)
                return t.matmul(bank7[:, 8 + n:9 + n], ysq[:, :], onesb_c[:, :],
                                start=False, stop=(n == NCH - 1))
            sttok = rec.step("tensor", t_gst)
            yt_toks[n] = (qtok, yttok, sttok)
        rec.wait("vector", yt_toks[NCH - 1][2])

        def v_gn(v, bank7=bank7):
            last = None
            for g4 in range(4):
                sl2 = slice(2 * g4, 2 * g4 + 2)
                v.tensor_scalar(gnmu[:, sl2], bank7[:, 2 * g4:2 * g4 + 2],
                                1.0 / D, None, ALU.mult)
            for g4 in range(4):
                sl2 = slice(2 * g4, 2 * g4 + 2)
                v.tensor_scalar(gnrs[:, sl2], bank7[:, 8 + 2 * g4:10 + 2 * g4],
                                1.0 / D, None, ALU.mult)
            for g4 in range(4):
                sl2 = slice(2 * g4, 2 * g4 + 2)
                v.tensor_tensor(gnmp[:, sl2], gnmu[:, sl2], gnmu[:, sl2], ALU.mult)
            for g4 in range(4):
                sl2 = slice(2 * g4, 2 * g4 + 2)
                v.tensor_sub(gnrs[:, sl2], gnrs[:, sl2], gnmp[:, sl2])
            for g4 in range(4):
                sl2 = slice(2 * g4, 2 * g4 + 2)
                last = v.tensor_scalar(gnrs[:, sl2], gnrs[:, sl2], EPS, None, ALU.add)
            return last
        vt = rec.step("vector", v_gn)
        bkg[7] = vt
        if h == 1:
            tap_dump("ybuf0", ybuf[:, :])
            tap_dump("sinc2", sincbf[:, :])
        if h == 1:
            tap_dump("gn0", gnrs[:, 0:16])
        rec.wait("scalar", vt)
        at4 = rec.step("scalar", lambda a: (
            a.activation(gnmp[:, 0:8], gnrs[:, 0:8], AF.Ln),
            a.activation(gnrs[:, 0:8], gnmp[:, 0:8], AF.Exp, scale=-0.5))[1])
        rec.wait("vector", at4)
        vt = rec.step("vector", lambda v: (
            v.tensor_tensor(gnmp[:, 0:8], gnmu[:, 0:8], gnrs[:, 0:8], ALU.mult),
            v.tensor_scalar(gnmp[:, 0:8], gnmp[:, 0:8], -1.0, None, ALU.mult))[1])
        rec.wait("tensor", vt)
        for n in range(NCH):
            rtok = col_to_row(gnrs[:, n:n + 1], rowb[0:1, :], bf16=True)
            bankA = acq(2)
            rec.wait("tensor", rtok)
            atok = rec.step("tensor", lambda t, bank=bankA, h=h:
                            t.matmul(bank[:, 0:128],
                                     gnw_r[0:1, h * 128:(h + 1) * 128],
                                     rowb[0:1, :], start=True, stop=True))
            rtok = col_to_row(gnmp[:, n:n + 1], rowb[0:1, :], bf16=True)
            bankB = acq(3)
            rec.wait("tensor", rtok)

            def t_B(t, bank=bankB, h=h):
                t.matmul(bank[:, 0:128], gnw_r[0:1, h * 128:(h + 1) * 128],
                         rowb[0:1, :], start=True, stop=False)
                return t.matmul(bank[:, 0:128], gnb_r[0:1, h * 128:(h + 1) * 128],
                                onesb_r[:, :], start=False, stop=True)
            btok = rec.step("tensor", t_B)
            rec.wait("vector", atok)
            rec.wait("vector", btok)
            rec.wait("vector", ("c_ac", rec.sems["c_ac"][1]))

            def v_sp(v, bankA=bankA, bankB=bankB, h=h, n=n, gslot=gslot):
                for qq in range(4):
                    s32 = slice(qq * 32, qq * 32 + 32)
                    v.tensor_tensor(t4a[:, s32], ybuf[:, n * 128 + qq * 32:
                                                      n * 128 + qq * 32 + 32],
                                    bankA[:, s32], ALU.mult)
                for qq in range(4):
                    s32 = slice(qq * 32, qq * 32 + 32)
                    v.tensor_add(t4b[:, s32], t4a[:, s32], bankB[:, s32])
                last = None
                for qq in range(4):
                    s32 = slice(qq * 32, qq * 32 + 32)
                    last = v.tensor_tensor(
                        sproj[:, h * NTOK + n * 128 + qq * 32:
                              h * NTOK + n * 128 + qq * 32 + 32],
                        t4b[:, s32], gslot[:, n * 128 + qq * 32:
                                           n * 128 + qq * 32 + 32], ALU.mult)
                return last
            sptok = rec.step("vector", v_sp)
            bkg[2] = sptok
            bkg[3] = sptok
            sp_done.append(sptok)
        gs_done[h % 2] = sp_done[-1]
        ve_h[h] = rec.sems["c_ve"][1]
    tap_dump("sproj", sproj[:, :])

    rec.wait("sync", cc["wp"])
    nfin = 0
    fin_toks = {}
    for co in range(4):
        rec.wait("sync", rec.cur_tok("tensor"))
        wp_tok = rec.dma("sync", wsl4[:, :].rearrange("p (c n) -> p c n", c=KC),
                         g_wp[:, co * 512:(co + 1) * 512]
                         .rearrange("(c p) n -> p c n", p=128), "d_wp")
        for tt in range(8):
            i4 = co * 8 + tt
            xs = xrg[i4 % 2]
            osl = org[i4 % 2]
            if i4 >= 2:
                rec.wait("sync", fin_toks[i4 - 2])
            xtok = rec.dma("sync", xs[:, :],
                           x_in[:, :].rearrange("(c p) n -> p c n", p=128)
                           [:, tt, co * 512:(co + 1) * 512], "d_xr")
            bank = acq(i4 % 4)
            rec.wait("tensor", wp_tok)
            if i4 == 0:
                rec.wait("tensor", sp_done[-1])

            def t_pr(t, tt=tt, bank=bank):
                i = None
                for kc in range(KC):
                    i = t.matmul(bank[:, :],
                                 sproj[:, kc * NTOK + tt * 128:kc * NTOK + tt * 128 + 128],
                                 wsl4[:, kc * 512:(kc + 1) * 512],
                                 start=(kc == 0), stop=(kc == KC - 1))
                return i
            mtok = rec.step("tensor", t_pr)
            rec.wait("vector", mtok)
            rec.wait("vector", xtok)

            def v_fin(v, bank=bank, tt=tt, co=co, xs=xs, osl=osl, i4=i4):
                base = (i4 % 2) * 512
                for qq in range(4):
                    s = slice(qq * 128, qq * 128 + 128)
                    v.tensor_tensor(tfin[:, base + qq * 128:base + qq * 128 + 128],
                                    bank[:, s],
                                    g_tm[:, tt * C + co * 512 + qq * 128:
                                         tt * C + co * 512 + qq * 128 + 128],
                                    ALU.mult)
                last = None
                for qq in range(4):
                    s = slice(qq * 128, qq * 128 + 128)
                    last = v.tensor_add(osl[:, s],
                                        tfin[:, base + qq * 128:base + qq * 128 + 128],
                                        xs[:, s])
                return last
            vtok = rec.step("vector", v_fin)
            bkg[i4 % 4] = vtok
            rec.wait("sync", vtok)
            fin_toks[i4] = rec.dma(
                "sync", out_ap[:, :].rearrange("(c p) n -> p c n", p=128)
                [:, tt, co * 512:(co + 1) * 512], osl[:, :], "d_out")
            nfin += 1
    rec.wait("sync", ("d_out", 16 * nfin))
    if tap_aps:
        rec.wait("sync", ("d_tap", rec.sems["d_tap"][1]))

    with nc.Block() as block:
        rec.emit_all(block)
    assert nc.sbuf_base <= BASE, (
        f"const pool overran guard: sbuf_base={nc.sbuf_base} > BASE={BASE}")
    return nc


def _perm_idx():
    return np.concatenate([np.arange(0, 128, 2), np.arange(1, 128, 2)])


def _build_tables():
    d2 = np.arange(0, C, 2, dtype=np.float64)
    dim_scale = (d2 + 0.4 * C) / (1.4 * C)
    pos = np.arange(T, dtype=np.float64)
    scale = dim_scale[None, :] ** (pos[:, None] / 512.0)
    inv_freq = 1.0 / 10000.0 ** (d2 / C)
    sinusoid = pos[:, None] * inv_freq[None, :]
    sin_full = np.repeat(np.sin(sinusoid) * scale, 2, axis=-1)
    cos_full = np.repeat(np.cos(sinusoid) * scale, 2, axis=-1)
    pidx = _perm_idx()
    gperm = np.concatenate([hh * 128 + pidx for hh in range(H)])
    cosp = cos_full[:, gperm]
    sinp = sin_full[:, gperm]
    i_loc = (pos % L)
    qfac = GAMMA[None, :] ** (i_loc[:, None] + 1.0) * D ** -0.5
    kfac = GAMMA[None, :] ** (-(i_loc[:, None] + 1.0))
    qf = np.repeat(qfac, 128, axis=1)
    kf = np.repeat(kfac, 128, axis=1)
    return (np.ascontiguousarray((cosp * qf).T).astype(BF),
            np.ascontiguousarray((sinp * qf).T).astype(BF),
            np.ascontiguousarray((cosp * kf).T).astype(BF),
            np.ascontiguousarray((sinp * kf).T).astype(BF))


def _host_prep(inputs):
    x = np.asarray(inputs["L0_attn_x"], np.float32)
    wqkv = np.asarray(inputs["L0_attn_w_qkv"], np.float32)
    wg = np.asarray(inputs["L0_attn_w_g"], np.float32)
    wproj = np.asarray(inputs["L0_attn_w_proj"], np.float32)
    gnw = np.asarray(inputs["L0_attn_gn_weight"], np.float32)
    gnb = np.asarray(inputs["L0_attn_gn_bias"], np.float32)
    w1 = np.asarray(inputs["L0_ffn_w1"], np.float32)
    b1 = np.asarray(inputs["L0_ffn_b1"], np.float32)
    lng = np.asarray(inputs["L0_ffn_ln_g"], np.float32)
    lnb = np.asarray(inputs["L0_ffn_ln_b"], np.float32)
    w2 = np.asarray(inputs["L0_ffn_w2"], np.float32)
    b2 = np.asarray(inputs["L0_ffn_b2"], np.float32)

    pidx = _perm_idx()
    gperm = np.concatenate([hh * 128 + pidx for hh in range(H)])
    wqkv_p = wqkv.copy()
    wqkv_p[:, :C] = wqkv[:, :C][:, gperm]
    wqkv_p[:, C:2 * C] = wqkv[:, C:2 * C][:, gperm]

    cosq, sinq, cosk, sink = _build_tables()
    causal = np.triu(np.ones((L, L), np.float32)).astype(BF)
    identf = np.eye(128, dtype=np.float32)
    identb = np.eye(128, dtype=np.float32).astype(BF)
    b1_fm = np.ascontiguousarray(b1.reshape(IT, 128).T)
    lng_fm = np.ascontiguousarray(lng.reshape(IT, 128).T)
    lnb_fm = np.ascontiguousarray(lnb.reshape(IT, 128).T)

    in_maps = []
    for core in range(NCORES):
        b_, p_ = core // 4, core % 4
        rows = slice(p_ * NTOK, (p_ + 1) * NTOK)
        trows = slice(b_ * 0 + p_ * NTOK, p_ * NTOK + NTOK)
        cvec = np.zeros((NCORES, H), np.float64)
        for pp in range(p_):
            cvec[b_ * 4 + pp] = GAMMA ** (1024.0 * (p_ - pp - 1))
        cvec_b = np.broadcast_to(cvec.reshape(1, NCORES * H).astype(np.float32),
                                 (128, NCORES * H)).copy()
        sl8 = core
        m = {
            "x_loc": np.ascontiguousarray(x[b_, rows]),
            "cosq": np.ascontiguousarray(cosq[:, trows]),
            "sinq": np.ascontiguousarray(sinq[:, trows]),
            "cosk": np.ascontiguousarray(cosk[:, trows]),
            "sink": np.ascontiguousarray(sink[:, trows]),
            "causal": causal,
            "ident_f": identf,
            "ident_b": identb,
            "gnw_row": gnw.reshape(1, C).astype(BF),
            "gnb_row": gnb.reshape(1, C).astype(BF),
            "b1_fm": b1_fm,
            "b2_row": b2.reshape(1, C).astype(BF),
            "lng_fm": lng_fm,
            "lnb_fm": lnb_fm,
            "cvec": cvec_b,
            "wqkv_s": np.ascontiguousarray(wqkv_p[sl8 * 256:(sl8 + 1) * 256]),
            "wg_s": np.ascontiguousarray(wg[sl8 * 256:(sl8 + 1) * 256]),
            "wproj_s": np.ascontiguousarray(wproj[sl8 * 256:(sl8 + 1) * 256]),
            "w1t_s": np.ascontiguousarray(w1[:, sl8 * 256:(sl8 + 1) * 256].T),
            "w2t_s": np.ascontiguousarray(w2[:, sl8 * 1024:(sl8 + 1) * 1024].T),
        }
        in_maps.append(m)
    return in_maps


def kernel(**inputs):
    if "nc" not in _CACHE:
        _CACHE["nc"] = build_program()
    nc = _CACHE["nc"]
    in_maps = _host_prep(inputs)
    res = run_bass_kernel_spmd(nc, in_maps, core_ids=list(range(NCORES)))
    out = np.zeros((B, T, C), np.float32)
    for core in range(NCORES):
        b_, p_ = core // 4, core % 4
        out[b_, p_ * NTOK:(p_ + 1) * NTOK] = res.results[core]["out"]
    return out


# version 13
# speedup vs baseline: 1.1429x; 1.1276x over previous
import numpy as np
import ml_dtypes
import concourse.bass as bass
import concourse.mybir as mybir
from concourse.bass_isa import ReduceOp
from concourse.bass_utils import run_bass_kernel_spmd

F32 = mybir.dt.float32
BF16 = mybir.dt.bfloat16
I32 = mybir.dt.int32
AF = mybir.ActivationFunctionType
ALU = mybir.AluOpType
AX = mybir.AxisListType
BF = ml_dtypes.bfloat16
BF16_ = mybir.dt.bfloat16

B, T, C = 2, 4096, 2048
H, D = 16, 128
L = 128
NTOK = 1024
NCH = 8
INNER = 8192
EPS = 1e-5
NCORES = 8
KC = C // 128
IT = INNER // 128

GAMMA = (1.0 - 2.0 ** (-5.0 - np.arange(H, dtype=np.float64)))
GL = [float(np.float32(g ** L)) for g in GAMMA]

_CACHE = {}


class Rec:
    ENGS = ("sync", "gpsimd", "vector", "scalar", "tensor")
    ESEM = {"sync": "c_sy", "gpsimd": "c_gp", "vector": "c_ve",
            "scalar": "c_ac", "tensor": "c_te"}

    def __init__(self, nc):
        self.nc = nc
        self.ops = {e: [] for e in self.ENGS}
        self.sems = {}
        self.emitted_inc = {}
        for s in self.ESEM.values():
            self.sem(s)

    def sem(self, name):
        if name not in self.sems:
            h = self.nc.semaphore(name).__enter__()
            self.sems[name] = [h, 0]
        return name

    def inc(self, name, by=1):
        ent = self.sems[name]
        ent[1] += by
        h = ent[0]
        return lambda inst: inst.then_inc(h, by)

    def cur_tok(self, eng):
        s = self.ESEM[eng]
        return (s, self.sems[s][1])

    def step(self, eng, fn):
        sname = self.ESEM[eng]
        ap = self.inc(sname)
        cnt = self.sems[sname][1]
        h = self.sems[sname][0]

        def wrapped(e, fn=fn, ap=ap, h=h, cnt=cnt):
            inst = fn(e)
            ap(inst)
            self.emitted_inc[id(inst)] = (h, cnt)
        self.ops[eng].append(wrapped)
        return (sname, cnt)

    def wait(self, eng, tok):
        if tok is None:
            return
        name, v = tok
        if v <= 0:
            return
        ent = self.sems[name]
        self.ops[eng].append(lambda e, h=ent[0], v=v: e.wait_ge(h, v))

    def wait_all(self, eng, prod_eng):
        self.wait(eng, self.cur_tok(prod_eng))

    def dma(self, eng, dst, src, sem):
        self.sem(sem)
        ap = self.inc(sem, 16)
        tok = (sem, self.sems[sem][1])
        self.ops[eng].append(lambda e, d=dst, s=src, ap=ap: ap(e.dma_start(out=d, in_=s)))
        return tok

    def emit_all(self, block):
        for eng in self.ENGS:
            def body(e, eng=eng):
                h = HazardGuard(e, self, eng) if eng in ("vector", "scalar") else e
                for fn in self.ops[eng]:
                    fn(h)
            getattr(block, eng)(body)


def _ap_range(a):
    t = getattr(a, "tensor", None)
    if t is None:
        return None, None
    rng = getattr(t, "manual_sbuf_range", None)
    name = getattr(t, "name", None)
    if rng is None:
        return name, None
    esz = mybir.dt.size(t.dtype)
    try:
        aps = list(a.ap)
        ext = 1
        for st, ct in aps[1:]:
            ext += abs(int(st)) * (int(ct) - 1)
        off = int(a.offset)
        row = int(aps[0][0]) if aps else ext
        if row > 0:
            off = off % row
        lo = rng[0] + off * esz
        hi = lo + ext * esz
    except Exception:
        lo, hi = rng[0], rng[1]
    return name, (lo, hi)


_AFC = mybir.ActivationFunctionType.Copy

_GUARDED_OPS = {
    "tensor_copy", "tensor_tensor", "tensor_add", "tensor_sub", "tensor_mul",
    "tensor_max", "tensor_scalar", "tensor_single_scalar", "tensor_scalar_add",
    "tensor_scalar_sub", "tensor_scalar_mul", "tensor_scalar_max",
    "tensor_scalar_min", "tensor_reduce", "reduce_max", "reduce_sum",
    "scalar_tensor_tensor", "reciprocal", "memset", "memzero", "activation",
    "copy", "mul", "add", "square", "sqrt", "sign",
}


class HazardGuard:

    def __init__(self, e, rec, eng):
        self._e = e
        self._rec = rec
        self._eng = eng
        self._zsem = rec.sem("z_" + eng)
        self._last = None

    def wait_ge(self, *a, **k):
        return self._e.wait_ge(*a, **k)

    def nop(self, *a, **k):
        return self._e.nop(*a, **k)

    def dma_start(self, *a, **k):
        return self._e.dma_start(*a, **k)

    def __getattr__(self, name):
        fn = getattr(self._e, name)
        if name not in _GUARDED_OPS or not callable(fn):
            return fn

        def call(*args, **kw):
            outs, ins = [], []
            aps = []
            for i, a in enumerate(args):
                if hasattr(a, "tensor"):
                    (outs if i == 0 else ins).append(_ap_range(a))
            for kname, a in kw.items():
                if hasattr(a, "tensor"):
                    (outs if kname == "accum_out" else ins).append(_ap_range(a))
            if self._last is not None:
                li, lr, lw, ltok = self._last
                if self._conf(lw, ins) or self._conf(lw, outs) or self._conf(lr, outs):
                    scr = getattr(self._rec, "guard_scratch", {}).get(self._eng)
                    if scr is not None:
                        sa, sb = scr
                        if self._eng == "vector":
                            self._e.tensor_copy(sa[:, :], sb[:, :])
                            self._e.tensor_copy(sa[:, :], sb[:, :])
                            self._e.tensor_copy(sa[:, :], sb[:, :])
                        else:
                            self._e.activation(sa[:, :], sb[:, :], _AFC)
                            self._e.activation(sa[:, :], sb[:, :], _AFC)
                            self._e.activation(sa[:, :], sb[:, :], _AFC)
                    else:
                        prev = self._rec.emitted_inc.get(id(li))
                        if prev is not None:
                            self._e.wait_ge(prev[0], prev[1])
                        else:
                            ent = self._rec.sems[self._zsem]
                            if ltok is None:
                                ent[1] += 1
                                li.then_inc(ent[0], 1)
                                ltok = ent[1]
                                self._rec.emitted_inc[id(li)] = (ent[0], ent[1])
                                self._last = (li, lr, lw, ltok)
                            self._e.wait_ge(ent[0], ltok)
            inst = fn(*args, **kw)
            self._last = (inst, ins, outs, None)
            return inst
        return call

    @staticmethod
    def _conf(aset, bset):
        for an, ar in aset:
            for bn, br in bset:
                if ar is None or br is None:
                    if an is not None and an == bn:
                        return True
                elif ar[0] < br[1] and br[0] < ar[1]:
                    return True
        return False


def build_program(taps=()):
    nc = bass.Bass("TRN2", target_bir_lowering=False, debug=False,
                   detect_race_conditions=False)
    taps = set(taps)

    def din(name, shape, dt=F32):
        return nc.dram_tensor(name, shape, dt, kind="ExternalInput").ap()

    x_in = din("x_loc", [NTOK, C])
    tab_in = {("q", 0): din("cosq", [C, NTOK], BF16),
              ("q", 1): din("sinq", [C, NTOK], BF16),
              ("k", 0): din("cosk", [C, NTOK], BF16),
              ("k", 1): din("sink", [C, NTOK], BF16)}
    causal_in = din("causal", [L, L], BF16)
    identf_in = din("ident_f", [128, 128], F32)
    identb_in = din("ident_b", [128, 128], BF16)
    gnw_in = din("gnw_row", [1, C], BF16)
    gnb_in = din("gnb_row", [1, C], BF16)
    b1fm_in = din("b1_fm", [128, IT])
    b2_in = din("b2_row", [1, C], BF16)
    lng_in = din("lng_fm", [128, IT])
    lnb_in = din("lnb_fm", [128, IT])
    cvec_in = din("cvec", [128, NCORES * H])
    wqkv_in = din("wqkv_s", [C // 8, 3 * C])
    wg_in = din("wg_s", [C // 8, C])
    wproj_in = din("wproj_s", [C // 8, C])
    w1t_in = din("w1t_s", [C // 8, INNER])
    w2t_in = din("w2t_s", [INNER // 8, C])

    out_ap = nc.dram_tensor("out", [NTOK, C], F32, kind="ExternalOutput").ap()
    tap_aps = {}

    def mktap(tname, shape, dt):
        if tname in taps:
            tap_aps[tname] = nc.dram_tensor("tap_" + tname, shape, dt,
                                            kind="ExternalOutput").ap()
    mktap("xn_tm", [128, 8 * C], F32)
    mktap("xraw", [128, 8 * C], F32)
    mktap("sstat1", [128, 64], F32)
    mktap("statsA", [128, 64], F32)
    mktap("q2", [128, H * NTOK], BF16)
    mktap("k2", [128, H * NTOK], BF16)
    mktap("v", [128, 8 * C], BF16)
    mktap("y", [128, H * NTOK], BF16)
    mktap("S", [128, H * D], F32)
    mktap("h", [128, 2 * IT * 512], BF16)
    mktap("n2", [128, 2 * IT * 512], BF16)
    mktap("g", [128, 8 * C], BF16)
    mktap("scl4", [128, 4], F32)
    mktap("ybuf0", [128, NTOK], BF16)
    mktap("gn0", [128, 16], F32)
    mktap("sinc2", [128, H * D], BF16)
    mktap("sproj", [128, H * NTOK], BF16)
    mktap("stats", [128, 64], F32)
    mktap("statB", [128, 64], F32)
    mktap("sinc", [128, H * D], BF16)

    def dint(name, shape, dt):
        return nc.dram_tensor(name, shape, dt).ap()

    g_wqkv_i = dint("g_wqkv_i", [C // 8, 3 * C], BF16)
    g_wqkv = dint("g_wqkv", [C, 3 * C], BF16)
    g_wg_i = dint("g_wg_i", [C // 8, C], BF16)
    g_wg = dint("g_wg", [C, C], BF16)
    g_wp_i = dint("g_wp_i", [C // 8, C], BF16)
    g_wp = dint("g_wp", [C, C], BF16)
    g_q1_i = dint("g_q1_i", [C // 8, INNER], BF16)
    g_q1 = dint("g_q1", [C, INNER], BF16)
    g_q2_i = dint("g_q2_i", [INNER // 8, C], BF16)
    g_q2 = dint("g_q2", [INNER, C], BF16)
    g_ws_i = dint("g_ws_i", [1, 16], F32)
    g_ws = dint("g_ws", [1, 16], F32)
    g_sl_i = dint("g_sl_i", [128, H * D], F32)
    g_sl = dint("g_sl", [NCORES * 128, H * D], F32)
    g_ysp = dint("g_ysp", [128, H * NTOK], BF16)
    g_qsp = dint("g_qsp", [128, H * NTOK], BF16)
    g_vsp = dint("g_vsp", [128, 8 * C], BF16)
    g_xnsp = dint("g_xnsp", [128, KC * NTOK], BF16)
    g_n1sp = dint("g_n1sp", [128, KC * NTOK], BF16)

    BASE = nc.sbuf_base + 4096
    TOP = nc.SBUF_PARTITION_SIZE_BYTES
    _nt = [0]

    def at(off_kb, shape, dt, name="m"):
        _nt[0] += 1
        esz = mybir.dt.size(dt)
        off = BASE + int(round(off_kb * 1024))
        off = (off + 31) // 32 * 32
        free = int(np.prod(shape[1:])) * esz
        assert off + free <= TOP, f"{name}: {off}+{free} > {TOP}"
        return nc.alloc_sbuf_tensor_at(f"{name}{_nt[0]}", shape, dt, offset=off)

    stats = at(0, [128, 64], F32, "stats")
    sstat = at(0.25, [128, 64], F32, "sstat")
    stat2 = at(0.5, [128, 64], F32, "stat2")
    statB = at(0.75, [128, 64], F32, "statB")
    identf = at(1.0, [128, 128], F32, "idf")
    identb = at(1.55, [128, 128], BF16, "idb")
    causal = at(1.85, [L, L], BF16, "caus")
    cvec = at(2.15, [128, NCORES * H], F32, "cvec")
    lng_fm = at(2.7, [128, IT], F32, "lng")
    lnb_fm = at(3.0, [128, IT], F32, "lnb")
    b1_fm = at(3.3, [128, IT], F32, "b1f")
    onesb_c = at(3.6, [128, 1], BF16, "o1")
    onesf_c = at(3.65, [128, 1], F32, "o1f")
    onesb_r = at(3.7, [1, 128], BF16, "o2")
    onesf_r = at(4.0, [1, 128], F32, "o3")
    sw_b = at(4.6, [128, 2], F32, "swb")
    wsum = at(4.7, [1, 16], F32, "wsum")
    rowf = at(4.8, [1, 128], F32, "rowf")
    rowb = at(5.4, [1, 128], BF16, "rowb")
    gnw_r = at(5.7, [1, C], BF16, "gnwr")
    gnb_r = at(9.8, [1, C], BF16, "gnbr")
    b2_r = at(13.9, [1, C], BF16, "b2r")
    S_t = at(18.0, [128, H * D], F32, "S")
    Sbf = at(26.1, [128, H * D], BF16, "Sbf")
    c1rows = at(30.2, [1, 8 * 128], F32, "c1r")
    svrows = at(34.3, [1, 8 * 128], BF16, "svr")
    sv2rows = at(36.4, [1, 8 * 128], BF16, "sv2")
    rmax = at(38.5, [128, 512], F32, "rmax")
    rmin = at(40.5, [128, 512], F32, "rmin")
    ybuf = at(42.5, [128, NTOK], BF16, "ybuf")
    scl4 = at(47.0, [128, 4], F32, "scl4")
    pre4 = at(47.1, [128, 128], F32, "pre4")
    scrv_a = at(47.6, [128, 32], F32, "scrva")
    scrv_b = at(47.73, [128, 32], F32, "scrvb")
    scrs_a = at(46.8, [128, 16], F32, "scrsa")
    scrs_b = at(46.9, [128, 16], F32, "scrsb")
    gnmu = at(44.5, [128, 16], F32, "gnmu")
    gnrs = at(44.6, [128, 16], F32, "gnrs")
    gnmp = at(44.7, [128, 16], F32, "gnmp")
    t4buf = at(44.8, [128, 512], F32, "t4buf")

    R0 = 48.0

    wbuf = [at(R0 + 32 * i, [128, INNER], F32, f"wb{i}") for i in range(2)]
    qbuf = at(R0 + 64, [128, INNER], BF16, "qb")
    qscr = at(R0 + 80, [128, C], BF16, "qs")

    x_tm = at(R0, [128, 8 * C], F32, "x")
    xn_tm = at(R0 + 64, [128, 8 * C], F32, "xn")
    sqd = at(R0 + 128.5, [128, C], BF16, "sqd")
    n1_tm = at(R0, [128, 8 * C], BF16, "n1tm")
    n1i_f = at(R0 + 32, [128, C], F32, "n1ir")
    xn_fm = at(R0 + 32, [128, KC * NTOK], BF16, "xnfm")
    n1_fm = at(R0 + 64, [128, KC * NTOK], BF16, "n1fm")

    q2_a = at(R0, [128, H * NTOK], BF16, "q2")
    k2_a = at(R0 + 64, [128, H * NTOK], BF16, "k2")
    wsl = [at(R0 + 96 + 16 * i, [128, KC * 512], BF16, f"wsl{i}") for i in range(2)]
    tabr = at(R0 + 128, [128, 2 * 1024], BF16, "tab")
    v_r = [at(R0 + 132 + 4 * i, [128, C], BF16, f"vr{i}") for i in range(2)]
    y_r = [at(R0 + 140 + 4 * i, [128, H * L], BF16, f"yr{i}") for i in range(2)]
    smr = [at(R0 + 148 + 0.25 * i, [128, L], BF16, f"smr{i}") for i in range(2)]
    kdr = [at(R0 + 148.5 + 0.25 * i, [128, L], BF16, f"kdr{i}") for i in range(2)]
    fr1 = at(R0 + 152, [128, 512], F32, "fr1")
    fr1b = at(R0 + 132, [128, 512], F32, "fr1b")
    fr2b = at(R0 + 134, [128, 512], F32, "fr2b")
    fr2 = at(45.0, [128, 512], F32, "fr2")

    harena = at(R0, [128, IT * 512], BF16, "h")
    n1r = at(R0 + 64, [128, KC * NTOK], BF16, "n1r")
    q1b = [at(R0 + 96, [128, KC * 128], BF16, "q1b"),
           at(26.1, [128, KC * 128], BF16, "q1c")]
    w2b = [at(R0 + 100 + 4 * i, [128, 4 * 512], BF16, f"w2b{i}") for i in range(2)]
    fri = at(R0 + 108, [128, 512], I32, "fri")
    gbr = [at(R0 + 110 + i, [128, 512], BF16, f"gbr{i}") for i in range(2)]
    sqrr = [at(R0 + 144 + i, [128, 512], BF16, f"sqr{i}") for i in range(2)]
    C1R = at(R0 + 146, [128, 512], F32, "C1R")
    R1 = at(R0 + 148, [128, 512], F32, "R1")
    M1R = at(R0 + 150, [128, 512], F32, "M1R")
    C2R = at(43.0, [128, 512], F32, "C2R")
    g_tm = at(R0 + 112, [128, 8 * C], BF16, "g")

    sproj = at(R0, [128, H * NTOK], BF16, "sp")
    xn_fm4 = at(R0 + 32, [128, KC * NTOK], BF16, "xnfm4")
    wsl4 = at(R0 + 64, [128, KC * 512], BF16, "wsl4")
    sincA = at(R0 + 80, [128, H * D], F32, "siA")
    tfin = at(R0 + 80, [128, 1024], F32, "tfin")
    sincB = at(R0 + 88, [128, H * D], F32, "siB")
    sring = [at(R0 + 96 + 4 * i, [128, H * D], BF16, f"sg{i}") for i in range(2)]
    sincbf = at(R0 + 104, [128, H * D], BF16, "sibf")
    y4r = [at(R0 + 80 + 2 * i, [128, NTOK], BF16, f"y4{i}") for i in range(2)]
    q4r = [at(R0 + 84 + 2 * i, [128, NTOK], BF16, f"q4{i}") for i in range(2)]
    gsil = [at(R0 + 88 + 2 * i, [128, NTOK], BF16, f"gs{i}") for i in range(2)]
    qdr4 = [at(R0 + 92 + 0.25 * i, [128, L], BF16, f"qd4{i}") for i in range(2)]
    ytot = at(R0 + 92.5, [128, L], BF16, "yt")
    ysq = at(R0 + 92.75, [128, L], BF16, "ysq")
    t4a = at(R0 + 93, [128, L], F32, "t4a")
    t4b = at(R0 + 93.5, [128, L], F32, "t4b")
    xrg = [at(R0 + 94 + 2 * i, [128, 512], F32, f"xr{i}") for i in range(2)]
    org = [at(R0 + 98 + 2 * i, [128, 512], F32, f"og{i}") for i in range(2)]

    pb = [nc.alloc_psum_tensor(f"pb{i}", [128, 512], F32) for i in range(8)]
    bkg = {i: None for i in range(8)}

    rec = Rec(nc)
    rec.guard_scratch = {"vector": (scrv_a, scrv_b), "scalar": (scrs_a, scrs_b)}
    RG = [list(range(NCORES))]

    def acq(i):
        rec.wait("tensor", bkg[i])
        return pb[i]

    def tap_dump(name, src_ap):
        if name in tap_aps:
            rec.wait_all("sync", "vector")
            rec.wait_all("sync", "scalar")
            rec.wait_all("sync", "tensor")
            rec.wait_all("sync", "gpsimd")
            t = rec.dma("sync", tap_aps[name][:, :], src_ap, "d_tap")
            for e in ("vector", "scalar", "tensor", "gpsimd"):
                rec.wait(e, t)

    def col_to_row(colap, rowslice, bf16=False):
        bank = acq(7)
        tok = rec.step("tensor", lambda t, colap=colap, bank=bank:
                       t.transpose(bank[0:1, 0:128], colap, identf[:, :]))
        rec.wait("vector", tok)
        ctok = rec.step("vector", lambda v, r=rowslice, bank=bank:
                        v.tensor_copy(r, bank[0:1, 0:128]))
        bkg[7] = ctok
        return ctok

    def rank1_to(dst_sb, row_ap, jj, tokdep=None):
        bank = acq(6)
        if tokdep is not None:
            rec.wait("tensor", tokdep)
        tok = rec.step("tensor", lambda t, bank=bank, row_ap=row_ap, jj=jj:
                       t.matmul(bank[:, jj * 128:(jj + 1) * 128], onesf_r[:, :],
                                row_ap, start=True, stop=True))
        rec.wait("vector", tok)
        ctok = rec.step("vector", lambda v, bank=bank, jj=jj, dst_sb=dst_sb:
                        v.tensor_copy(dst_sb[:, jj * 128:(jj + 1) * 128],
                                      bank[:, jj * 128:(jj + 1) * 128]))
        bkg[6] = ctok
        return ctok

    t_conv = [rec.dma("gpsimd", d[:, :], s[:, :], "d_conv")
              for d, s in [(g_wqkv_i, wqkv_in), (g_wg_i, wg_in), (g_wp_i, wproj_in)]]

    t_pers = [rec.dma("sync", d, s, "d_pers") for d, s in [
        (identf[:, :], identf_in[:, :]), (identb[:, :], identb_in[:, :]),
        (causal[:, :], causal_in[:, :]), (cvec[:, :], cvec_in[:, :]),
        (lng_fm[:, :], lng_in[:, :]), (lnb_fm[:, :], lnb_in[:, :]),
        (b1_fm[:, :], b1fm_in[:, :]), (gnw_r[:, :], gnw_in[:, :]),
        (gnb_r[:, :], gnb_in[:, :]), (b2_r[:, :], b2_in[:, :])]]
    TPERS = t_pers[-1]

    def v_init(v):
        v.memset(stats[:, :], 0.0)
        v.memset(sstat[:, :], 0.0)
        v.memset(stat2[:, :], 0.0)
        v.memset(statB[:, :], 0.0)
        v.memset(wsum[:, :], 0.0)
        v.memset(onesb_c[:, :], 1.0)
        v.memset(onesf_c[:, :], 1.0)
        v.memset(onesb_r[:, :], 1.0)
        v.memset(onesf_r[:, :], 1.0)
        v.memset(S_t[:, :], 0.0)
        return v.memset(Sbf[:, :], 0.0)
    T_VINIT = rec.step("vector", v_init)

    W_TILES = [(w1t_in, 2, INNER), (w2t_in, 8, C)]
    absq = 0
    abs_toks = []
    for src, nt, width in W_TILES:
        for i in range(nt):
            buf = wbuf[absq % 2]
            if absq >= 2:
                rec.wait("sync", abs_toks[absq - 2])
            tok = rec.dma("sync", buf[:, 0:width],
                          src[:, :].rearrange("(c p) n -> p c n", p=128)[:, i, :],
                          f"d_w{absq % 2}")
            rec.wait("vector", tok)
            col = 16 + absq
            abs_toks.append(rec.step(
                "vector", lambda v, buf=buf, width=width, col=col:
                v.tensor_reduce(sstat[:, col:col + 1], buf[:, 0:width],
                                AX.X, ALU.add, apply_absolute_value=True)))
            absq += 1

    def v_comb(v):
        v.tensor_add(sstat[:, 0:1], sstat[:, 16:17], sstat[:, 17:18])
        return v.tensor_reduce(sstat[:, 1:2], sstat[:, 18:26], AX.X, ALU.add)
    tok = rec.step("vector", v_comb)
    rec.wait("tensor", tok)
    rec.wait("tensor", TPERS)
    bank0 = pb[0]
    ttok = rec.step("tensor", lambda t: t.matmul(
        bank0[0:1, 0:2], onesf_c[:, :], sstat[:, 0:2], start=True, stop=True))
    rec.wait("vector", ttok)
    tok = rec.step("vector", lambda v: v.tensor_copy(wsum[0:1, 0:2], bank0[0:1, 0:2]))
    bkg[0] = tok
    rec.wait("scalar", tok)
    tok = rec.dma("scalar", g_ws_i[:, :], wsum[0:1, 0:16], "d_ws")
    rec.wait("gpsimd", tok)
    for t in t_conv:
        rec.wait("gpsimd", t)
    T_AR = rec.step("gpsimd", lambda g: g.collective_compute(
        "AllReduce", ALU.add, replica_groups=RG,
        ins=[g_ws_i.tensor.ap().opt()], outs=[g_ws.tensor.ap().opt()]))
    cc = {}
    for nm, gi, go in [("wqkv", g_wqkv_i, g_wqkv), ("wg", g_wg_i, g_wg),
                       ("wp", g_wp_i, g_wp)]:
        cc[nm] = rec.step("gpsimd", lambda g, gi=gi, go=go: g.collective_compute(
            "AllGather", ALU.bypass, replica_groups=RG,
            ins=[gi.tensor.ap().opt()], outs=[go.tensor.ap().opt()]))

    rec.wait("scalar", T_AR)
    tok = rec.dma("scalar", wsum[0:1, 0:16], g_ws[:, :], "d_ws2")
    rec.wait("vector", tok)

    def v_sw(v):
        v.tensor_scalar(wsum[0:1, 4:6], wsum[0:1, 0:2], 1.0 / (INNER * C), None, ALU.mult)
        v.tensor_scalar(wsum[0:1, 6:8], wsum[0:1, 4:6], 1e-5, None, ALU.max)
        return v.reciprocal(wsum[0:1, 8:10], wsum[0:1, 6:8])
    tok = rec.step("vector", v_sw)
    rec.wait("tensor", tok)
    bank1 = pb[1]
    ttok = rec.step("tensor", lambda t: t.matmul(
        bank1[:, 0:2], onesf_r[:, :], wsum[0:1, 8:10], start=True, stop=True))
    rec.wait("vector", ttok)
    T_SWB = rec.step("vector", lambda v: v.tensor_copy(sw_b[:, 0:2], bank1[:, 0:2]))
    bkg[1] = T_SWB

    stq = 0
    q_store = []
    vq_toks = []
    for wi, (src, nt, width) in enumerate(W_TILES):
        dsti = (g_q1_i, g_q2_i)[wi]
        for i in range(nt):
            buf = wbuf[stq % 2]
            if stq == 0:
                rec.wait("sync", abs_toks[-1])
            if stq >= 2:
                rec.wait("sync", vq_toks[stq - 2])
            tok = rec.dma("sync", buf[:, 0:width],
                          src[:, :].rearrange("(c p) n -> p c n", p=128)[:, i, :],
                          f"d_p2w{stq % 2}")
            rec.wait("vector", tok)
            if q_store:
                rec.wait("vector", q_store[-1])

            def v_q(v, buf=buf, width=width, wi=wi):
                last = None
                for sbi in range(width // C):
                    sl = slice(sbi * C, (sbi + 1) * C)
                    v.tensor_scalar(qbuf[:, sl], buf[:, sl], sw_b[:, wi:wi + 1],
                                    0.5, ALU.mult, ALU.is_gt)
                    v.tensor_scalar(qscr[:, 0:C], buf[:, sl], sw_b[:, wi:wi + 1],
                                    -0.5, ALU.mult, ALU.is_lt)
                    last = v.tensor_sub(qbuf[:, sl], qbuf[:, sl], qscr[:, 0:C])
                return last
            tok = rec.step("vector", v_q)
            vq_toks.append(tok)
            rec.wait("scalar", tok)
            q_store.append(rec.dma(
                "scalar", dsti[:, :].rearrange("(c p) n -> p c n", p=128)[:, i, :],
                qbuf[:, 0:width], "d_qst"))
            stq += 1
    for t in q_store:
        rec.wait("gpsimd", t)
    for nm, gi, go in [("q1", g_q1_i, g_q1), ("q2", g_q2_i, g_q2)]:
        cc[nm] = rec.step("gpsimd", lambda g, gi=gi, go=go: g.collective_compute(
            "AllGather", ALU.bypass, replica_groups=RG,
            ins=[gi.tensor.ap().opt()], outs=[go.tensor.ap().opt()]))

    rec.wait("sync", q_store[-1])
    rec.wait("sync", q_store[-2])
    tok_x = rec.dma("sync", x_tm[:, :].rearrange("p (c n) -> p c n", c=8),
                    x_in[:, :].rearrange("(c p) n -> p c n", p=128), "d_x")
    rec.wait("scalar", tok_x)
    rec.wait("scalar", TPERS)
    rec.wait("scalar", T_VINIT)
    rec.wait("vector", tok_x)

    def sc_ss(s):
        last = None
        for j in range(8):
            last = s.activation(sqd[:, :], x_tm[:, j * C:(j + 1) * C], AF.Square,
                                accum_out=sstat[:, 8 + j:9 + j])
        return last
    T_SS = rec.step("scalar", sc_ss)
    if "xraw" in tap_aps:
        rec.wait("sync", tok_x)
        t = rec.dma("sync", tap_aps["xraw"][:, :], x_tm[:, :], "d_tap")
        for e in ("vector", "scalar", "tensor"):
            rec.wait(e, t)
    rec.wait("vector", T_SS)

    def v_ln(v):
        for j in range(8):
            v.tensor_reduce(sstat[:, j:j + 1], x_tm[:, j * C:(j + 1) * C],
                            AX.X, ALU.add)
        v.tensor_scalar(stats[:, 0:8], sstat[:, 0:8], 1.0 / C, None, ALU.mult)
        v.tensor_scalar(sstat[:, 16:24], sstat[:, 8:16], 1.0 / C, None, ALU.mult)
        v.tensor_tensor(sstat[:, 24:32], stats[:, 0:8], stats[:, 0:8], ALU.mult)
        v.tensor_sub(sstat[:, 32:40], sstat[:, 16:24], sstat[:, 24:32])
        v.tensor_scalar(sstat[:, 40:48], sstat[:, 32:40], EPS, None, ALU.add)
        return v.reciprocal(stats[:, 16:24], sstat[:, 40:48])
    tok = rec.step("vector", v_ln)
    if "sstat1" in tap_aps:
        rec.wait("sync", tok)
        rec.wait("sync", T_SS)
        t = rec.dma("sync", tap_aps["sstat1"][:, :], sstat[:, :], "d_tap")
        for e in ("vector", "scalar", "tensor"):
            rec.wait(e, t)
    rec.wait("scalar", tok)

    def sc_rstd(s):
        s.activation(sstat[:, 48:56], stats[:, 16:24], AF.Ln)
        return s.activation(stats[:, 8:16], sstat[:, 48:56], AF.Exp, scale=0.5)
    tok = rec.step("scalar", sc_rstd)
    rec.wait("vector", tok)

    def v_xn(v):
        for j in range(8):
            v.tensor_tensor(sstat[:, 48 + j:49 + j], stats[:, j:j + 1],
                            stats[:, 8 + j:9 + j], ALU.mult)
            v.tensor_scalar(sstat[:, 48 + j:49 + j], sstat[:, 48 + j:49 + j],
                            -1.0, None, ALU.mult)
            v.tensor_scalar(xn_tm[:, j * C:(j + 1) * C], x_tm[:, j * C:(j + 1) * C],
                            stats[:, 8 + j:9 + j], sstat[:, 48 + j:49 + j],
                            ALU.mult, ALU.add)
        for j in range(8):
            v.tensor_reduce(stats[:, 24 + j:25 + j], xn_tm[:, j * C:(j + 1) * C],
                            AX.X, ALU.max, apply_absolute_value=True)
        v.tensor_tensor(sstat[:, 0:8], sstat[:, 32:40], stats[:, 16:24], ALU.mult)
        v.tensor_scalar(sstat[:, 0:8], sstat[:, 0:8], EPS, None, ALU.add)
        return v.reciprocal(sstat[:, 8:16], sstat[:, 0:8])
    tok = rec.step("vector", v_xn)
    tap_dump("statsA", stats[:, :])
    tap_dump("xn_tm", xn_tm[:, :])
    rec.wait("scalar", tok)

    def sc_rs1(s):
        s.activation(sstat[:, 16:24], sstat[:, 8:16], AF.Ln)
        return s.activation(stats[:, 32:40], sstat[:, 16:24], AF.Exp, scale=0.5)
    tok = rec.step("scalar", sc_rs1)
    rec.wait("vector", tok)

    def v_n1(v):
        v.tensor_tensor(sstat[:, 0:8], stats[:, 32:40], stats[:, 24:32], ALU.mult)
        v.tensor_scalar(sstat[:, 0:8], sstat[:, 0:8], 1e-5, None, ALU.max)
        v.reciprocal(sstat[:, 8:16], sstat[:, 0:8])
        v.tensor_scalar(stats[:, 40:48], sstat[:, 8:16], 127.0, None, ALU.mult)
        v.tensor_tensor(stats[:, 48:56], stats[:, 32:40], stats[:, 40:48], ALU.mult)
        last = None
        MAGIC = 12582912.0
        for j in range(8):
            v.tensor_scalar(xn_tm[:, j * C:(j + 1) * C] if False else n1i_f[:, :],
                            xn_tm[:, j * C:(j + 1) * C],
                            stats[:, 48 + j:49 + j], MAGIC, ALU.mult, ALU.add)
            last = v.tensor_scalar(n1_tm[:, j * C:(j + 1) * C], n1i_f[:, :],
                                   -MAGIC, None, ALU.add)
        return last
    T_N1 = rec.step("vector", v_n1)

    rec.wait("tensor", TPERS)
    rec.wait("tensor", T_N1)
    for src_tm, dst_fm, ident, isbf in [(xn_tm, xn_fm, identf, False),
                                        (n1_tm, n1_fm, identb, True)]:
        for j in range(8):
            for kc in range(KC):
                bkid = 4 + (j * KC + kc) % 4
                bank = acq(bkid)
                bout = bank.ap().bitcast(BF16_) if isbf else bank

                def t_tr(t, s=src_tm, j=j, kc=kc, bout=bout, ident=ident):
                    return t.transpose(
                        bout[:, 0:128],
                        s[:, j * C + kc * 128:j * C + kc * 128 + 128],
                        ident[:, :])
                tok = rec.step("tensor", t_tr)
                rec.wait("vector", tok)
                ctok = rec.step("vector", lambda v, d=dst_fm, j=j, kc=kc, bout=bout:
                                v.tensor_copy(
                                    d[:, kc * NTOK + j * 128:kc * NTOK + j * 128 + 128],
                                    bout[:, 0:128]))
                bkg[bkid] = ctok

    rec.wait_all("sync", "vector")
    T_N1SP = rec.dma("sync", g_n1sp[:, :], n1_fm[:, :], "d_sp")

    def v_sv(v):
        v.tensor_scalar(statB[:, 0:8], stats[:, 40:48], sw_b[:, 0:1], None, ALU.mult)
        return v.reciprocal(statB[:, 8:16], statB[:, 0:8])
    T_SV = rec.step("vector", v_sv)
    rec.wait("tensor", T_SV)
    for j in range(8):
        col_to_row(statB[:, 8 + j:9 + j], c1rows[0:1, j * 128:(j + 1) * 128])
        col_to_row(statB[:, j:j + 1], svrows[0:1, j * 128:(j + 1) * 128])

    rec.wait("sync", cc["wqkv"])
    rec.wait_all("tensor", "vector")
    wsl_tok = [None, None]
    wsl_free = [None, None]

    def load_wslice(sl_i, which):
        s = sl_i % 2
        rec.wait("sync", wsl_free[s])
        src = {"qkv": g_wqkv}[which]
        wsl_tok[s] = rec.dma("sync", wsl[s][:, :].rearrange("p (c n) -> p c n", c=KC),
                             src[:, sl_i * 512:(sl_i + 1) * 512]
                             .rearrange("(c p) n -> p c n", p=128), f"d_wsl{s}")

    qk_tiles = [("q", hh, t2) for hh in range(H) for t2 in range(2)] + \
               [("k", hh, t2) for hh in range(H) for t2 in range(2)]
    tabslot_free = [None] * 2
    load_wslice(0, "qkv")
    cur_slice = [0]
    for pidx in range(0, len(qk_tiles), 2):
        pair_ctx = []
        for idx in (pidx, pidx + 1):
            qk, hh, t2 = qk_tiles[idx]
            sl_i = hh // 4 + (0 if qk == "q" else 4)
            if sl_i > cur_slice[0]:
                wsl_free[sl_i % 2] = rec.cur_tok("tensor")
                load_wslice(sl_i, "qkv")
                cur_slice[0] = sl_i
            slot = idx % 2
            rec.wait("sync", tabslot_free[slot])
            rec.sem("d_tab")
            rec.inc("d_tab", 32)
            ttok = ("d_tab", rec.sems["d_tab"][1])
            h_tab = rec.sems["d_tab"][0]

            def sy_tab(sy, slot=slot, qk=qk, hh=hh, t2=t2, h_tab=h_tab):
                sy.dma_start(out=tabr[:, slot * 1024:slot * 1024 + 512],
                             in_=tab_in[(qk, 0)][hh * 128:(hh + 1) * 128,
                                                 t2 * 512:(t2 + 1) * 512]).then_inc(h_tab, 16)
                sy.dma_start(out=tabr[:, slot * 1024 + 512:slot * 1024 + 1024],
                             in_=tab_in[(qk, 1)][hh * 128:(hh + 1) * 128,
                                                 t2 * 512:(t2 + 1) * 512]).then_inc(h_tab, 16)
            rec.ops["sync"].append(sy_tab)

            bkid = idx % 2
            bank = acq(bkid)
            rec.wait("tensor", wsl_tok[sl_i % 2])

            def t_mm(t, buf=wsl[sl_i % 2], co=hh % 4, t2=t2, bank=bank):
                i = None
                for kc in range(KC):
                    i = t.matmul(bank[:, :],
                                 buf[:, kc * 512 + co * 128:kc * 512 + co * 128 + 128],
                                 xn_fm[:, kc * NTOK + t2 * 512:kc * NTOK + (t2 + 1) * 512],
                                 start=(kc == 0), stop=(kc == KC - 1))
                return i
            mtok = rec.step("tensor", t_mm)
            dst = q2_a if qk == "q" else k2_a
            pair_ctx.append((bank, dst, hh, t2, slot, mtok, ttok))

        for (_, _, _, _, _, mtok, ttok) in pair_ctx:
            rec.wait("vector", mtok)
            rec.wait("vector", ttok)

        def v_xpair(v, ctxs=tuple(pair_ctx)):
            frs = ((fr1, fr2), (fr1b, fr2b))
            oaps = []
            for (bank, dst, hh, t2, slot, _, _), (f1, f2) in zip(ctxs, frs):
                ct = tabr[:, slot * 1024:slot * 1024 + 512]
                st = tabr[:, slot * 1024 + 512:slot * 1024 + 1024]
                o = dst[:, hh * NTOK + t2 * 512:hh * NTOK + (t2 + 1) * 512]
                v.tensor_tensor(f2[:, :], bank[:, :], ct[:, :], ALU.mult)
                v.tensor_tensor(f1[0:64, :], bank[64:128, :], st[0:64, :], ALU.mult)
                v.tensor_tensor(f1[64:128, :], bank[0:64, :], st[64:128, :], ALU.mult)
                oaps.append((o, f1, f2))
            last = None
            for o, f1, f2 in oaps:
                v.tensor_sub(o[0:64, :], f2[0:64, :], f1[0:64, :])
                last = v.tensor_add(o[64:128, :], f2[64:128, :], f1[64:128, :])
            return last
        ctok = rec.step("vector", v_xpair)
        bkg[0] = ctok
        bkg[1] = ctok
        tabslot_free[0] = ctok
        tabslot_free[1] = ctok
    wsl_free[0] = rec.cur_tok("tensor")
    wsl_free[1] = rec.cur_tok("tensor")

    vsp = []
    for vs in range(4):
        sl_i = 8 + vs
        load_wslice(sl_i, "qkv")
        for tt in range(8):
            bkid = (vs * 8 + tt) % 2
            bank = acq(bkid)
            rec.wait("tensor", wsl_tok[sl_i % 2])

            def t_vmm(t, buf=wsl[sl_i % 2], tt=tt, bank=bank):
                i = None
                for kc in range(KC):
                    i = t.matmul(bank[:, :],
                                 xn_fm[:, kc * NTOK + tt * 128:kc * NTOK + tt * 128 + 128],
                                 buf[:, kc * 512:(kc + 1) * 512],
                                 start=(kc == 0), stop=(kc == KC - 1))
                return i
            mtok = rec.step("tensor", t_vmm)
            rec.wait("vector", mtok)
            if len(vsp) >= 2:
                rec.wait("vector", vsp[-2])
            vr = v_r[(vs * 8 + tt) % 2]
            ctok = rec.step("vector", lambda v, bank=bank, vr=vr:
                            v.tensor_copy(vr[:, 0:512], bank[:, :]))
            bkg[bkid] = ctok
            rec.wait("sync", ctok)
            vsp.append(rec.dma(
                "sync", g_vsp[:, tt * C + vs * 512:tt * C + (vs + 1) * 512],
                vr[:, 0:512], "d_vsp"))
        wsl_free[sl_i % 2] = rec.cur_tok("tensor")

    tap_dump("q2", q2_a[:, :])
    tap_dump("k2", k2_a[:, :])
    if "v" in tap_aps:
        for t in vsp:
            rec.wait("sync", t)
        rec.dma("sync", tap_aps["v"][:, :], g_vsp[:, :], "d_tap")

    ysp = {}
    vld = [None, None]
    te_chunk = {}
    ve_chunk = {}
    T_SBF = None
    for n in range(NCH):
        vr = v_r[n % 2]
        if n < 2:
            for t in vsp:
                rec.wait("sync", t)
        else:
            rec.wait("sync", ("c_te", te_chunk[n - 2]))
        vld[n % 2] = rec.dma("sync", vr[:, :], g_vsp[:, n * C:(n + 1) * C], "d_vld")
        if n > 0:
            rec.wait("vector", ("c_ve", ve_chunk[n - 1]))
            T_SBF = rec.step("vector", lambda v: v.tensor_copy(Sbf[:, :], S_t[:, :]))
        yslot = y_r[n % 2]
        if n >= 2:
            rec.wait("scalar", ysp[n - 2])
        for h in range(H):
            i = n * H + h
            q2c = q2_a[:, h * NTOK + n * 128:h * NTOK + n * 128 + 128]
            k2c = k2_a[:, h * NTOK + n * 128:h * NTOK + n * 128 + 128]
            sm = smr[i % 2]
            kd = kdr[i % 2]
            bank_s = acq(i % 2)
            stok = rec.step("tensor", lambda t, bank=bank_s, k2c=k2c, q2c=q2c:
                            t.matmul(bank[:, 0:128], k2c, q2c, start=True, stop=True))
            bank_k = acq(4 + i % 2)
            bkb = bank_k.ap().bitcast(BF16_)
            ktok = rec.step("tensor", lambda t, bkb=bkb, k2c=k2c:
                            t.transpose(bkb[:, 0:128], k2c, identb[:, :]))
            rec.wait("vector", stok)
            smtok = rec.step("vector", lambda v, sm=sm, bank=bank_s:
                             v.tensor_tensor(sm[:, :], bank[:, 0:128],
                                             causal[:, :], ALU.mult))
            bkg[i % 2] = smtok
            rec.wait("scalar", ktok)
            kdtok = rec.step("scalar", lambda s, kd=kd, bkb=bkb, h=h:
                             s.activation(kd[:, :], bkb[:, 0:128], AF.Copy,
                                          scale=GL[h]))
            bkg[4 + i % 2] = kdtok
            bank_y = acq(2 + i % 2)
            rec.wait("tensor", smtok)
            rec.wait("tensor", vld[n % 2])
            if n > 0:
                rec.wait("tensor", T_SBF)

            def t_y(t, bank=bank_y, vr=vr, h=h, sm=sm, q2c=q2c, n=n):
                i1 = t.matmul(bank[:, 0:128], vr[:, h * 128:(h + 1) * 128],
                              sm[:, :], start=True, stop=(n == 0))
                if n > 0:
                    i1 = t.matmul(bank[:, 0:128], Sbf[:, h * 128:(h + 1) * 128],
                                  q2c, start=False, stop=True)
                return i1
            ytok = rec.step("tensor", t_y)
            bank_u = acq(6 + i % 2)
            rec.wait("tensor", kdtok)
            utok = rec.step("tensor", lambda t, bank=bank_u, kd=kd, vr=vr, h=h:
                            t.matmul(bank[:, 0:128], kd[:, :],
                                     vr[:, h * 128:(h + 1) * 128],
                                     start=True, stop=True))
            rec.wait("scalar", ytok)
            yetok = rec.step("scalar", lambda s, yslot=yslot, h=h, bank=bank_y:
                             s.activation(yslot[:, h * 128:(h + 1) * 128],
                                          bank[:, 0:128], AF.Copy))
            bkg[2 + i % 2] = yetok
            rec.wait("vector", utok)
            sutok = rec.step("vector", lambda v, h=h, bank=bank_u:
                             v.scalar_tensor_tensor(
                                 S_t[:, h * 128:(h + 1) * 128],
                                 S_t[:, h * 128:(h + 1) * 128], GL[h],
                                 bank[:, 0:128], ALU.mult, ALU.add))
            bkg[6 + i % 2] = sutok
        te_chunk[n] = rec.sems["c_te"][1]
        ve_chunk[n] = rec.sems["c_ve"][1]
        rec.wait_all("sync", "scalar")
        ysp[n] = rec.dma(
            "sync",
            g_ysp[:, :].rearrange("p (hh tt) -> p hh tt", hh=H)[:, :, n * 128:(n + 1) * 128],
            yslot[:, :].rearrange("p (hh ll) -> p hh ll", hh=H), "d_ysp")

    tap_dump("S", S_t[:, :])
    tap_dump("stats", stats[:, :])
    if "y" in tap_aps:
        for n in range(NCH):
            rec.wait("sync", ysp[n])
        rec.dma("sync", tap_aps["y"][:, :], g_ysp[:, :], "d_tap")
    rec.wait_all("sync", "vector")
    T_QSP = rec.dma("sync", g_qsp[:, :], q2_a[:, :], "d_sp")
    T_XNSP = rec.dma("sync", g_xnsp[:, :], xn_fm[:, :], "d_sp")
    T_SLSP = rec.dma("sync", g_sl_i[:, :], S_t[:, :], "d_sp")
    rec.wait("gpsimd", T_SLSP)
    cc["sl"] = rec.step("gpsimd", lambda g: g.collective_compute(
        "AllGather", ALU.bypass, replica_groups=RG,
        ins=[g_sl_i.tensor.ap().opt()], outs=[g_sl.tensor.ap().opt()]))

    rec.wait("sync", T_N1SP)
    rec.wait("sync", T_QSP)
    rec.wait("sync", T_XNSP)
    for n in range(NCH):
        rec.wait("sync", ysp[n])
    rec.wait_all("sync", "tensor")
    rec.wait_all("sync", "vector")
    rec.wait_all("sync", "scalar")
    T_N1R = rec.dma("sync", n1r[:, :], g_n1sp[:, :], "d_n1r")
    rec.wait("sync", cc["q1"])
    rec.wait("scalar", T_QSP)
    rec.wait("scalar", T_XNSP)
    rec.wait("vector", T_QSP)
    rec.wait("vector", T_XNSP)

    for hf in range(2):
        ctk = None
        for jj in range(4):
            j = hf * 4 + jj
            ctk = rank1_to(C1R, c1rows[0:1, j * 128:(j + 1) * 128], jj)

        q1free = [None, None]
        ep_prev = []
        for it in range(IT):
            rec.wait("sync", q1free[it % 2])
            q1tok = rec.dma("sync", q1b[it % 2][:, :].rearrange("p (c n) -> p c n", c=KC),
                            g_q1[:, it * 128:(it + 1) * 128]
                            .rearrange("(c p) n -> p c n", p=128), "d_q1")
            bkid = it % 2
            bank = acq(bkid)
            rec.wait("tensor", q1tok)
            if it == 0:
                rec.wait("tensor", T_N1R)
                rec.wait("tensor", ctk)

            def t_mm1(t, qb=q1b[it % 2], hf=hf, bank=bank):
                i = None
                for kc in range(KC):
                    i = t.matmul(bank[:, :], qb[:, kc * 128:(kc + 1) * 128],
                                 n1r[:, kc * NTOK + hf * 512:kc * NTOK + (hf + 1) * 512],
                                 start=(kc == 0), stop=(kc == KC - 1))
                return i
            mtok = rec.step("tensor", t_mm1)
            q1free[it % 2] = rec.cur_tok("tensor")
            rec.wait("vector", mtok)
            if ep_prev:
                rec.wait("vector", ep_prev[-1])
            vtok = rec.step("vector", lambda v, bank=bank:
                            v.tensor_tensor(fr1[:, :], bank[:, :], C1R[:, :], ALU.mult))
            bkg[bkid] = vtok
            rec.wait("scalar", vtok)
            atok = rec.step("scalar", lambda a, it=it:
                            a.activation(harena[:, it * 512:(it + 1) * 512],
                                         fr1[:, :], AF.Silu, bias=b1_fm[:, it:it + 1]))
            ep_prev.append(atok)

        gel = []
        for it in range(IT):
            r = gbr[it % 2]
            if it < 2:
                rec.wait("scalar", ep_prev[-1])
            else:
                rec.wait("scalar", gel[it - 2][1])
            gt = rec.step("scalar", lambda a, it=it, r=r:
                          a.activation(r[:, :], harena[:, it * 512:(it + 1) * 512],
                                       AF.Gelu))
            rec.wait("vector", gt)
            ct = rec.step("vector", lambda v, it=it, r=r:
                          v.tensor_copy(harena[:, it * 512:(it + 1) * 512], r[:, :]))
            gel.append((gt, ct))

        if "h" in tap_aps:
            rec.wait_all("sync", "vector")
            rec.wait_all("sync", "scalar")
            rec.wait_all("sync", "tensor")
            t = rec.dma("sync", tap_aps["h"][:, hf * IT * 512:(hf + 1) * IT * 512],
                        harena[:, :], "d_tap")
            for e in ("vector", "scalar", "tensor"):
                rec.wait(e, t)
        bank7 = acq(7)
        sq = []
        for it in range(IT):
            r = sqrr[it % 2]
            rec.wait("scalar", gel[it][1])
            if it >= 2:
                rec.wait("scalar", sq[it - 2][1])
            qt = rec.step("scalar", lambda a, it=it, r=r:
                          a.activation(r[:, :], harena[:, it * 512:(it + 1) * 512],
                                       AF.Square))
            rec.wait("tensor", qt)
            if it == 0:
                rec.wait("tensor", gel[-1][1])

            def t_st(t, it=it, r=r, bank7=bank7):
                i = None
                for sub in range(4):
                    t.matmul(bank7[:, sub:sub + 1],
                             harena[:, it * 512 + sub * 128:it * 512 + (sub + 1) * 128],
                             onesb_c[:, :], start=(it == 0 and sub == 0),
                             stop=False)
                    i = t.matmul(bank7[:, 8 + sub:9 + sub],
                                 r[:, sub * 128:(sub + 1) * 128], onesb_c[:, :],
                                 start=False,
                                 stop=(it == IT - 1 and sub == 3))
                return i
            sq.append((qt, rec.step("tensor", t_st)))
        rec.wait("vector", sq[-1][1])

        def v_hst(v, bank7=bank7):
            v.tensor_scalar(stat2[:, 0:4], bank7[:, 0:4], 1.0 / INNER, None, ALU.mult)
            v.tensor_scalar(stat2[:, 4:8], bank7[:, 8:12], 1.0 / INNER, None, ALU.mult)
            v.tensor_tensor(stat2[:, 8:12], stat2[:, 0:4], stat2[:, 0:4], ALU.mult)
            v.tensor_sub(stat2[:, 12:16], stat2[:, 4:8], stat2[:, 8:12])
            v.tensor_scalar(stat2[:, 16:20], stat2[:, 12:16], EPS, None, ALU.add)
            return v.reciprocal(stat2[:, 20:24], stat2[:, 16:20])
        vt = rec.step("vector", v_hst)
        bkg[7] = vt
        rec.wait("scalar", vt)
        at2 = rec.step("scalar", lambda a: (
            a.activation(stat2[:, 24:28], stat2[:, 20:24], AF.Ln),
            a.activation(stat2[:, 28:32], stat2[:, 24:28], AF.Exp, scale=0.5))[1])
        rec.wait("vector", at2)
        vt = rec.step("vector", lambda v: v.tensor_tensor(
            stat2[:, 32:36], stat2[:, 0:4], stat2[:, 28:32], ALU.mult))
        rec.wait("tensor", vt)
        for jj in range(4):
            col_to_row(stat2[:, 28 + jj:29 + jj], rowf[0:1, :])
            rank1_to(R1, rowf[0:1, :], jj, tokdep=rec.cur_tok("vector"))
            col_to_row(stat2[:, 32 + jj:33 + jj], rowf[0:1, :])
            rank1_to(M1R, rowf[0:1, :], jj, tokdep=rec.cur_tok("vector"))

        T_RM = rec.step("vector", lambda v: (v.memset(rmax[:, :], -1e30),
                                             v.memset(rmin[:, :], 1e30))[1])
        hln = []
        for it in range(IT):
            def v_hln(v, it=it):
                v.tensor_tensor(fr1[:, :], harena[:, it * 512:(it + 1) * 512],
                                R1[:, :], ALU.mult)
                v.tensor_sub(fr2[:, :], fr1[:, :], M1R[:, :])
                v.tensor_scalar(fr1[:, :], fr2[:, :], lng_fm[:, it:it + 1],
                                lnb_fm[:, it:it + 1], ALU.mult, ALU.add)
                v.tensor_copy(harena[:, it * 512:(it + 1) * 512], fr1[:, :])
                v.tensor_max(rmax[:, :], rmax[:, :], fr1[:, :])
                return v.tensor_tensor(rmin[:, :], rmin[:, :], fr1[:, :], ALU.min)
            hln.append(rec.step("vector", v_hln))
        vt0 = rec.step("vector", lambda v: (
            v.tensor_scalar(fr1[:, :], rmin[:, :], -1.0, None, ALU.mult),
            v.tensor_max(fr2[:, :], rmax[:, :], fr1[:, :]))[1])
        rec.wait("tensor", vt0)
        for jj in range(4):
            bank = acq(5)
            tok = rec.step("tensor", lambda t, jj=jj, bank=bank:
                           t.transpose(bank[:, 0:128],
                                       fr2[:, jj * 128:(jj + 1) * 128], identf[:, :]))
            rec.wait("vector", tok)
            ct = rec.step("vector", lambda v, jj=jj, bank=bank:
                          v.tensor_reduce(stat2[:, 36 + jj:37 + jj], bank[:, 0:128],
                                          AX.X, ALU.max))
            bkg[5] = ct

        bank7 = acq(7)
        sq2 = []
        for it in range(IT):
            r = sqrr[it % 2]
            if it >= 2:
                rec.wait("scalar", sq2[it - 2][1])
            rec.wait("scalar", hln[it] if it != IT - 1 else hln[-1])
            qt = rec.step("scalar", lambda a, it=it, r=r:
                          a.activation(r[:, :], harena[:, it * 512:(it + 1) * 512],
                                       AF.Square))
            rec.wait("tensor", qt)

            def t_st2(t, it=it, r=r, bank7=bank7):
                i = None
                for sub in range(4):
                    i = t.matmul(bank7[:, 16 + sub:17 + sub],
                                 r[:, sub * 128:(sub + 1) * 128], onesb_c[:, :],
                                 start=(it == 0 and sub == 0),
                                 stop=(it == IT - 1 and sub == 3))
                return i
            sq2.append((qt, rec.step("tensor", t_st2)))
        rec.wait("vector", sq2[-1][1])

        def v_rms2(v, bank7=bank7):
            v.tensor_scalar(stat2[:, 40:44], bank7[:, 16:20], 1.0 / INNER, None, ALU.mult)
            v.tensor_scalar(stat2[:, 44:48], stat2[:, 40:44], EPS, None, ALU.add)
            return v.reciprocal(stat2[:, 48:52], stat2[:, 44:48])
        vt = rec.step("vector", v_rms2)
        bkg[7] = vt
        rec.wait("scalar", vt)
        at3 = rec.step("scalar", lambda a: (
            a.activation(stat2[:, 52:56], stat2[:, 48:52], AF.Ln),
            a.activation(stat2[:, 56:60], stat2[:, 52:56], AF.Exp, scale=0.5))[1])
        rec.wait("vector", at3)

        def v_c2(v, hf=hf):
            v.tensor_tensor(statB[:, 16:20], stat2[:, 56:60], stat2[:, 36:40], ALU.mult)
            v.tensor_scalar(statB[:, 16:20], statB[:, 16:20], 1e-5, None, ALU.max)
            v.reciprocal(statB[:, 20:24], statB[:, 16:20])
            v.tensor_scalar(statB[:, 24:28], statB[:, 20:24], 127.0, None, ALU.mult)
            v.tensor_tensor(statB[:, 28:32], stat2[:, 56:60], statB[:, 24:28], ALU.mult)
            return v.tensor_scalar(statB[:, 32:36], statB[:, 24:28], sw_b[:, 1:2], None, ALU.mult)
        vtA = rec.step("vector", v_c2)
        rec.wait("scalar", vtA)
        stA = rec.step("scalar", lambda a: a.nop())
        rec.wait("vector", stA)
        vt = rec.step("vector", lambda v: v.reciprocal(scl4[:, 0:4], statB[:, 32:36]))
        rec.wait("scalar", vt)
        stB = rec.step("scalar", lambda a: a.nop())
        rec.wait("vector", stB)
        vt2 = vt
        if hf == 0:
            tap_dump("statB", statB[:, :])
            tap_dump("scl4", scl4[:, :])
        rec.wait("tensor", vt)
        for jj in range(4):
            j = hf * 4 + jj
            col_to_row(statB[:, 28 + jj:29 + jj], rowf[0:1, :])
            rank1_to(C2R, rowf[0:1, :], jj, tokdep=rec.cur_tok("vector"))
            col_to_row(statB[:, 32 + jj:33 + jj], sv2rows[0:1, j * 128:(j + 1) * 128])

        n2t = []
        for it in range(IT):
            def v_n2(v, it=it):
                MAGIC = 12582912.0
                v.tensor_tensor(fr1[:, :], harena[:, it * 512:(it + 1) * 512],
                                C2R[:, :], ALU.mult)
                v.tensor_scalar(fr2[:, :], fr1[:, :], MAGIC, None, ALU.add)
                return v.tensor_scalar(harena[:, it * 512:(it + 1) * 512],
                                       fr2[:, :], -MAGIC, None, ALU.add)
            n2t.append(rec.step("vector", v_n2))
        if "n2" in tap_aps:
            rec.wait_all("sync", "vector")
            rec.wait_all("sync", "tensor")
            t = rec.dma("sync", tap_aps["n2"][:, hf * IT * 512:(hf + 1) * IT * 512],
                        harena[:, :], "d_tap")
            for e in ("vector", "scalar", "tensor"):
                rec.wait(e, t)

        rec.wait("sync", cc["q2"])
        w2tok = [None, None]
        w2free = [None, None]
        for cs in range(4):
            banks4 = [acq(tcj) for tcj in range(4)]
            rec.wait("tensor", n2t[-1])
            if cs > 0:
                w2free[0] = rec.cur_tok("tensor")
                w2free[1] = rec.cur_tok("tensor")
            for ikb in range(16):
                s = ikb % 2
                rec.wait("sync", w2free[s])
                w2tok[s] = rec.dma(
                    "sync", w2b[s][:, :].rearrange("p (c n) -> p c n", c=4),
                    g_q2[ikb * 512:(ikb + 1) * 512, cs * 512:(cs + 1) * 512]
                    .rearrange("(c p) n -> p c n", p=128), f"d_w2{s}")
                rec.wait("tensor", w2tok[s])

                def t_mm2(t, s=s, ikb=ikb, banks4=banks4):
                    i = None
                    for ikc in range(4):
                        it = ikb * 4 + ikc
                        for tcj in range(4):
                            i = t.matmul(
                                banks4[tcj][:, :],
                                harena[:, it * 512 + tcj * 128:it * 512 + (tcj + 1) * 128],
                                w2b[s][:, ikc * 512:(ikc + 1) * 512],
                                start=(it == 0), stop=False)
                    return i
                rec.step("tensor", t_mm2)
                w2free[s] = rec.cur_tok("tensor")
            for tcj in range(4):
                j = hf * 4 + tcj
                bank = banks4[tcj]
                if cs == 0 and tcj == 0:
                    rec.wait("tensor", rec.cur_tok("vector"))
                btok = rec.step("tensor", lambda t, bank=bank, j=j, cs=cs:
                                t.matmul(bank[:, :],
                                         sv2rows[0:1, j * 128:(j + 1) * 128],
                                         b2_r[0:1, cs * 512:(cs + 1) * 512],
                                         start=False, stop=True))
                rec.wait("vector", btok)
                rec.wait("vector", vt2)
                if cs > 0 or tcj > 0:
                    rec.wait("vector", rec.cur_tok("scalar"))
                vtok2 = rec.step("vector", lambda v, bank=bank, tcj=tcj:
                                 v.tensor_scalar(fr1[:, :], bank[:, :],
                                                 scl4[:, tcj:tcj + 1], None, ALU.mult))
                bkg[tcj] = vtok2
                rec.wait("scalar", vtok2)
                etok = rec.step("scalar", lambda a, j=j, cs=cs:
                                a.activation(
                                    g_tm[:, j * C + cs * 512:j * C + (cs + 1) * 512],
                                    fr1[:, :], AF.Silu))
    tap_dump("g", g_tm[:, :])

    TE_S3 = rec.cur_tok("tensor")
    AC_S3 = rec.cur_tok("scalar")
    VE_S3 = rec.cur_tok("vector")
    rec.wait("sync", T_XNSP)
    rec.wait("sync", TE_S3)
    rec.wait("sync", AC_S3)
    rec.wait("sync", VE_S3)
    T_XNR = rec.dma("sync", xn_fm4[:, :], g_xnsp[:, :], "d_xnr")
    rec.wait("gpsimd", cc["sl"])
    rec.wait("gpsimd", TE_S3)
    rec.wait("gpsimd", AC_S3)
    rec.wait("vector", TE_S3)
    rec.wait("vector", AC_S3)
    rec.wait("scalar", TE_S3)
    rec.wait("scalar", VE_S3)
    T_SI0 = rec.step("vector", lambda v: v.memset(sincA[:, :], 0.0))
    sl_toks = []
    prev = T_SI0
    srcbuf, dstbuf = sincA, sincB
    for pp in range(NCORES):
        sslot = sring[pp % 2]
        if pp >= 2:
            rec.wait("gpsimd", sl_toks[pp - 2][1])
        ltok = rec.dma("gpsimd", sslot[:, :], g_sl[pp * 128:(pp + 1) * 128, :],
                       "d_sl")
        rec.wait("vector", ltok)
        rec.wait("vector", prev)

        def v_si(v, pp=pp, sslot=sslot, src=srcbuf, dst=dstbuf):
            last = None
            for h in range(H):
                last = v.scalar_tensor_tensor(
                    dst[:, h * 128:(h + 1) * 128],
                    sslot[:, h * 128:(h + 1) * 128],
                    cvec[:, pp * H + h:pp * H + h + 1],
                    src[:, h * 128:(h + 1) * 128], ALU.mult, ALU.add)
            return last
        prev = rec.step("vector", v_si)
        sl_toks.append((ltok, prev))
        srcbuf, dstbuf = dstbuf, srcbuf
    T_SINC = rec.step("vector", lambda v, src=srcbuf: v.tensor_copy(
        sincbf[:, :], src[:, :]))
    if "sinc" in tap_aps:
        rec.wait_all("sync", "vector")
        rec.dma("sync", tap_aps["sinc"][:, :], sincbf[:, :], "d_tap")

    rec.wait("sync", cc["wg"])
    wg_tok = [None]
    wg_free = [None]
    y4tok = [None, None]
    q4tok = [None, None]
    gs_done = [None, None]
    sp_done = []
    ve_h = {}
    for h in range(H):
        if h % 4 == 0:
            hs = h // 4
            rec.wait("sync", wg_free[0])
            if h == 0:
                rec.wait_all("sync", "tensor")
            wg_tok[0] = rec.dma("sync", wsl4[:, :].rearrange("p (c n) -> p c n", c=KC),
                                g_wg[:, hs * 512:(hs + 1) * 512]
                                .rearrange("(c p) n -> p c n", p=128), "d_wg")
        if h >= 2:
            rec.wait("sync", ("c_ve", ve_h[h - 2]))
        else:
            rec.wait("sync", T_SINC)
        y4tok[h % 2] = rec.dma("sync", y4r[h % 2][:, :],
                               g_ysp[:, h * NTOK:(h + 1) * NTOK], "d_y4")
        q4tok[h % 2] = rec.dma("sync", q4r[h % 2][:, :],
                               g_qsp[:, h * NTOK:(h + 1) * NTOK], "d_q4")
        gslot = gsil[h % 2]
        for t2 in range(2):
            bank = acq(4 + t2)
            rec.wait("tensor", wg_tok[0])
            if h == 0 and t2 == 0:
                rec.wait("tensor", T_XNR)

            def t_wg(t, h=h, t2=t2, bank=bank):
                i = None
                for kc in range(KC):
                    i = t.matmul(bank[:, :],
                                 wsl4[:, kc * 512 + (h % 4) * 128:kc * 512 + (h % 4) * 128 + 128],
                                 xn_fm4[:, kc * NTOK + t2 * 512:kc * NTOK + (t2 + 1) * 512],
                                 start=(kc == 0), stop=(kc == KC - 1))
                return i
            wtok = rec.step("tensor", t_wg)
            rec.wait("scalar", wtok)
            if h < 2:
                rec.wait("scalar", T_SINC)
            if gs_done[h % 2] is not None:
                rec.wait("scalar", gs_done[h % 2])
            gtok = rec.step("scalar", lambda a, bank=bank, gslot=gslot, t2=t2:
                            a.activation(gslot[:, t2 * 512:(t2 + 1) * 512],
                                         bank[:, :], AF.Silu))
            bkg[4 + t2] = gtok
        if h % 4 == 3:
            wg_free[0] = rec.cur_tok("tensor")
        bank7 = acq(7)
        yt_toks = {}
        for n in range(NCH):
            rec.wait("vector", q4tok[h % 2])
            qd = qdr4[n % 2]
            gnl = float(np.float32(GAMMA[h] ** (n * L)))
            if n >= 2:
                rec.wait("vector", yt_toks[n - 2][1])
            qtok = rec.step("vector", lambda v, qd=qd, h=h, n=n, gnl=gnl:
                            v.tensor_scalar(qd[:, :],
                                            q4r[h % 2][:, n * 128:(n + 1) * 128],
                                            gnl, None, ALU.mult))
            bank6 = acq(5)
            rec.wait("tensor", qtok)
            rec.wait("tensor", T_SINC)
            itok = rec.step("tensor", lambda t, bank=bank6, h=h, qd=qd:
                            t.matmul(bank[:, 0:128],
                                     sincbf[:, h * 128:(h + 1) * 128], qd[:, :],
                                     start=True, stop=True))
            rec.wait("vector", itok)
            rec.wait("vector", y4tok[h % 2])
            yttok = rec.step("vector", lambda v, bank=bank6, h=h, n=n:
                             v.tensor_tensor(ybuf[:, n * 128:(n + 1) * 128],
                                             y4r[h % 2][:, n * 128:(n + 1) * 128],
                                             bank[:, 0:128], ALU.add))
            bkg[5] = yttok
            rec.wait("scalar", yttok)
            if n >= 1:
                rec.wait("scalar", yt_toks[n - 1][2])
            sqtok = rec.step("scalar", lambda a, n=n:
                             a.activation(ysq[:, :], ybuf[:, n * 128:(n + 1) * 128],
                                          AF.Square))
            rec.wait("tensor", sqtok)

            def t_gst(t, n=n, bank7=bank7):
                t.matmul(bank7[:, n:n + 1], ybuf[:, n * 128:(n + 1) * 128],
                         onesb_c[:, :], start=(n == 0), stop=False)
                return t.matmul(bank7[:, 8 + n:9 + n], ysq[:, :], onesb_c[:, :],
                                start=False, stop=(n == NCH - 1))
            sttok = rec.step("tensor", t_gst)
            yt_toks[n] = (qtok, yttok, sttok)
        rec.wait("vector", yt_toks[NCH - 1][2])

        def v_gn(v, bank7=bank7):
            last = None
            for g4 in range(4):
                sl2 = slice(2 * g4, 2 * g4 + 2)
                v.tensor_scalar(gnmu[:, sl2], bank7[:, 2 * g4:2 * g4 + 2],
                                1.0 / D, None, ALU.mult)
            for g4 in range(4):
                sl2 = slice(2 * g4, 2 * g4 + 2)
                v.tensor_scalar(gnrs[:, sl2], bank7[:, 8 + 2 * g4:10 + 2 * g4],
                                1.0 / D, None, ALU.mult)
            for g4 in range(4):
                sl2 = slice(2 * g4, 2 * g4 + 2)
                v.tensor_tensor(gnmp[:, sl2], gnmu[:, sl2], gnmu[:, sl2], ALU.mult)
            for g4 in range(4):
                sl2 = slice(2 * g4, 2 * g4 + 2)
                v.tensor_sub(gnrs[:, sl2], gnrs[:, sl2], gnmp[:, sl2])
            for g4 in range(4):
                sl2 = slice(2 * g4, 2 * g4 + 2)
                last = v.tensor_scalar(gnrs[:, sl2], gnrs[:, sl2], EPS, None, ALU.add)
            return last
        vt = rec.step("vector", v_gn)
        bkg[7] = vt
        if h == 1:
            tap_dump("ybuf0", ybuf[:, :])
            tap_dump("sinc2", sincbf[:, :])
        if h == 1:
            tap_dump("gn0", gnrs[:, 0:16])
        rec.wait("scalar", vt)
        at4 = rec.step("scalar", lambda a: (
            a.activation(gnmp[:, 0:8], gnrs[:, 0:8], AF.Ln),
            a.activation(gnrs[:, 0:8], gnmp[:, 0:8], AF.Exp, scale=-0.5))[1])
        rec.wait("vector", at4)
        vt = rec.step("vector", lambda v: (
            v.tensor_tensor(gnmp[:, 0:8], gnmu[:, 0:8], gnrs[:, 0:8], ALU.mult),
            v.tensor_scalar(gnmp[:, 0:8], gnmp[:, 0:8], -1.0, None, ALU.mult))[1])
        rec.wait("tensor", vt)
        for n in range(NCH):
            rtok = col_to_row(gnrs[:, n:n + 1], rowb[0:1, :], bf16=True)
            bankA = acq(2)
            rec.wait("tensor", rtok)
            atok = rec.step("tensor", lambda t, bank=bankA, h=h:
                            t.matmul(bank[:, 0:128],
                                     gnw_r[0:1, h * 128:(h + 1) * 128],
                                     rowb[0:1, :], start=True, stop=True))
            rtok = col_to_row(gnmp[:, n:n + 1], rowb[0:1, :], bf16=True)
            bankB = acq(3)
            rec.wait("tensor", rtok)

            def t_B(t, bank=bankB, h=h):
                t.matmul(bank[:, 0:128], gnw_r[0:1, h * 128:(h + 1) * 128],
                         rowb[0:1, :], start=True, stop=False)
                return t.matmul(bank[:, 0:128], gnb_r[0:1, h * 128:(h + 1) * 128],
                                onesb_r[:, :], start=False, stop=True)
            btok = rec.step("tensor", t_B)
            rec.wait("vector", atok)
            rec.wait("vector", btok)
            rec.wait("vector", ("c_ac", rec.sems["c_ac"][1]))

            def v_sp(v, bankA=bankA, bankB=bankB, h=h, n=n, gslot=gslot):
                for qq in range(4):
                    s32 = slice(qq * 32, qq * 32 + 32)
                    v.tensor_tensor(t4a[:, s32], ybuf[:, n * 128 + qq * 32:
                                                      n * 128 + qq * 32 + 32],
                                    bankA[:, s32], ALU.mult)
                for qq in range(4):
                    s32 = slice(qq * 32, qq * 32 + 32)
                    v.tensor_add(t4b[:, s32], t4a[:, s32], bankB[:, s32])
                last = None
                for qq in range(4):
                    s32 = slice(qq * 32, qq * 32 + 32)
                    last = v.tensor_tensor(
                        sproj[:, h * NTOK + n * 128 + qq * 32:
                              h * NTOK + n * 128 + qq * 32 + 32],
                        t4b[:, s32], gslot[:, n * 128 + qq * 32:
                                           n * 128 + qq * 32 + 32], ALU.mult)
                return last
            sptok = rec.step("vector", v_sp)
            bkg[2] = sptok
            bkg[3] = sptok
            sp_done.append(sptok)
        gs_done[h % 2] = sp_done[-1]
        ve_h[h] = rec.sems["c_ve"][1]
    tap_dump("sproj", sproj[:, :])

    rec.wait("sync", cc["wp"])
    nfin = 0
    fin_toks = {}
    for co in range(4):
        rec.wait("sync", rec.cur_tok("tensor"))
        wp_tok = rec.dma("sync", wsl4[:, :].rearrange("p (c n) -> p c n", c=KC),
                         g_wp[:, co * 512:(co + 1) * 512]
                         .rearrange("(c p) n -> p c n", p=128), "d_wp")
        for tt in range(8):
            i4 = co * 8 + tt
            xs = xrg[i4 % 2]
            osl = org[i4 % 2]
            if i4 >= 2:
                rec.wait("sync", fin_toks[i4 - 2])
            xtok = rec.dma("sync", xs[:, :],
                           x_in[:, :].rearrange("(c p) n -> p c n", p=128)
                           [:, tt, co * 512:(co + 1) * 512], "d_xr")
            bank = acq(i4 % 4)
            rec.wait("tensor", wp_tok)
            if i4 == 0:
                rec.wait("tensor", sp_done[-1])

            def t_pr(t, tt=tt, bank=bank):
                i = None
                for kc in range(KC):
                    i = t.matmul(bank[:, :],
                                 sproj[:, kc * NTOK + tt * 128:kc * NTOK + tt * 128 + 128],
                                 wsl4[:, kc * 512:(kc + 1) * 512],
                                 start=(kc == 0), stop=(kc == KC - 1))
                return i
            mtok = rec.step("tensor", t_pr)
            rec.wait("vector", mtok)
            rec.wait("vector", xtok)

            def v_fin(v, bank=bank, tt=tt, co=co, xs=xs, osl=osl, i4=i4):
                base = (i4 % 2) * 512
                for qq in range(4):
                    s = slice(qq * 128, qq * 128 + 128)
                    v.tensor_tensor(tfin[:, base + qq * 128:base + qq * 128 + 128],
                                    bank[:, s],
                                    g_tm[:, tt * C + co * 512 + qq * 128:
                                         tt * C + co * 512 + qq * 128 + 128],
                                    ALU.mult)
                last = None
                for qq in range(4):
                    s = slice(qq * 128, qq * 128 + 128)
                    last = v.tensor_add(osl[:, s],
                                        tfin[:, base + qq * 128:base + qq * 128 + 128],
                                        xs[:, s])
                return last
            vtok = rec.step("vector", v_fin)
            bkg[i4 % 4] = vtok
            rec.wait("sync", vtok)
            fin_toks[i4] = rec.dma(
                "sync", out_ap[:, :].rearrange("(c p) n -> p c n", p=128)
                [:, tt, co * 512:(co + 1) * 512], osl[:, :], "d_out")
            nfin += 1
    rec.wait("sync", ("d_out", 16 * nfin))
    if tap_aps:
        rec.wait("sync", ("d_tap", rec.sems["d_tap"][1]))

    with nc.Block() as block:
        rec.emit_all(block)
    assert nc.sbuf_base <= BASE, (
        f"const pool overran guard: sbuf_base={nc.sbuf_base} > BASE={BASE}")
    return nc


def _perm_idx():
    return np.concatenate([np.arange(0, 128, 2), np.arange(1, 128, 2)])


def _build_tables():
    d2 = np.arange(0, C, 2, dtype=np.float64)
    dim_scale = (d2 + 0.4 * C) / (1.4 * C)
    pos = np.arange(T, dtype=np.float64)
    scale = dim_scale[None, :] ** (pos[:, None] / 512.0)
    inv_freq = 1.0 / 10000.0 ** (d2 / C)
    sinusoid = pos[:, None] * inv_freq[None, :]
    sin_full = np.repeat(np.sin(sinusoid) * scale, 2, axis=-1)
    cos_full = np.repeat(np.cos(sinusoid) * scale, 2, axis=-1)
    pidx = _perm_idx()
    gperm = np.concatenate([hh * 128 + pidx for hh in range(H)])
    cosp = cos_full[:, gperm]
    sinp = sin_full[:, gperm]
    i_loc = (pos % L)
    qfac = GAMMA[None, :] ** (i_loc[:, None] + 1.0) * D ** -0.5
    kfac = GAMMA[None, :] ** (-(i_loc[:, None] + 1.0))
    qf = np.repeat(qfac, 128, axis=1)
    kf = np.repeat(kfac, 128, axis=1)
    return (np.ascontiguousarray((cosp * qf).T).astype(BF),
            np.ascontiguousarray((sinp * qf).T).astype(BF),
            np.ascontiguousarray((cosp * kf).T).astype(BF),
            np.ascontiguousarray((sinp * kf).T).astype(BF))


def _host_prep(inputs):
    x = np.asarray(inputs["L0_attn_x"], np.float32)
    wqkv = np.asarray(inputs["L0_attn_w_qkv"], np.float32)
    wg = np.asarray(inputs["L0_attn_w_g"], np.float32)
    wproj = np.asarray(inputs["L0_attn_w_proj"], np.float32)
    gnw = np.asarray(inputs["L0_attn_gn_weight"], np.float32)
    gnb = np.asarray(inputs["L0_attn_gn_bias"], np.float32)
    w1 = np.asarray(inputs["L0_ffn_w1"], np.float32)
    b1 = np.asarray(inputs["L0_ffn_b1"], np.float32)
    lng = np.asarray(inputs["L0_ffn_ln_g"], np.float32)
    lnb = np.asarray(inputs["L0_ffn_ln_b"], np.float32)
    w2 = np.asarray(inputs["L0_ffn_w2"], np.float32)
    b2 = np.asarray(inputs["L0_ffn_b2"], np.float32)

    pidx = _perm_idx()
    gperm = np.concatenate([hh * 128 + pidx for hh in range(H)])
    wqkv_p = wqkv.copy()
    wqkv_p[:, :C] = wqkv[:, :C][:, gperm]
    wqkv_p[:, C:2 * C] = wqkv[:, C:2 * C][:, gperm]

    cosq, sinq, cosk, sink = _build_tables()
    causal = np.triu(np.ones((L, L), np.float32)).astype(BF)
    identf = np.eye(128, dtype=np.float32)
    identb = np.eye(128, dtype=np.float32).astype(BF)
    b1_fm = np.ascontiguousarray(b1.reshape(IT, 128).T)
    lng_fm = np.ascontiguousarray(lng.reshape(IT, 128).T)
    lnb_fm = np.ascontiguousarray(lnb.reshape(IT, 128).T)

    in_maps = []
    for core in range(NCORES):
        b_, p_ = core // 4, core % 4
        rows = slice(p_ * NTOK, (p_ + 1) * NTOK)
        trows = slice(b_ * 0 + p_ * NTOK, p_ * NTOK + NTOK)
        cvec = np.zeros((NCORES, H), np.float64)
        for pp in range(p_):
            cvec[b_ * 4 + pp] = GAMMA ** (1024.0 * (p_ - pp - 1))
        cvec_b = np.broadcast_to(cvec.reshape(1, NCORES * H).astype(np.float32),
                                 (128, NCORES * H)).copy()
        sl8 = core
        m = {
            "x_loc": np.ascontiguousarray(x[b_, rows]),
            "cosq": np.ascontiguousarray(cosq[:, trows]),
            "sinq": np.ascontiguousarray(sinq[:, trows]),
            "cosk": np.ascontiguousarray(cosk[:, trows]),
            "sink": np.ascontiguousarray(sink[:, trows]),
            "causal": causal,
            "ident_f": identf,
            "ident_b": identb,
            "gnw_row": gnw.reshape(1, C).astype(BF),
            "gnb_row": gnb.reshape(1, C).astype(BF),
            "b1_fm": b1_fm,
            "b2_row": b2.reshape(1, C).astype(BF),
            "lng_fm": lng_fm,
            "lnb_fm": lnb_fm,
            "cvec": cvec_b,
            "wqkv_s": np.ascontiguousarray(wqkv_p[sl8 * 256:(sl8 + 1) * 256]),
            "wg_s": np.ascontiguousarray(wg[sl8 * 256:(sl8 + 1) * 256]),
            "wproj_s": np.ascontiguousarray(wproj[sl8 * 256:(sl8 + 1) * 256]),
            "w1t_s": np.ascontiguousarray(w1[:, sl8 * 256:(sl8 + 1) * 256].T),
            "w2t_s": np.ascontiguousarray(w2[:, sl8 * 1024:(sl8 + 1) * 1024].T),
        }
        in_maps.append(m)
    return in_maps


def kernel(**inputs):
    if "nc" not in _CACHE:
        _CACHE["nc"] = build_program()
    nc = _CACHE["nc"]
    in_maps = _host_prep(inputs)
    res = run_bass_kernel_spmd(nc, in_maps, core_ids=list(range(NCORES)))
    out = np.zeros((B, T, C), np.float32)
    for core in range(NCORES):
        b_, p_ = core // 4, core % 4
        out[b_, p_ * NTOK:(p_ + 1) * NTOK] = res.results[core]["out"]
    return out
